# version 05
# baseline (speedup 1.0000x reference)
import math
from contextlib import ExitStack

import numpy as np
import ml_dtypes

import concourse.bass as bass
import concourse.mybir as mybir
from concourse.bass_utils import run_bass_kernel_spmd

F32 = mybir.dt.float32
BF16 = mybir.dt.bfloat16
AF = mybir.ActivationFunctionType
ALU = mybir.AluOpType
AX = mybir.AxisListType
bf16 = ml_dtypes.bfloat16

NCORES = 8
BPC = 4
T = 2048
TQ = BPC * T
D = 1024


class Sched:

    ENGINES = ("sync", "tensor", "scalar", "vector", "gpsimd")

    def __init__(self):
        self.ops = {e: [] for e in self.ENGINES}
        self.counts = {}

    def val(self, sem):
        return self.counts.get(sem, 0)

    def add(self, eng, fn, waits=(), inc=None, inc_by=1):
        new_val = None
        if inc is not None:
            self.counts[inc] = self.counts.get(inc, 0) + inc_by
            new_val = self.counts[inc]
        self.ops[eng].append((fn, tuple(waits), inc, inc_by))
        return new_val

    def barrier(self):
        waits = tuple((k, self.val(k)) for k in ("TE", "SC", "VE", "GP", "LD",
                                                 "ST", "WD") if self.val(k) > 0)
        for eng in self.ENGINES:
            self.add(eng, lambda e: e.nop(), waits)

    def dma(self, queue, fn, waits=(), inc=None):
        if inc is None:
            inc = "WD" if queue == "gpsimd" else "LD"
        assert (queue == "gpsimd") == (inc == "WD"), (queue, inc)
        v = self.add(queue, fn, waits, inc, 16)
        return (inc, v)

    def emit(self, nc, sems):
        outer = self

        with nc.Block() as block:
            def make_body(eng_name):
                def body(eng):
                    last_wait = {}
                    for fn, waits, inc, inc_by in outer.ops[eng_name]:
                        for w in waits:
                            if w is None:
                                continue
                            sname, v = w
                            if v is None or v <= 0:
                                continue
                            if last_wait.get(sname, -1) >= v:
                                continue
                            eng.wait_ge(sems[sname], v)
                            last_wait[sname] = v
                        instr = fn(eng)
                        if inc is not None:
                            assert instr is not None, f"op on {eng_name} returned None"
                            instr.then_inc(sems[inc], inc_by)
                return body

            for name in self.ENGINES:
                if self.ops[name]:
                    getattr(block, name)(make_body(name))


SEM_NAMES = ("LD", "ST", "WD", "TE", "SC", "VE", "GP")


def _pretile_T(w, tile=128):
    fo, fi = w.shape
    wt = np.ascontiguousarray(np.asarray(w, np.float32).T)
    return np.ascontiguousarray(
        wt.reshape(fi, fo // tile, tile).transpose(1, 0, 2))


def _pack_scale(s, tile=128):
    fo = s.shape[0]
    return np.ascontiguousarray(np.asarray(s, np.float32).reshape(fo // tile, tile).T)


def _to_T(x):
    b, t, d = x.shape
    return np.ascontiguousarray(np.asarray(x).transpose(2, 0, 1).reshape(d, b * t))


def _from_T(xT, b, t, d):
    return np.ascontiguousarray(np.asarray(xT).reshape(d, b, t).transpose(1, 2, 0))


def emit_ln(s, bufs, res_src, t0, TH, xn_out, xn_col0, state):
    xb, x2b, srow = bufs["xb"], bufs["x2b"], bufs["srow"]
    pS1, pS2, pM, pR = bufs["pS1"], bufs["pS2"], bufs["pM"], bufs["pR"]
    ones_col_b = bufs["onesb"][:, 0:1]
    ones_row_f = bufs["onesf"][0:1, :]

    ld_x = state.pop("preload_ld", None)
    if ld_x is None:
        src = res_src[:, t0:t0 + TH].rearrange("(a p) t -> p a t", p=128)
        ld_x = s.dma("gpsimd", lambda e, sc=src: e.dma_start(xb[:, :, :], sc),
                     [state["xb_free"]])
    te_s1 = s.add("tensor", lambda e: [
        e.matmul(pS1[0:1, :], ones_col_b, xb[:, fi, :],
                 start=(fi == 0), stop=(fi == 7)) for fi in range(8)][-1],
        [ld_x, state["pS_free"]], inc="TE")
    te_s2 = None
    for rnd in range(4):
        sc_sq = s.add("scalar", lambda e, r=rnd: e.activation(
            x2b[:, :, :], xb[:, 2 * r:2 * (r + 1), :], AF.Square),
            [ld_x, ("TE", te_s1 if rnd == 0 else te_s2)], inc="SC")
        te_s2 = s.add("tensor", lambda e, r=rnd: [
            e.matmul(pS2[0:1, :], ones_col_b, x2b[:, i, :],
                     start=(r == 0 and i == 0), stop=(r == 3 and i == 1))
            for i in range(2)][-1],
            [("SC", sc_sq)], inc="TE")
    sc_rows = s.add("scalar", lambda e: [
        e.activation(srow[0:1, 0, :], pS1[0:1, :], AF.Identity, scale=1.0 / 1024.0),
        e.activation(srow[0:1, 1, :], pS2[0:1, :], AF.Identity, scale=1.0 / 1024.0),
    ][-1], [("TE", te_s2), state["xb_free"]], inc="SC")
    ve_v1 = s.add("vector", lambda e: e.tensor_mul(
        srow[0:1, 2, :], srow[0:1, 0, :], srow[0:1, 0, :]),
        [("SC", sc_rows)], inc="VE")
    ve_var = s.add("vector", lambda e: e.tensor_sub(
        srow[0:1, 2, :], srow[0:1, 1, :], srow[0:1, 2, :]),
        [("VE", ve_v1)], inc="VE")
    sc_ln = s.add("scalar", lambda e: e.activation(
        srow[0:1, 2, :], srow[0:1, 2, :], AF.Ln, bias=bufs["cvals"][0:1, 0:1]),
        [("VE", ve_var)], inc="SC")
    sc_rstd = s.add("scalar", lambda e: e.activation(
        srow[0:1, 2, :], srow[0:1, 2, :], AF.Exp, scale=-0.5),
        [("SC", sc_ln)], inc="SC")
    te_bc = s.add("tensor", lambda e: [
        e.matmul(pM[:, :], ones_row_f, srow[0:1, 0, :], start=True, stop=True),
        e.matmul(pR[:, :], ones_row_f, srow[0:1, 2, :], start=True, stop=True),
    ][-1], [("SC", sc_rstd), state["pMR_free"]], inc="TE")
    ve_apply = None
    xn_free = state.get("xn_free")
    for fi in range(8):
        ve_d = s.add("vector", lambda e, f=fi: e.tensor_sub(
            xn_out[:, f, xn_col0:xn_col0 + TH], xb[:, f, :], pM[:, :]),
            [("TE", te_bc), xn_free] if xn_free else [("TE", te_bc)], inc="VE")
        ve_apply = s.add("vector", lambda e, f=fi: e.tensor_mul(
            xn_out[:, f, xn_col0:xn_col0 + TH],
            xn_out[:, f, xn_col0:xn_col0 + TH], pR[:, :]),
            [("VE", ve_d)], inc="VE")
    state["xb_free"] = ("VE", ve_apply)
    state["pMR_free"] = ("VE", ve_apply)
    state["pS_free"] = ("SC", sc_rows)
    return ve_apply


def emit_ffn0(s, nc, es, hbm, pfx="f0_", cache_ready=None):
    sb = lambda name, shape, dt: es.enter_context(nc.sbuf_tensor(pfx + name, shape, dt))
    ps = lambda name, shape: es.enter_context(nc.psum_tensor(pfx + name, shape, F32))

    res_in = hbm["f0_res_in"]
    res_out = hbm["f0_res_out"]
    wts = {"w1": hbm["f0_w1b"], "w2": hbm["f0_w2b"], "h1": hbm["f0_h1b"],
           "h2": hbm["f0_h2b"], "w3": hbm["f0_w3b"]}

    TB = 1024
    NTB = TQ // TB
    TH = 512

    onesf = sb("onesf", [128, 128], F32)
    onesb = sb("onesb", [128, 128], BF16)
    scl = sb("scl", [128, 136], F32)
    xb = sb("xb", [128, 8, TH], BF16)
    x2b = sb("x2b", [128, 2, TH], BF16)
    xn = sb("xn", [128, 8, TB], BF16)
    h0 = sb("h0", [128, 32, TB], BF16)
    h1 = sb("h1", [128, 32, TB], BF16)
    wsl = sb("wsl", [128, 8192], BF16)
    xres = sb("xres", [128, 2 * TH], F32)
    tmp = sb("tmp", [128, 2, TH], F32)
    ot = sb("ot", [128, 2, TH], F32)
    srow = sb("srow", [1, 3, TH], F32)

    pA = ps("pA", [128, TH])
    pB = ps("pB", [128, TH])
    pS1 = ps("pS1", [1, TH])
    pS2 = ps("pS2", [1, TH])
    pM = ps("pM", [128, TH])
    pR = ps("pR", [128, TH])

    def wsl_q(q):
        return wsl[:, q * 2048:(q + 1) * 2048].rearrange("p (r c) -> p r c", c=128)

    def wsl_h(h):
        return wsl[:, h * 4096:(h + 1) * 4096].rearrange("p (r c) -> p r c", c=128)

    s.dma("sync", lambda e: e.dma_start(onesf[:, :], hbm["c_onesf"][:, :]))
    s.dma("sync", lambda e: e.dma_start(onesb[:, :], hbm["c_onesb"][:, :]))
    s.dma("sync", lambda e: e.dma_start(scl[:, :], hbm["f0_scl"][:, :]))
    ld_const = s.val("LD")

    cvals = sb("cvals", [128, 8], F32)
    s.dma("sync", lambda e: e.dma_start(cvals[:, :], hbm["c_cvals"][:, :]))
    ln_bufs = {"xb": xb, "x2b": x2b, "srow": srow, "pS1": pS1, "pS2": pS2,
               "pM": pM, "pR": pR, "onesb": onesb, "onesf": onesf, "cvals": cvals}
    ln_state = {"xb_free": ("LD", ld_const), "pMR_free": None, "pS_free": None,
                "xn_free": None}

    cache_wait = ("WD", cache_ready["f0"]) if cache_ready else None

    qtr_te = [None, None, None, None]

    def dma_slab(src, dst, quarters):
        waits = [cache_wait] if cache_wait else []
        for q in quarters:
            if qtr_te[q] is not None:
                waits.append(("TE", qtr_te[q]))
        return s.dma("gpsimd", lambda e, d=dst, sc=src: e.dma_start(d, sc), waits)

    bank_free = {"pA": None, "pB": None}
    ot_store = [0, 0]
    st_total = 0

    ve_ln_cur = None
    for th in range(2):
        ve_ln_cur = emit_ln(s, ln_bufs, res_in, th * TH, TH, xn, th * TH, ln_state)

    for tb in range(NTB):
        t0 = tb * TB

        pair_wd = [None] * 32
        n_pair = [0]

        def load_pair():
            k = n_pair[0]
            if k >= 32:
                return
            p_i = k // 2
            wname = "w1" if k % 2 == 0 else "w2"
            src = wts[wname][2 * p_i:2 * p_i + 2].rearrange(
                "s (a p) f -> p (s a) f", p=128)
            pair_wd[k] = dma_slab(src, wsl_q(k % 4), (k % 4,))
            n_pair[0] += 1

        for _ in range(4):
            load_pair()

        ve_ln = ve_ln_cur
        for fo in range(32):
            off = (fo % 2) * 8
            k1, k2 = 2 * (fo // 2), 2 * (fo // 2) + 1
            q1, q2 = k1 % 4, k2 % 4
            for th in range(2):
                g1 = s.add("tensor", lambda e, q=q1, o=off, t=th: [
                    e.matmul(pA[:, :], wsl_q(q)[:, o + fi, :],
                             xn[:, fi, t * TH:(t + 1) * TH],
                             start=(fi == 0), stop=(fi == 7)) for fi in range(8)][-1],
                    [pair_wd[k1], ("VE", ve_ln), bank_free["pA"]], inc="TE")
                g2 = s.add("tensor", lambda e, q=q2, o=off, t=th: [
                    e.matmul(pB[:, :], wsl_q(q)[:, o + fi, :],
                             xn[:, fi, t * TH:(t + 1) * TH],
                             start=(fi == 0), stop=(fi == 7)) for fi in range(8)][-1],
                    [pair_wd[k2], bank_free["pB"]], inc="TE")
                sc_e = s.add("scalar", lambda e, f=fo, t=th: e.activation(
                    tmp[:, t, :], pA[:, :], AF.Silu, scale=scl[:, f:f + 1]),
                    [("TE", g1), ("VE", s.val("VE"))], inc="SC")
                bank_free["pA"] = ("SC", sc_e)
                ve_h0 = s.add("vector", lambda e, f=fo, t=th: e.scalar_tensor_tensor(
                    h0[:, f, t * TH:(t + 1) * TH], pB[:, :], scl[:, 32 + f:33 + f],
                    tmp[:, t, :], ALU.mult, ALU.mult),
                    [("TE", g2), ("SC", sc_e)], inc="VE")
                bank_free["pB"] = ("VE", ve_h0)
            if fo % 2 == 1:
                qtr_te[q1] = s.val("TE")
                qtr_te[q2] = s.val("TE")
                load_pair()
                load_pair()
        te_12 = s.val("TE")

        slab_wd = {}

        def load_big(which, fo2, seq):
            src = wts[which][fo2].rearrange("(a p) f -> p a f", p=128)
            h = seq % 2
            slab_wd[(which, fo2)] = dma_slab(src, wsl_h(h), (2 * h, 2 * h + 1))

        load_big("h1", 0, 0)
        load_big("h1", 1, 1)
        big_seq = [2]

        def load_next_big():
            sq = big_seq[0]
            if sq < 32:
                load_big("h1", sq, sq)
            elif sq < 64:
                load_big("h2", sq - 32, sq)
            elif sq < 72:
                load_big("w3", sq - 64, sq)
            else:
                return
            big_seq[0] += 1

        ln_next = [None]

        def preload_ln(th2):
            if tb + 1 >= NTB:
                return
            src = res_in[:, (tb + 1) * TB + th2 * TH:
                         (tb + 1) * TB + (th2 + 1) * TH].rearrange(
                "(a p) t -> p a t", p=128)
            ln_state["preload_ld"] = s.dma(
                "gpsimd", lambda e, sc=src: e.dma_start(xb[:, :, :], sc),
                [ln_state["xb_free"]])

        def emit_ln_next(th2):
            if tb + 1 >= NTB:
                return
            ln_state["xn_free"] = ("TE", te_12)
            ln_next[0] = emit_ln(s, ln_bufs, res_in, (tb + 1) * TB + th2 * TH,
                                 TH, xn, th2 * TH, ln_state)
            ln_state["xn_free"] = None

        rhs_ready = ("VE", s.val("VE"))
        for stage, (rhs_buf, out_buf, scol, which) in enumerate((
                (h0, h1, 64, "h1"), (h1, h0, 96, "h2"))):
            for fo in range(32):
                h = (stage * 32 + fo) % 2
                for th in range(2):
                    pname = "pA" if (fo * 2 + th) % 2 == 0 else "pB"
                    pp = pA if pname == "pA" else pB
                    g = s.add("tensor", lambda e, hh=h, t=th, rb=rhs_buf, p=pp: [
                        e.matmul(p[:, :], wsl_h(hh)[:, fi, :],
                                 rb[:, fi, t * TH:(t + 1) * TH],
                                 start=(fi == 0), stop=(fi == 31)) for fi in range(32)][-1],
                        [slab_wd[(which, fo)], bank_free[pname], rhs_ready], inc="TE")
                    sc_e = s.add("scalar", lambda e, f=fo, t=th, ob=out_buf, sc0=scol, p=pp:
                                 e.activation(ob[:, f, t * TH:(t + 1) * TH], p[:, :],
                                              AF.Gelu, scale=scl[:, sc0 + f:sc0 + f + 1]),
                                 [("TE", g)], inc="SC")
                    bank_free[pname] = ("SC", sc_e)
                qtr_te[2 * h] = s.val("TE")
                qtr_te[2 * h + 1] = s.val("TE")
                load_next_big()
                if fo == 2:
                    preload_ln(stage)
                if fo == 10:
                    emit_ln_next(stage)
            rhs_ready = ("SC", s.val("SC"))

        for fo in range(8):
            h = fo % 2
            src = res_in[fo * 128:(fo + 1) * 128, t0:t0 + TB]
            ld_r = s.dma("sync", lambda e, sc=src: e.dma_start(xres[:, :], sc),
                         [("VE", s.val("VE"))])
            for th in range(2):
                pname = "pA" if (fo * 2 + th) % 2 == 0 else "pB"
                pp = pA if pname == "pA" else pB
                g = s.add("tensor", lambda e, hh=h, t=th, p=pp: [
                    e.matmul(p[:, :], wsl_h(hh)[:, fi, :],
                             h0[:, fi, t * TH:(t + 1) * TH],
                             start=(fi == 0), stop=(fi == 31)) for fi in range(32)][-1],
                    [slab_wd[("w3", fo)], bank_free[pname], rhs_ready], inc="TE")
                r = (fo * 2 + th) % 2
                ve_o = s.add("vector", lambda e, f=fo, t=th, rr=r, p=pp:
                             e.scalar_tensor_tensor(
                                 ot[:, rr, :], p[:, :], scl[:, 128 + f:129 + f],
                                 xres[:, t * TH:(t + 1) * TH], ALU.mult, ALU.add),
                             [("TE", g), ld_r, ("ST", ot_store[r])], inc="VE")
                bank_free[pname] = ("VE", ve_o)
                dst = res_out[fo * 128:(fo + 1) * 128, t0 + th * TH:t0 + (th + 1) * TH]
                st_total = s.dma("sync", lambda e, d=dst, rr=r: e.dma_start(d, ot[:, rr, :]),
                                 [("VE", ve_o)], inc="ST")[1]
                ot_store[r] = st_total
            qtr_te[2 * h] = s.val("TE")
            qtr_te[2 * h + 1] = s.val("TE")
            load_next_big()
        ve_ln_cur = ln_next[0]

    s.add("sync", lambda e: e.nop(), [("ST", st_total)])
    return st_total


def _ffn0_weight_dram(nc, hbm):
    hbm["f0_w1t"] = nc.dram_tensor("w1t", [32, 1024, 128], F32, kind="ExternalInput").ap()
    hbm["f0_w2t"] = nc.dram_tensor("w2t", [32, 1024, 128], F32, kind="ExternalInput").ap()
    hbm["f0_h1t"] = nc.dram_tensor("h1t", [32, 4096, 128], F32, kind="ExternalInput").ap()
    hbm["f0_h2t"] = nc.dram_tensor("h2t", [32, 4096, 128], F32, kind="ExternalInput").ap()
    hbm["f0_w3t"] = nc.dram_tensor("w3t", [8, 4096, 128], F32, kind="ExternalInput").ap()
    hbm["f0_scl"] = nc.dram_tensor("scl", [128, 136], F32, kind="ExternalInput").ap()
    hbm["f0_w1b"] = nc.dram_tensor("w1b", [32, 1024, 128], BF16, kind="Internal").ap()
    hbm["f0_w2b"] = nc.dram_tensor("w2b", [32, 1024, 128], BF16, kind="Internal").ap()
    hbm["f0_h1b"] = nc.dram_tensor("h1b", [32, 4096, 128], BF16, kind="Internal").ap()
    hbm["f0_h2b"] = nc.dram_tensor("h2b", [32, 4096, 128], BF16, kind="Internal").ap()
    hbm["f0_w3b"] = nc.dram_tensor("w3b", [8, 4096, 128], BF16, kind="Internal").ap()
    return hbm


def _mk_ffn0_dram(nc):
    hbm = {}
    hbm["f0_res_in"] = nc.dram_tensor("res_in", [D, TQ], F32, kind="ExternalInput").ap()
    hbm["f0_res_out"] = nc.dram_tensor("res_out", [D, TQ], F32, kind="ExternalOutput").ap()
    _ffn0_weight_dram(nc, hbm)
    _const_dram(nc, hbm)
    return hbm


def build_ffn0_nc():
    nc = bass.Bass("TRN2", target_bir_lowering=False, debug=False)
    hbm = _mk_ffn0_dram(nc)
    s = Sched()
    with ExitStack() as es:
        sems = {n: es.enter_context(nc.semaphore(n)) for n in SEM_NAMES}
        emit_ffn0(s, nc, es, hbm)
        s.emit(nc, sems)
    return nc


def _cvals():
    c = np.zeros((128, 8), np.float32)
    c[:, 0] = 1e-5
    c[:, 1] = 1e-6
    c[:, 2] = 1.0
    return c


def ffn0_host_inputs(inputs):
    scl = np.zeros((128, 136), np.float32)
    scl[:, 0:32] = _pack_scale(inputs["L0_ffn_w1_s"])
    scl[:, 32:64] = _pack_scale(inputs["L0_ffn_w2_s"])
    scl[:, 64:96] = _pack_scale(inputs["L0_ffn_h1_s"])
    scl[:, 96:128] = _pack_scale(inputs["L0_ffn_h2_s"])
    scl[:, 128:136] = _pack_scale(inputs["L0_ffn_w3_s"])
    return {
        "w1t": _pretile_T(inputs["L0_ffn_w1_w"]),
        "w2t": _pretile_T(inputs["L0_ffn_w2_w"]),
        "h1t": _pretile_T(inputs["L0_ffn_h1_w"]),
        "h2t": _pretile_T(inputs["L0_ffn_h2_w"]),
        "w3t": _pretile_T(inputs["L0_ffn_w3_w"]),
        "scl": scl,
        "onesf": np.ones((128, 128), np.float32),
        "onesb": np.ones((128, 128), bf16),
        "cvals": _cvals(),
    }


def run_ffn0(res_T_shards, common, core_ids=None):
    if core_ids is None:
        core_ids = list(range(len(res_T_shards)))
    nc = build_ffn0_nc()
    in_maps = [dict(common, res_in=r) for r in res_T_shards]
    res = run_bass_kernel_spmd(nc, in_maps, core_ids=core_ids)
    return [r["res_out"] for r in res.results]



NH = 16
HD = 64
TK = 1024


def _rhs_tile(w):
    fo, fi = w.shape
    return np.ascontiguousarray(
        np.asarray(w, np.float32).T.reshape(fi // 128, 128, fo))


def emit_attn(s, nc, es, hbm, pfx="at_", cache_ready=None):
    sb = lambda name, shape, dt: es.enter_context(nc.sbuf_tensor(pfx + name, shape, dt))
    ps = lambda name, shape: es.enter_context(nc.psum_tensor(pfx + name, shape, F32))

    res_in = hbm["at_res_in"]
    res_out = hbm["at_res_out"]

    onesf = sb("onesf", [128, 128], F32)
    onesb = sb("onesb", [128, 128], BF16)
    onesrow = sb("onesrow", [1, 512], BF16)
    cvals = sb("cvals", [128, 8], F32)
    cpk = sb("cpk", [128, 48], F32)
    vob = sb("vob", [1, 2048], BF16)
    xb = sb("xb", [128, 8, 512], BF16)
    x2b = sb("x2b", [128, 2, 512], BF16)
    srow = sb("srow", [1, 3, 512], F32)
    xn = sb("xn", [128, 8, T], BF16)
    xd = sb("xd", [128, 8, TK], BF16)
    QT = sb("QT", [128, 8, T], BF16)
    KT = sb("KT", [128, 8, TK], BF16)
    V65 = sb("V65", [128, 8, NH * 65], BF16)
    expS = sb("expS", [128, 8, T], BF16)
    wq = sb("wq", [128, 8, 1024], BF16)
    orw = sb("orw", [128, 2, 512], F32)
    rcp = sb("rcp", [1, 2, 512], F32)
    xres = sb("xres", [128, T], F32)
    ot = sb("ot", [128, 2, 512], F32)

    pA = ps("pA", [128, 512])
    pB = ps("pB", [128, 512])
    pS1 = ps("pS1", [1, 512])
    pS2 = ps("pS2", [1, 512])
    pM = ps("pM", [128, 512])
    pR = ps("pR", [128, 512])
    pO = ps("pO", [65, 512])
    pD = ps("pD", [64, 512])

    for dst, key in ((onesf, "c_onesf"), (onesb, "c_onesb"), (onesrow, "c_onesrow"),
                     (cvals, "c_cvals"), (cpk, "at_cpk"), (vob, "at_vob")):
        s.dma("sync", lambda e, d=dst, k=key: e.dma_start(d[:, :], hbm[k][:, :]))
    ld_const = s.val("LD")

    ln_bufs = {"xb": xb, "x2b": x2b, "srow": srow, "pS1": pS1, "pS2": pS2,
               "pM": pM, "pR": pR, "onesb": onesb, "onesf": onesf, "cvals": cvals}
    ln_state = {"xb_free": ("LD", ld_const), "pMR_free": None, "pS_free": None}

    bank_free = {"pA": None, "pB": None, "pO": None, "pD": None}
    wq_free = None
    ot_store = [0, 0]
    st_total = 0

    def proj(n_blk, rhs_buf, out_buf, bias_col, rhs_ready, wd_val, n_fi=8):
        nonlocal st_total
        for fo in range(8):
            for blk in range(n_blk):
                pname = "pA" if (fo * n_blk + blk) % 2 == 0 else "pB"
                pp = pA if pname == "pA" else pB
                g = s.add("tensor", lambda e, f=fo, bl=blk, p=pp, rb=rhs_buf: [
                    e.matmul(p[:, :], wq[:, fi, f * 128:(f + 1) * 128],
                             rb[:, fi, bl * 512:(bl + 1) * 512],
                             start=(fi == 0), stop=(fi == n_fi - 1))
                    for fi in range(n_fi)][-1],
                    [wd_val, bank_free[pname], rhs_ready], inc="TE")
                sc_e = s.add("scalar", lambda e, f=fo, bl=blk, p=pp, ob=out_buf, bc=bias_col:
                             e.activation(ob[:, f, bl * 512:(bl + 1) * 512], p[:, :],
                                          AF.Identity, bias=cpk[:, bc + f:bc + f + 1]),
                             [("TE", g)], inc="SC")
                bank_free[pname] = ("SC", sc_e)
        return s.val("TE"), ("SC", s.val("SC"))

    cache_at = ("WD", cache_ready["at"]) if cache_ready else None

    for b in range(BPC):
        tb0 = b * T
        ve_ln = None
        for j in range(4):
            ve_ln = emit_ln(s, ln_bufs, res_in, tb0 + j * 512, 512, xn, j * 512, ln_state)
        bank_free["pM"] = ln_state["pMR_free"]
        bank_free["pR"] = ln_state["pMR_free"]

        ld_q = s.dma("gpsimd", lambda e: e.dma_start(
            wq[:, :, :], hbm["at_qwb"].rearrange("a p f -> p a f")),
            [wq_free, cache_at])
        te_q, rdy_q = proj(4, xn, QT, 0, ("VE", ve_ln), ld_q)

        conv_done = None
        for fi in range(8):
            c1 = s.add("vector", lambda e, f=fi: e.tensor_scalar(
                xd[:, f, :], xn[:, f, 0:T:2], cpk[:, 24 + f:25 + f],
                cpk[:, 40 + f:41 + f], ALU.mult, ALU.add),
                [("VE", ve_ln)], inc="VE")
            c2 = s.add("vector", lambda e, f=fi: e.scalar_tensor_tensor(
                xd[:, f, :], xn[:, f, 1:T:2], cpk[:, 32 + f:33 + f], xd[:, f, :],
                ALU.mult, ALU.add),
                [("VE", c1)], inc="VE")
            conv_done = s.add("vector", lambda e, f=fi: e.scalar_tensor_tensor(
                xd[:, f, 1:TK], xn[:, f, 1:T - 2:2], cpk[:, 16 + f:17 + f],
                xd[:, f, 1:TK], ALU.mult, ALU.add),
                [("VE", c2)], inc="VE")

        ld_k = s.dma("gpsimd", lambda e: e.dma_start(
            wq[:, :, :], hbm["at_kwb"].rearrange("a p f -> p a f")),
            [("TE", te_q)])
        te_k, rdy_k = proj(2, xd, KT, 8, ("VE", conv_done), ld_k)

        ld_v = s.dma("gpsimd", lambda e: e.dma_start(
            wq[:, :, :], hbm["at_vwb"].rearrange("a p f -> p a f")),
            [("TE", te_k)])
        ve_ms = s.add("vector", lambda e: e.memset(V65[:, :, :], 1.0),
                      [("TE", s.val("TE"))], inc="VE")
        v_evac = None
        for tkt in range(8):
            for fb in range(2):
                pname = "pA" if (tkt * 2 + fb) % 2 == 0 else "pB"
                pp = pA if pname == "pA" else pB
                g = s.add("tensor", lambda e, tk=tkt, f=fb, p=pp: ([
                    e.matmul(p[:, :], xd[:, fi, tk * 128:(tk + 1) * 128],
                             wq[:, fi, f * 512:(f + 1) * 512],
                             start=(fi == 0), stop=False) for fi in range(8)] + [
                    e.matmul(p[:, :], onesb[0:1, :],
                             vob[0:1, f * 512:(f + 1) * 512], start=False, stop=True)])[-1],
                    [ld_v, bank_free[pname], ("VE", conv_done)], inc="TE")
                v_evac = s.add("vector", lambda e, tk=tkt, f=fb, p=pp: e.tensor_copy(
                    V65[:, tk, :].rearrange("p (h c) -> p h c", c=65)[:, f * 8:(f + 1) * 8, 0:64],
                    p[:, :].rearrange("p (h c) -> p h c", c=64)),
                    [("TE", g), ("VE", ve_ms)], inc="VE")
                bank_free[pname] = ("VE", v_evac)

        vem_v = [None, None]
        pend_r = [None, None]
        for h in range(NH):
            hp = (h % 2) * 64
            ha = h // 2
            exp_blk = [None] * 4
            for tkt in range(8):
                for blk in range(4):
                    pname = "pA" if (tkt * 4 + blk) % 2 == 0 else "pB"
                    pp = pA if pname == "pA" else pB
                    g = s.add("tensor", lambda e, tk=tkt, bl=blk, p=pp, q0=hp, a0=ha:
                              e.matmul(p[:, :], KT[q0:q0 + 64, a0, tk * 128:(tk + 1) * 128],
                                       QT[q0:q0 + 64, a0, bl * 512:(bl + 1) * 512],
                                       start=True, stop=True),
                              [bank_free[pname], rdy_q, rdy_k, ("VE", v_evac)], inc="TE")
                    sc_exp = s.add("scalar", lambda e, tk=tkt, bl=blk, p=pp: e.activation(
                        expS[:, tk, bl * 512:(bl + 1) * 512], p[:, :], AF.Exp, scale=0.125),
                        [("TE", g)], inc="SC")
                    bank_free[pname] = ("SC", sc_exp)
                    exp_blk[blk] = sc_exp
            ver_v = [None] * 4

            def emit_td(bl):
                r2 = bl % 2
                key_d = "pD" if r2 == 0 else "pR"
                ppd = pD if r2 == 0 else pR
                td = s.add("tensor", lambda e, rr=r2, p=ppd: e.matmul(
                    p[0:64, :], onesf[0:1, 0:64], rcp[0:1, rr, :],
                    start=True, stop=True),
                    [("VE", ver_v[bl]), bank_free[key_d]], inc="TE")
                vm = s.add("vector", lambda e, bl2=bl, rr=r2, q0=hp, a0=ha, p=ppd:
                           e.tensor_mul(
                               xn[q0:q0 + 64, a0, bl2 * 512:(bl2 + 1) * 512],
                               orw[0:64, rr, :], p[0:64, :]),
                           [("TE", td)], inc="VE")
                bank_free[key_d] = ("VE", vm)
                vem_v[r2] = vm
                pend_r[r2] = td

            for blk in range(4):
                r = blk % 2
                key_o = "pO" if r == 0 else "pM"
                ppo = pO if r == 0 else pM
                g = s.add("tensor", lambda e, bl=blk, hh=h, p=ppo: [
                    e.matmul(p[0:65, :], V65[:, tk, hh * 65:(hh + 1) * 65],
                             expS[:, tk, bl * 512:(bl + 1) * 512],
                             start=(tk == 0), stop=(tk == 7)) for tk in range(8)][-1],
                    [bank_free[key_o], ("SC", exp_blk[blk])], inc="TE")
                sc_o = s.add("scalar", lambda e, rr=r, p=ppo: e.activation(
                    orw[0:65, rr, :], p[0:65, :], AF.Identity),
                    [("TE", g)] + ([("VE", vem_v[r])] if vem_v[r] else []), inc="SC")
                bank_free[key_o] = ("SC", sc_o)
                ver_v[blk] = s.add("vector", lambda e, rr=r: e.reciprocal(
                    rcp[0:1, rr, :], orw[64:65, rr, :]),
                    [("SC", sc_o)] + ([("TE", pend_r[r])] if pend_r[r] else []),
                    inc="VE")
                if blk >= 1:
                    emit_td(blk - 1)
            emit_td(3)
        ln_state["pMR_free"] = ("VE", s.val("VE"))

        ld_o = s.dma("gpsimd", lambda e: e.dma_start(
            wq[:, :, :], hbm["at_owb"].rearrange("a p f -> p a f")),
            [("TE", s.val("TE"))])
        ve_last = s.val("VE")
        for fo in range(8):
            src = res_in[fo * 128:(fo + 1) * 128, tb0:tb0 + T]
            ld_r = s.dma("sync", lambda e, sc=src: e.dma_start(xres[:, :], sc),
                         [("VE", s.val("VE"))])
            for blk in range(4):
                pname = "pA" if (fo * 4 + blk) % 2 == 0 else "pB"
                pp = pA if pname == "pA" else pB
                g = s.add("tensor", lambda e, f=fo, bl=blk, p=pp: ([
                    e.matmul(p[:, :], wq[:, fi, f * 128:(f + 1) * 128],
                             xn[:, fi, bl * 512:(bl + 1) * 512],
                             start=(fi == 0), stop=False) for fi in range(8)] + [
                    e.matmul(p[:, :], vob[0:1, 1024 + f * 128:1024 + (f + 1) * 128],
                             onesrow[0:1, :], start=False, stop=True)])[-1],
                    [ld_o, bank_free[pname], ("VE", ve_last), ld_r],
                    inc="TE")
                r = (fo * 4 + blk) % 2
                ve_o = s.add("vector", lambda e, bl=blk, rr=r, p=pp: e.tensor_add(
                    ot[:, rr, :], p[:, :], xres[:, bl * 512:(bl + 1) * 512]),
                    [("TE", g), ("ST", ot_store[r])], inc="VE")
                bank_free[pname] = ("VE", ve_o)
                dst = res_out[fo * 128:(fo + 1) * 128, tb0 + blk * 512:tb0 + (blk + 1) * 512]
                st_total = s.dma("sync", lambda e, d=dst, rr=r: e.dma_start(d, ot[:, rr, :]),
                                 [("VE", ve_o)], inc="ST")[1]
                ot_store[r] = st_total
        wq_free = ("TE", s.val("TE"))

    s.add("sync", lambda e: e.nop(), [("ST", st_total)])
    return st_total


def _attn_weight_dram(nc, hbm):
    for w in ("qw", "kw", "vw", "ow"):
        hbm["at_" + w] = nc.dram_tensor(w, [8, 128, 1024], F32, kind="ExternalInput").ap()
        hbm["at_" + w + "b"] = nc.dram_tensor(w + "b", [8, 128, 1024], BF16,
                                              kind="Internal").ap()
    hbm["at_cpk"] = nc.dram_tensor("cpk", [128, 48], F32, kind="ExternalInput").ap()
    hbm["at_vob"] = nc.dram_tensor("vob", [1, 2048], BF16, kind="ExternalInput").ap()
    return hbm


def _mk_attn_dram(nc):
    hbm = {}
    hbm["at_res_in"] = nc.dram_tensor("res_in", [D, TQ], F32, kind="ExternalInput").ap()
    hbm["at_res_out"] = nc.dram_tensor("res_out", [D, TQ], F32, kind="ExternalOutput").ap()
    _attn_weight_dram(nc, hbm)
    _const_dram(nc, hbm)
    return hbm


def build_attn_nc():
    nc = bass.Bass("TRN2", target_bir_lowering=False, debug=False)
    hbm = _mk_attn_dram(nc)
    s = Sched()
    with ExitStack() as es:
        sems = {n: es.enter_context(nc.semaphore(n)) for n in SEM_NAMES}
        emit_attn(s, nc, es, hbm)
        s.emit(nc, sems)
    return nc


def attn_host_inputs(inputs):
    cpk = np.zeros((128, 48), np.float32)
    cpk[:, 0:8] = _pack_scale(inputs["L1_attn_q_b"])
    cpk[:, 8:16] = _pack_scale(inputs["L1_attn_k_b"])
    cw = np.asarray(inputs["L1_attn_conv_w"], np.float32)
    cpk[:, 16:24] = _pack_scale(cw[:, 0, 0])
    cpk[:, 24:32] = _pack_scale(cw[:, 0, 1])
    cpk[:, 32:40] = _pack_scale(cw[:, 0, 2])
    cpk[:, 40:48] = _pack_scale(inputs["L1_attn_conv_b"])
    vob = np.zeros((1, 2048), np.float32)
    vob[0, 0:1024] = inputs["L1_attn_v_b"]
    vob[0, 1024:2048] = inputs["L1_attn_out_b"]
    return {
        "qw": _rhs_tile(inputs["L1_attn_q_w"]),
        "kw": _rhs_tile(inputs["L1_attn_k_w"]),
        "vw": _rhs_tile(inputs["L1_attn_v_w"]),
        "ow": _rhs_tile(inputs["L1_attn_out_w"]),
        "cpk": cpk,
        "vob": vob.astype(bf16),
        "onesf": np.ones((128, 128), np.float32),
        "onesb": np.ones((128, 128), bf16),
        "onesrow": np.ones((1, 512), bf16),
        "cvals": _cvals(),
    }


def run_attn(res_T_shards, common, core_ids=None):
    if core_ids is None:
        core_ids = list(range(len(res_T_shards)))
    nc = build_attn_nc()
    in_maps = [dict(common, res_in=r) for r in res_T_shards]
    res = run_bass_kernel_spmd(nc, in_maps, core_ids=core_ids)
    return [r["res_out"] for r in res.results]



FI1 = 2816


def emit_quant(s, src, dst, W, tm, src_ready, sgn_src=None):
    nb = W // 16
    xa, v, a1, a2 = tm["xa"], tm["v"], tm["a1"], tm["a2"]
    sgn, scq, sc2, rcp = tm["sgn"], tm["scq"], tm["sc2"], tm["rcp"]
    pve, pgp, psc = tm.get("fVE", 0), tm.get("fGP", 0), tm.get("fSC", 0)

    w1 = list(src_ready) + [("VE", pve), ("GP", pgp), ("SC", psc)]
    r1 = s.add("vector", lambda e: e.reduce_max(
        scq[:, 0:nb], src.rearrange("p (a b) -> p a b", b=16),
        axis=AX.X, apply_absolute_value=True), w1, inc="VE")
    r2 = s.add("vector", lambda e: e.tensor_scalar(
        sc2[:, 0:nb], scq[:, 0:nb], 1.0 / 6.0, 1e-12, ALU.mult, ALU.max),
        [("VE", r1)], inc="VE")
    r3 = s.add("vector", lambda e: e.reciprocal(rcp[:, 0:nb], sc2[:, 0:nb]),
               [("VE", r2)], inc="VE")
    c4 = s.add("scalar", lambda e: e.activation(xa[:, 0:W], src, AF.Abs),
               w1, inc="SC")
    c5 = s.add("scalar", lambda e: e.activation(sgn[:, 0:W], src, AF.Sign),
               [("SC", c4)], inc="SC")
    rb = rcp[:, 0:nb].unsqueeze(2).broadcast_to((128, nb, 16))
    c6 = s.add("gpsimd", lambda e: e.tensor_tensor(
        v[:, 0:W].rearrange("p (a b) -> p a b", b=16),
        xa[:, 0:W].rearrange("p (a b) -> p a b", b=16), rb, ALU.mult),
        [("SC", c4), ("VE", r3), ("GP", pgp)], inc="GP")
    g = s.add("vector", lambda e: e.tensor_single_scalar(
        a2[:, 0:W], v[:, 0:W], 2.5, ALU.is_gt), [("GP", c6)], inc="VE")
    for thr in (3.5, 5.0, 5.0):
        g = s.add("vector", lambda e, t=thr: e.scalar_tensor_tensor(
            a2[:, 0:W], v[:, 0:W], t, a2[:, 0:W], ALU.is_gt, ALU.add),
            [("VE", g)], inc="VE")
    q = s.add("vector", lambda e: e.tensor_scalar(
        a1[:, 0:W], v[:, 0:W], 2.0, MAGIC, ALU.mult, ALU.add),
        [("GP", c6)], inc="VE")
    q = s.add("vector", lambda e: e.tensor_scalar(
        a1[:, 0:W], a1[:, 0:W], MAGIC, 4.0, ALU.subtract, ALU.min),
        [("VE", q)], inc="VE")
    q2 = s.add("vector", lambda e: e.scalar_tensor_tensor(
        xa[:, 0:W], a1[:, 0:W], 0.5, a2[:, 0:W], ALU.mult, ALU.add),
        [("VE", q), ("VE", g)], inc="VE")
    q3 = s.add("vector", lambda e: e.tensor_tensor(
        a1[:, 0:W], xa[:, 0:W], sgn[:, 0:W], ALU.mult),
        [("VE", q2), ("SC", c5)], inc="VE")
    sb2 = sc2[:, 0:nb].unsqueeze(2).broadcast_to((128, nb, 16))
    q4 = s.add("vector", lambda e: e.tensor_tensor(
        dst.rearrange("p (a b) -> p a b", b=16),
        a1[:, 0:W].rearrange("p (a b) -> p a b", b=16), sb2, ALU.mult),
        [("VE", q3)], inc="VE")
    tm["fVE"], tm["fGP"], tm["fSC"] = q4, c6, c5
    return q4


def emit_f1(s, nc, es, hbm, pfx="f1_"):
    sb = lambda name, shape, dt: es.enter_context(nc.sbuf_tensor(pfx + name, shape, dt))
    ps = lambda name, shape, dt=F32: es.enter_context(nc.psum_tensor(pfx + name, shape, dt))

    res_in = hbm["f1_res_in"]
    res_out = hbm["f1_res_out"]
    wgq = nc.dram_tensor("f1wgq", [8, 128, FI1], BF16, kind="Internal").ap()
    wuq = nc.dram_tensor("f1wuq", [8, 128, FI1], BF16, kind="Internal").ap()
    wdq = nc.dram_tensor("f1wdq", [8, FI1, 128], BF16, kind="Internal").ap()

    onesf = sb("onesf", [128, 128], F32)
    onesb = sb("onesb", [128, 128], BF16)
    idb = sb("idb", [128, 128], BF16)
    idf = sb("idf", [128, 128], F32)
    cvals = sb("cvals", [128, 8], F32)
    xb = sb("xb", [128, 8, 512], BF16)
    x2b = sb("x2b", [128, 2, 512], BF16)
    srow = sb("srow", [1, 3, 512], F32)
    xn = sb("xn", [128, 8, T], BF16)
    xnat = sb("xnat", [128, 8, 128], BF16)
    xqn = sb("xqn", [128, 8, 128], BF16)
    xqT = sb("xqT", [128, 8, 512], BF16)
    htile = sb("htile", [128, 2, 512], F32)
    stmp = sb("stmp", [128, 2, 512], F32)
    h_nat = sb("h_nat", [128, 4, FI1], BF16)
    hqn = sb("hqn", [128, 2, FI1], BF16)
    hqT = sb("hqT", [128, 22, 512], BF16)
    wgu = sb("wgu", [128, 2, 8, 128], BF16)
    wdsl = sb("wdsl", [128, 2, 22, 128], BF16)
    wload = sb("wload", [128, FI1], F32)
    wqnat = sb("wqnat", [128, FI1], BF16)
    wstage = sb("wstage", [128, 22, 128], BF16)
    xres = sb("xres", [128, 2, 512], F32)
    ot = sb("ot", [128, 2, 512], F32)
    q_xa = sb("q_xa", [128, 1408], F32)
    q_v = sb("q_v", [128, 1408], F32)
    q_a1 = sb("q_a1", [128, 1408], F32)
    q_a2 = sb("q_a2", [128, 1408], F32)
    q_sgn = sb("q_sgn", [128, 1408], BF16)
    q_scq = sb("q_scq", [128, 88], F32)
    q_sc2 = sb("q_sc2", [128, 88], F32)
    q_rcp = sb("q_rcp", [128, 88], F32)
    tm = {"xa": q_xa, "v": q_v, "a1": q_a1, "a2": q_a2, "sgn": q_sgn,
          "scq": q_scq, "sc2": q_sc2, "rcp": q_rcp}

    pA = ps("pA", [128, 512])
    pB = ps("pB", [128, 512])
    pS1 = ps("pS1", [1, 512])
    pS2 = ps("pS2", [1, 512])
    pM = ps("pM", [128, 512])
    pR = ps("pR", [128, 512])
    pTb = ps("pTb", [128, 128], BF16)
    pTf = ps("pTf", [128, 128], F32)

    for dst, key in ((onesf, "c_onesf"), (onesb, "c_onesb"), (idb, "c_idb"),
                     (idf, "c_idf"), (cvals, "c_cvals")):
        s.dma("sync", lambda e, d=dst, k=key: e.dma_start(d[:, :], hbm[k][:, :]))
    ld_const = s.val("LD")

    ln_bufs = {"xb": xb, "x2b": x2b, "srow": srow, "pS1": pS1, "pS2": pS2,
               "pM": pM, "pR": pR, "onesb": onesb, "onesf": onesf, "cvals": cvals}
    ln_state = {"xb_free": ("LD", ld_const), "pMR_free": None, "pS_free": None}

    bank_free = {"pA": None, "pB": None, "pTb": None, "pTf": None}
    st_scratch = 0

    wl_free = None
    for wsrc, wdst, ntile, Wq in ((hbm["f1_wg"], wgq, 22, 1024),
                                  (hbm["f1_wu"], wuq, 22, 1024),
                                  (hbm["f1_wd"], wdq, 8, FI1)):
        for a in range(ntile):
            src = wsrc[a * 128:(a + 1) * 128, :]
            ld = s.dma("sync", lambda e, sc=src, w=Wq: e.dma_start(wload[:, 0:w], sc),
                       [wl_free] if wl_free else [])
            for off in range(0, Wq, 1408):
                wqw = min(1408, Wq - off)
                qd = emit_quant(s, wload[:, off:off + wqw], wqnat[:, off:off + wqw],
                                wqw, tm, [ld])
            wl_free = ("VE", qd)
            nfb = Wq // 128
            ev = None
            for fb in range(nfb):
                tp = s.add("tensor", lambda e, f=fb: e.transpose(
                    pTb[:, :], wqnat[:, f * 128:(f + 1) * 128], idb[:, :]),
                    [("VE", qd), bank_free["pTb"]], inc="TE")
                ev = s.add("vector", lambda e, f=fb: e.tensor_copy(
                    wstage[:, f, :], pTb[:, :]),
                    [("TE", tp), ("ST", st_scratch)], inc="VE")
                bank_free["pTb"] = ("VE", ev)
            if Wq == 1024:
                dst = wdst[:, :, a * 128:(a + 1) * 128].rearrange("b p f -> p b f")
                st_scratch = s.dma("sync", lambda e, d=dst: e.dma_start(
                    d, wstage[:, 0:8, :]), [("VE", ev)], inc="ST")[1]
            else:
                dst = wdst[a].rearrange("(b p) f -> p b f", p=128)
                st_scratch = s.dma("sync", lambda e, d=dst: e.dma_start(
                    d, wstage[:, 0:22, :]), [("VE", ev)], inc="ST")[1]
    wq_ready = ("ST", st_scratch)

    slab_meta = []
    for i in range(16):
        for fo in range(22):
            slab_meta.append(("g", fo))
            slab_meta.append(("u", fo))
        for fo in range(8):
            slab_meta.append(("d", fo))
    slab_done = [None] * len(slab_meta)
    slab_wd = [None] * len(slab_meta)
    n_loaded = [0]

    def load_slab():
        k = n_loaded[0]
        if k >= len(slab_meta):
            return
        kind, fo = slab_meta[k]
        waits = [wq_ready]
        if k >= 2:
            assert slab_done[k - 2] is not None
            waits.append(("TE", slab_done[k - 2]))
        if kind == "d":
            buf = k % 2
            src = wdq[fo].rearrange("(b p) f -> p b f", p=128)
            dst = wdsl[:, buf, :, :]
        else:
            buf = k % 2
            hb = wgq if kind == "g" else wuq
            src = hb[:, :, fo * 128:(fo + 1) * 128].rearrange("b p f -> p b f")
            dst = wgu[:, buf, :, :]
        slab_wd[k] = s.dma("gpsimd", lambda e, d=dst, sc=src: e.dma_start(d, sc), waits)
        n_loaded[0] += 1

    load_slab()
    load_slab()

    ot_store = [0, 0]
    st_total = 0

    for b in range(BPC):
        tb0 = b * T
        ve_ln = None
        for j in range(4):
            ve_ln = emit_ln(s, ln_bufs, res_in, tb0 + j * 512, 512, xn, j * 512, ln_state)

        for tblk in range(4):
            tc0 = tblk * 512
            xqT_ready = None
            for tt in range(4):
                tp_done = None
                for ft in range(8):
                    tp = s.add("tensor", lambda e, f=ft, c=tc0 + tt * 128: e.transpose(
                        pTb[:, :], xn[:, f, c:c + 128], idb[:, :]),
                        [("VE", ve_ln), bank_free["pTb"]], inc="TE")
                    tp_done = s.add("vector", lambda e, f=ft: e.tensor_copy(
                        xnat[:, f, :], pTb[:, :]), [("TE", tp)], inc="VE")
                    bank_free["pTb"] = ("VE", tp_done)
                qd = emit_quant(s, xnat[:, :, :].rearrange("p a b -> p (a b)"),
                                xqn[:, :, :].rearrange("p a b -> p (a b)"),
                                1024, tm, [("VE", tp_done), ("TE", s.val("TE"))])
                for ft in range(8):
                    tp = s.add("tensor", lambda e, f=ft: e.transpose(
                        pTb[:, :], xqn[:, f, :], idb[:, :]),
                        [("VE", qd), bank_free["pTb"]], inc="TE")
                    xqT_ready = s.add("vector", lambda e, f=ft, c=tt * 128: e.tensor_copy(
                        xqT[:, f, c:c + 128], pTb[:, :]), [("TE", tp)], inc="VE")
                    bank_free["pTb"] = ("VE", xqT_ready)

            h_ready = None
            for fo in range(22):
                kg = slab_meta_base = (b * 4 + tblk) * 52 + 2 * fo
                ku = kg + 1
                bg, bu = kg % 2, ku % 2
                g1 = s.add("tensor", lambda e, bb=bg: [
                    e.matmul(pA[:, :], wgu[:, bb, fi, :], xqT[:, fi, :],
                             start=(fi == 0), stop=(fi == 7)) for fi in range(8)][-1],
                    [slab_wd[kg], ("VE", xqT_ready), bank_free["pA"]], inc="TE")
                g2 = s.add("tensor", lambda e, bb=bu: [
                    e.matmul(pB[:, :], wgu[:, bb, fi, :], xqT[:, fi, :],
                             start=(fi == 0), stop=(fi == 7)) for fi in range(8)][-1],
                    [slab_wd[ku], bank_free["pB"]], inc="TE")
                r = fo % 2
                sc_s = s.add("scalar", lambda e, rr=r: e.activation(
                    stmp[:, rr, :], pA[:, :], AF.Silu),
                    [("TE", g1), ("VE", s.val("VE"))], inc="SC")
                bank_free["pA"] = ("SC", sc_s)
                ve_h = s.add("vector", lambda e, rr=r: e.tensor_mul(
                    htile[:, rr, :], stmp[:, rr, :], pB[:, :]),
                    [("TE", g2), ("SC", sc_s)], inc="VE")
                bank_free["pB"] = ("VE", ve_h)
                for tt in range(4):
                    tp = s.add("tensor", lambda e, rr=r, c=tt * 128: e.transpose(
                        pTf[:, :], htile[:, rr, c:c + 128], idf[:, :]),
                        [("VE", ve_h), bank_free["pTf"]], inc="TE")
                    h_ready = s.add("vector", lambda e, f=fo, t=tt: e.tensor_copy(
                        h_nat[:, t, f * 128:(f + 1) * 128], pTf[:, :]),
                        [("TE", tp), ("SC", tm.get("fSC", 0))], inc="VE")
                    bank_free["pTf"] = ("VE", h_ready)
                slab_done[kg] = s.val("TE")
                slab_done[ku] = s.val("TE")
                load_slab()
                load_slab()

            hqT_ready = None
            for tt in range(4):
                rr = tt % 2
                for half in range(2):
                    qd = emit_quant(s, h_nat[:, tt, half * 1408:(half + 1) * 1408],
                                    hqn[:, rr, half * 1408:(half + 1) * 1408],
                                    1408, tm, [("VE", h_ready), ("TE", s.val("TE"))])
                for ft in range(22):
                    tp = s.add("tensor", lambda e, f=ft, r2=rr: e.transpose(
                        pTb[:, :], hqn[:, r2, f * 128:(f + 1) * 128], idb[:, :]),
                        [("VE", qd), bank_free["pTb"]], inc="TE")
                    hqT_ready = s.add("vector", lambda e, f=ft, c=tt * 128: e.tensor_copy(
                        hqT[:, f, c:c + 128], pTb[:, :]), [("TE", tp)], inc="VE")
                    bank_free["pTb"] = ("VE", hqT_ready)

            for fo in range(8):
                kd = (b * 4 + tblk) * 52 + 44 + fo
                bd = kd % 2
                src = res_in[fo * 128:(fo + 1) * 128, tb0 + tc0:tb0 + tc0 + 512]
                r = fo % 2
                ld_r = s.dma("sync", lambda e, sc=src, rr=r: e.dma_start(
                    xres[:, rr, :], sc), [("VE", s.val("VE"))])
                pname = "pA" if fo % 2 == 0 else "pB"
                pp = pA if pname == "pA" else pB
                g = s.add("tensor", lambda e, bb=bd, p=pp: [
                    e.matmul(p[:, :], wdsl[:, bb, fi, :], hqT[:, fi, :],
                             start=(fi == 0), stop=(fi == 21)) for fi in range(22)][-1],
                    [slab_wd[kd], ("VE", hqT_ready), bank_free[pname]], inc="TE")
                ve_o = s.add("vector", lambda e, rr=r, p=pp: e.tensor_add(
                    ot[:, rr, :], p[:, :], xres[:, rr, :]),
                    [("TE", g), ld_r, ("ST", ot_store[r])], inc="VE")
                bank_free[pname] = ("VE", ve_o)
                dst = res_out[fo * 128:(fo + 1) * 128, tb0 + tc0:tb0 + tc0 + 512]
                st_total = s.dma("sync", lambda e, d=dst, rr=r: e.dma_start(
                    d, ot[:, rr, :]), [("VE", ve_o)], inc="ST")[1]
                ot_store[r] = st_total
                slab_done[kd] = s.val("TE")
                load_slab()

    s.add("sync", lambda e: e.nop(), [("ST", st_total)])
    return st_total


def _f1_weight_dram(nc, hbm):
    hbm["f1_wg"] = nc.dram_tensor("wg", [FI1, 1024], F32, kind="ExternalInput").ap()
    hbm["f1_wu"] = nc.dram_tensor("wu", [FI1, 1024], F32, kind="ExternalInput").ap()
    hbm["f1_wd"] = nc.dram_tensor("wd", [1024, FI1], F32, kind="ExternalInput").ap()
    return hbm


def _mk_f1_dram(nc):
    hbm = {}
    hbm["f1_res_in"] = nc.dram_tensor("res_in", [D, TQ], F32, kind="ExternalInput").ap()
    hbm["f1_res_out"] = nc.dram_tensor("res_out", [D, TQ], F32, kind="ExternalOutput").ap()
    _f1_weight_dram(nc, hbm)
    _const_dram(nc, hbm)
    return hbm


def build_f1_nc():
    nc = bass.Bass("TRN2", target_bir_lowering=False, debug=False)
    hbm = _mk_f1_dram(nc)
    s = Sched()
    with ExitStack() as es:
        sems = {n: es.enter_context(nc.semaphore(n)) for n in SEM_NAMES}
        emit_f1(s, nc, es, hbm)
        s.emit(nc, sems)
    return nc


def f1_host_inputs(inputs):
    return {
        "wg": np.ascontiguousarray(np.asarray(inputs["L1_ffn_w_gate"], np.float32)),
        "wu": np.ascontiguousarray(np.asarray(inputs["L1_ffn_w_up"], np.float32)),
        "wd": np.ascontiguousarray(np.asarray(inputs["L1_ffn_w_down"], np.float32)),
        "onesf": np.ones((128, 128), np.float32),
        "onesb": np.ones((128, 128), bf16),
        "idb": np.eye(128, dtype=np.float32).astype(bf16),
        "idf": np.eye(128, dtype=np.float32),
        "cvals": _cvals(),
    }


def run_f1(res_T_shards, common, core_ids=None):
    if core_ids is None:
        core_ids = list(range(len(res_T_shards)))
    nc = build_f1_nc()
    in_maps = [dict(common, res_in=r) for r in res_T_shards]
    res = run_bass_kernel_spmd(nc, in_maps, core_ids=core_ids)
    return [r["res_out"] for r in res.results]



MH = 8
MP = 64
MN = 128
NCK = 16
MAGIC = 12582912.0
TWO_PI = float(2.0 * math.pi)


def emit_mamba(s, nc, es, hbm, pfx="mb_", cache_jobs=()):
    sb = lambda name, shape, dt: es.enter_context(nc.sbuf_tensor(pfx + name, shape, dt))
    ps = lambda name, shape, dt=F32: es.enter_context(nc.psum_tensor(pfx + name, shape, dt))

    res_in = hbm["mb_res_in"]
    res_out = hbm["mb_res_out"]
    lscr = nc.dram_tensor("mb_lscr", [8, 2049], F32, kind="Internal").ap()

    onesf = sb("onesf", [128, 128], F32)
    onesb = sb("onesb", [128, 128], BF16)
    cvals = sb("cvals", [128, 8], F32)
    idf = sb("idf", [128, 128], F32)
    idb = sb("idb", [128, 128], BF16)
    tri = sb("tri", [128, 128], F32)
    selm = sb("selm", [8, 512], BF16)
    avec = sb("avec", [8, 4], F32)
    rmsw = sb("rmsw", [1, 512], F32)
    rmsb = sb("rmsb", [128, 512], BF16)
    xb = sb("xb", [128, 8, 512], BF16)
    x2b = sb("x2b", [128, 2, 512], BF16)
    srow = sb("srow", [1, 3, 512], F32)
    xny = sb("xny", [128, 16384], BF16)
    HT = 1024
    uT = sb("uT", [128, 4, HT + 1], BF16)
    Bz0 = sb("Bz0", [128, 4, HT + 1], BF16)
    Bz1 = sb("Bz1", [128, 4, HT + 1], BF16)
    Cz0 = sb("Cz0", [128, 4, HT], BF16)
    Cz1 = sb("Cz1", [128, 4, HT], BF16)
    winp = sb("winp", [128, 8, 512], BF16)
    selS = sb("selS", [128, 8, 24], BF16)
    outp = sb("outp", [128, 4, 1024], BF16)
    wsl = sb("wsl", [128, 2, 8, 128], BF16)
    rb_ = sb("rb", [8, 10, 2048], BF16)
    rf_ = sb("rf", [8, 2, 2048], F32)
    lpad = sb("lpad", [8, 2049], F32)
    lpfh = sb("lpfh", [1, 1032], F32)
    colsb = sb("colsb", [128, 7, 8, 8], F32)
    hstate = sb("hstate", [128, 8, 64], F32)
    hstateb = sb("hstateb", [128, 8, 64], BF16)
    DTm = sb("DTm", [128, 128], F32)
    DTt = sb("DTt", [128, 128], F32)
    m1m = sb("m1m", [128, 128], F32)
    m2m = sb("m2m", [128, 128], F32)
    M1 = sb("M1", [128, 128], BF16)
    M2 = sb("M2", [128, 128], BF16)
    decb = sb("decb", [128, 128], F32)
    Cdec = sb("Cdec", [128, 128], BF16)
    Unat = sb("Unat", [128, 128], BF16)
    Bgn = sb("Bgn", [128, 128], BF16)
    Bbn = sb("Bbn", [128, 128], BF16)
    hacc = sb("hacc", [64, 128], F32)
    cb = sb("cb", [128, 512], BF16)
    sbb = sb("sbb", [128, 512], BF16)
    rt = sb("rt", [128, 4, 512], BF16)
    y2 = sb("y2", [128, 512], BF16)
    ms = sb("ms", [128, 2], F32)
    yrms = sb("yrms", [128, 512], BF16)
    yT = sb("yT", [128, 4, 128], BF16)
    xres = sb("xres", [128, 8, 128], F32)
    ot = sb("ot", [128, 2, 128], F32)

    pS1 = ps("pS1", [1, 512])
    pS2 = ps("pS2", [1, 512])
    pM = ps("pM", [128, 512])
    pR = ps("pR", [128, 512])
    bkA = ps("bkA", [128, 512])
    bkB = ps("bkB", [128, 512])
    bkC = ps("bkC", [128, 512], BF16)

    for dst, key in ((onesf, "c_onesf"), (onesb, "c_onesb"), (cvals, "c_cvals"),
                     (idf, "c_idf"), (idb, "c_idb"), (tri, "mb_tri"),
                     (selm, "mb_selm"), (avec, "mb_avec"), (rmsw, "mb_rmsw")):
        s.dma("sync", lambda e, d=dst, k=key: e.dma_start(d[:, :], hbm[k][:, :]))
    ld_wres = s.dma("gpsimd", lambda e: e.dma_start(
        winp[:, :, :], hbm["mb_inp"][:, :, :]))
    ld_sels = s.dma("gpsimd", lambda e: e.dma_start(
        selS[:, :, :], hbm["mb_selS"].rearrange("a p f -> p a f")))
    ld_outp = s.dma("gpsimd", lambda e: e.dma_start(
        outp[:, :, :], hbm["mb_outp"][:, :, :]))
    ld_const = s.val("LD")

    cache_it = iter(cache_jobs)
    cache_last = {"at": None, "f0": None}

    def emit_cache(n):
        for _ in range(n):
            job = next(cache_it, None)
            if job is None:
                return
            tag, dst, src = job
            cache_last[tag] = s.dma("gpsimd",
                                    lambda e, d=dst, sc=src: e.dma_start(d, sc))

    ln_bufs = {"xb": xb, "x2b": x2b, "srow": srow, "pS1": pS1, "pS2": pS2,
               "pM": pM, "pR": pR, "onesb": onesb, "onesf": onesf, "cvals": cvals}
    ln_state = {"xb_free": ("LD", ld_const), "pMR_free": None, "pS_free": None}

    xnv = xny[:, :].rearrange("p (a t) -> p a t", t=T)

    sc_a = s.add("scalar", lambda e: e.activation(
        avec[0:8, 2:3], avec[0:8, 0:1], AF.Exp), [("LD", ld_const)], inc="SC")
    ve_a = s.add("vector", lambda e: e.tensor_scalar_mul(
        avec[0:8, 2:3], avec[0:8, 2:3], -1.0), [("SC", sc_a)], inc="VE")
    te_rw = s.add("tensor", lambda e: e.matmul(
        pM[:, :], onesf[0:1, :], rmsw[0:1, :], start=True, stop=True),
        [("LD", ld_const)], inc="TE")
    ve_rw = s.add("vector", lambda e: e.tensor_copy(rmsb[:, :], pM[:, :]),
                  [("TE", te_rw)], inc="VE")
    ln_state["pMR_free"] = ("VE", ve_rw)

    bank_free = {"pM": ("VE", ve_rw), "pR": None, "bkA": None, "bkB": None,
                 "bkC": None}
    n_slab = [0]
    slab_wd = [None] * (16 * 8)
    slab_done = [None] * (16 * 8)

    def load_slab():
        k = n_slab[0]
        if k >= 16 * 8:
            return
        waits = []
        if k >= 2:
            waits.append(("TE", slab_done[k - 2]))
        ft = k % 16
        src = hbm["mb_selT"][ft].rearrange("(a p) f -> p a f", p=128)
        dst = wsl[:, k % 2, :, :]
        slab_wd[k] = s.dma("gpsimd", lambda e, d=dst, sc=src: e.dma_start(d, sc), waits)
        n_slab[0] += 1

    load_slab()
    load_slab()

    ot_store = [0, 0]
    st_total = 0
    hq = {"lpad_free": None, "lpfh_free": None, "h_free": None}
    y_done = None
    RB = lambda i: rb_[0:8, i, :]
    RF = lambda i: rf_[0:8, i, :]

    for b in range(BPC):
        tb0 = b * T
        ve_ln = None
        for j in range(4):
            ve_ln = emit_ln(s, ln_bufs, res_in, tb0 + j * 512, 512, xnv, j * 512, ln_state)
        bank_free["pM"] = ln_state["pMR_free"]
        bank_free["pR"] = ln_state["pMR_free"]

        def _half_body(half):
            hc0 = half * HT
            nsl = b * 32 + half * 16

            if half == 0:
                ve_z = s.add("vector", lambda e: [
                    e.memset(uT[:, :, 0:1], 0.0),
                    e.memset(Bz0[:, :, 0:1], 0.0),
                    e.memset(Bz1[:, :, 0:1], 0.0),
                ][-1], [("TE", s.val("TE"))], inc="VE")
            else:
                ve_z = s.add("vector", lambda e: [
                    e.tensor_copy(uT[:, :, 0:1], uT[:, :, HT:HT + 1]),
                    e.tensor_copy(Bz0[:, :, 0:1], Bz0[:, :, HT:HT + 1]),
                    e.tensor_copy(Bz1[:, :, 0:1], Bz1[:, :, HT:HT + 1]),
                ][-1], [("TE", s.val("TE")), ("VE", s.val("VE"))], inc="VE")

            for a in range(4):
                for blk in range(2):
                    pn = "pM" if (a * 2 + blk) % 2 == 0 else "pR"
                    pp = pM if pn == "pM" else pR
                    g = s.add("tensor", lambda e, aa=a, bl=blk, p=pp: [
                        e.matmul(p[:, :], winp[:, fi, aa * 128:(aa + 1) * 128],
                                 xnv[:, fi, hc0 + bl * 512:hc0 + (bl + 1) * 512],
                                 start=(fi == 0), stop=(fi == 7)) for fi in range(8)][-1],
                        [ld_wres, ("VE", ve_ln), bank_free[pn]], inc="TE")
                    sc_e = s.add("scalar", lambda e, aa=a, bl=blk, p=pp: e.activation(
                        uT[:, aa, 1 + bl * 512:1 + (bl + 1) * 512], p[:, :], AF.Identity),
                        [("TE", g), ("VE", ve_z)], inc="SC")
                    bank_free[pn] = ("SC", sc_e)

            for ft in range(16):
                k = nsl + ft
                if ft < 4:
                    dstb, da, c0 = Bz0, ft, 1
                elif ft < 8:
                    dstb, da, c0 = Bz1, ft - 4, 1
                elif ft < 12:
                    dstb, da, c0 = Cz0, ft - 8, 0
                else:
                    dstb, da, c0 = Cz1, ft - 12, 0
                for blk in range(2):
                    pn = "pM" if blk % 2 == 0 else "pR"
                    pp = pM if pn == "pM" else pR
                    g = s.add("tensor", lambda e, kk=k, bl=blk, p=pp: [
                        e.matmul(p[:, :], wsl[:, kk % 2, fi, :],
                                 xnv[:, fi, hc0 + bl * 512:hc0 + (bl + 1) * 512],
                                 start=(fi == 0), stop=(fi == 7)) for fi in range(8)][-1],
                        [slab_wd[k], bank_free[pn]], inc="TE")
                    sc_e = s.add("scalar", lambda e, d=dstb, aa=da, bl=blk, cc=c0, p=pp:
                                 e.activation(d[:, aa, cc + bl * 512:cc + (bl + 1) * 512],
                                              p[:, :], AF.Identity),
                                 [("TE", g), ("VE", ve_z)], inc="SC")
                    bank_free[pn] = ("SC", sc_e)
                slab_done[k] = s.val("TE")
                load_slab()

            for si in range(3):
                for blk in range(2):
                    g = s.add("tensor", lambda e, ss=si, bl=blk: [
                        e.matmul(pM[0:8, :], selS[:, fi, ss * 8:(ss + 1) * 8],
                                 xnv[:, fi, hc0 + bl * 512:hc0 + (bl + 1) * 512],
                                 start=(fi == 0), stop=(fi == 7)) for fi in range(8)][-1],
                        [ld_sels, bank_free["pM"]], inc="TE")
                    sc_e = s.add("scalar", lambda e, ss=si, bl=blk: e.activation(
                        rb_[0:8, ss, hc0 + bl * 512:hc0 + (bl + 1) * 512], pM[0:8, :],
                        AF.Identity), [("TE", g)], inc="SC")
                    bank_free["pM"] = ("SC", sc_e)

            H = lambda i: rb_[0:8, i, hc0:hc0 + HT]
            HF = lambda i: rf_[0:8, i, hc0:hc0 + HT]
            sc_last = s.val("SC")
            v1 = s.add("vector", lambda e: e.tensor_scalar_add(H(0), H(0), avec[0:8, 1:2]),
                       [("SC", sc_last), ("VE", ve_a)], inc="VE")
            v2 = s.add("scalar", lambda e: e.activation(H(4), H(0), AF.Abs),
                       [("VE", v1)], inc="SC")
            s1 = s.add("scalar", lambda e: e.activation(H(4), H(4), AF.Exp, scale=-1.0),
                       [("SC", v2)], inc="SC")
            s2 = s.add("scalar", lambda e: e.activation(
                H(4), H(4), AF.Ln, bias=cvals[0:8, 2:3]), [("SC", s1)], inc="SC")
            v3 = s.add("vector", lambda e: e.tensor_scalar_max(H(3), H(0), 0.0),
                       [("SC", s2)], inc="VE")
            v4 = s.add("vector", lambda e: e.tensor_add(H(3), H(3), H(4)),
                       [("VE", v3), ("SC", s2)], inc="VE")
            s3 = s.add("scalar", lambda e: e.activation(H(1), H(1), AF.Sigmoid),
                       [("SC", sc_last)], inc="SC")
            v5 = s.add("vector", lambda e: e.tensor_mul(H(5), H(1), H(3)),
                       [("SC", s3), ("VE", v4)], inc="VE")
            v6 = s.add("vector", lambda e: e.tensor_scalar_mul(HF(1), H(3), avec[0:8, 2:3]),
                       [("VE", v5)], inc="VE")
            if half == 0:
                zp = s.add("vector", lambda e: e.memset(lpad[0:8, 0:1], 0.0),
                           [hq["lpad_free"]], inc="VE")
                zbc = cvals[0:8, 3:4].broadcast_to((8, HT))
                v7 = s.add("vector", lambda e, zb=zbc: e.tensor_tensor_scan(
                    lpad[0:8, 1:1 + HT], zb, HF(1), 0.0, ALU.add, ALU.add),
                    [("VE", v6), ("VE", zp), hq["lpad_free"]], inc="VE")
            else:
                zbc = cvals[0:8, 3:4].broadcast_to((8, HT))
                v7 = s.add("vector", lambda e, zb=zbc: e.tensor_tensor_scan(
                    lpad[0:8, 1 + HT:2049], zb, HF(1), lpad[0:8, HT:HT + 1],
                    ALU.add, ALU.add),
                    [("VE", v6)], inc="VE")
            s4 = s.add("scalar", lambda e: e.activation(H(4), HF(1), AF.Exp),
                       [("VE", v7)], inc="SC")
            v8 = s.add("vector", lambda e: e.tensor_scalar(
                H(6), H(1), -1.0, 1.0, ALU.mult, ALU.add), [("SC", s4)], inc="VE")
            v9 = s.add("vector", lambda e: e.tensor_mul(H(6), H(6), H(3)),
                       [("VE", v8)], inc="VE")
            v10 = s.add("vector", lambda e: e.tensor_mul(H(6), H(6), H(4)),
                        [("VE", v9), ("SC", s4)], inc="VE")
            v11 = s.add("vector", lambda e: e.tensor_mul(H(4), H(3), H(2)),
                        [("VE", v10)], inc="VE")
            zbc2 = cvals[0:8, 3:4].broadcast_to((8, HT))
            if half == 0:
                v12 = s.add("vector", lambda e, zb=zbc2: e.tensor_tensor_scan(
                    HF(0), zb, H(4), 0.0, ALU.add, ALU.add), [("VE", v11)], inc="VE")
            else:
                v12 = s.add("vector", lambda e, zb=zbc2: e.tensor_tensor_scan(
                    HF(0), zb, H(4), rf_[0:8, 0, HT - 1:HT], ALU.add, ALU.add),
                    [("VE", v11)], inc="VE")

            def rr_sin(dst_slot, shift, wprev):
                w1a = s.add("vector", lambda e, sh=shift: e.tensor_scalar(
                    HF(1), HF(0), 1.0 / TWO_PI, sh / TWO_PI, ALU.mult, ALU.add),
                    [wprev], inc="VE")
                w1 = s.add("vector", lambda e: e.tensor_scalar_add(HF(1), HF(1), MAGIC),
                           [("VE", w1a)], inc="VE")
                w2 = s.add("vector", lambda e: e.tensor_scalar_sub(HF(1), HF(1), MAGIC),
                           [("VE", w1)], inc="VE")
                w3 = s.add("vector", lambda e, sh=shift: e.tensor_scalar(
                    HF(1), HF(1), -TWO_PI, sh, ALU.mult, ALU.add), [("VE", w2)], inc="VE")
                w4 = s.add("vector", lambda e: e.tensor_add(HF(1), HF(1), HF(0)),
                           [("VE", w3)], inc="VE")
                w5 = s.add("scalar", lambda e, d=dst_slot: e.activation(H(d), HF(1), AF.Sin),
                           [("VE", w4)], inc="SC")
                return w5
            s5 = rr_sin(2, 0.0, ("VE", v12))
            s6 = rr_sin(9, math.pi / 2.0, ("SC", s5))

            v13 = s.add("vector", lambda e: e.tensor_copy(
                HF(1).rearrange("p (c q) -> p c q", q=128),
                lpad[0:8, 128 + hc0:1025 + hc0:128].unsqueeze(2)
                .broadcast_to((8, 8, 128))), [("SC", s6)], inc="VE")
            v14 = s.add("vector", lambda e: e.tensor_sub(
                HF(1).rearrange("p (c q) -> p c q", q=128),
                HF(1).rearrange("p (c q) -> p c q", q=128),
                lpad[0:8, hc0:hc0 + HT:128].unsqueeze(2).broadcast_to((8, 8, 128))),
                [("VE", v13)], inc="VE")
            s7 = s.add("scalar", lambda e: e.activation(H(4), HF(1), AF.Exp),
                       [("VE", v14)], inc="SC")
            v15 = s.add("vector", lambda e: e.tensor_copy(
                HF(1).rearrange("p (c q) -> p c q", q=128),
                lpad[0:8, 128 + hc0:1025 + hc0:128].unsqueeze(2)
                .broadcast_to((8, 8, 128))), [("SC", s7)], inc="VE")
            v16 = s.add("vector", lambda e: e.tensor_sub(
                HF(1), HF(1), lpad[0:8, 1 + hc0:1 + hc0 + HT]), [("VE", v15)], inc="VE")
            s8 = s.add("scalar", lambda e: e.activation(H(7), HF(1), AF.Exp),
                       [("VE", v16)], inc="SC")
            v17 = s.add("vector", lambda e: e.tensor_mul(H(8), H(7), H(6)),
                        [("SC", s8)], inc="VE")
            v18 = s.add("vector", lambda e: e.tensor_mul(H(7), H(7), H(5)),
                        [("VE", v17)], inc="VE")
            v19 = s.add("vector", lambda e: e.tensor_scalar_mul(
                HF(1).rearrange("p (c q) -> p c q", q=128),
                lpad[0:8, hc0:hc0 + HT:128].unsqueeze(2).broadcast_to((8, 8, 128)),
                -1.0), [("VE", v18), ("SC", s8)], inc="VE")

            st_l = s.dma("sync", lambda e: e.dma_start(
                lscr[:, hc0:hc0 + HT + 1], lpad[0:8, hc0:hc0 + HT + 1]),
                [("VE", v7), hq["lpfh_free"]], inc="ST")

            col_done = None
            for cl in range(8):
                ca = half * 8 + cl
                tp = s.add("tensor", lambda e, o=1 + ca * 128: e.transpose(
                    bkA[:, 384:392], lpad[0:8, o:o + 128], idf[0:8, 0:8]),
                    [("VE", v19), bank_free["bkA"]], inc="TE")
                col_done = s.add("vector", lambda e, cc=cl: e.tensor_scalar_mul(
                    colsb[:, 0, cc, :], bkA[:, 384:392], -1.0),
                    [("TE", tp)], inc="VE")
                bank_free["bkA"] = ("VE", col_done)
                tp = s.add("tensor", lambda e, o=hc0 + cl * 128: e.transpose(
                    bkA[:, 384:392], rf_[0:8, 1, o:o + 128], idf[0:8, 0:8]),
                    [("VE", v19), bank_free["bkA"]], inc="TE")
                col_done = s.add("vector", lambda e, cc=cl: e.tensor_copy(
                    colsb[:, 6, cc, :], bkA[:, 384:392]),
                    [("TE", tp)], inc="VE")
                bank_free["bkA"] = ("VE", col_done)
                for fam, slot in ((1, 5), (2, 6), (3, 7), (4, 8), (5, 4)):
                    tp = s.add("tensor", lambda e, sl=slot, o=hc0 + cl * 128: e.transpose(
                        bkC[:, 384:392], rb_[0:8, sl, o:o + 128], idb[0:8, 0:8]),
                        [("VE", v18), ("SC", s7), bank_free["bkC"]], inc="TE")
                    col_done = s.add("vector", lambda e, ff=fam, cc=cl: e.tensor_copy(
                        colsb[:, ff, cc, :], bkC[:, 384:392]),
                        [("TE", tp)], inc="VE")
                    bank_free["bkC"] = ("VE", col_done)

            rope_done = None
            for a in range(4):
                for blk in range(2):
                    cof = hc0 + blk * 512
                    g1 = s.add("tensor", lambda e, aa=a, c2=cof: e.matmul(
                        pM[:, :], selm[0:8, aa * 128:(aa + 1) * 128],
                        rb_[0:8, 9, c2:c2 + 512], start=True, stop=True),
                        [("SC", s6), bank_free["pM"]], inc="TE")
                    g2 = s.add("tensor", lambda e, aa=a, c2=cof: e.matmul(
                        pR[:, :], selm[0:8, aa * 128:(aa + 1) * 128],
                        rb_[0:8, 2, c2:c2 + 512], start=True, stop=True),
                        [bank_free["pR"]], inc="TE")
                    e1 = s.add("scalar", lambda e: e.activation(cb[:, :], pM[:, :],
                                                                AF.Identity),
                               [("TE", g1), ("VE", s.val("VE"))], inc="SC")
                    e2 = s.add("scalar", lambda e: e.activation(sbb[:, :], pR[:, :],
                                                                AF.Identity),
                               [("TE", g2), ("SC", e1)], inc="SC")
                    bank_free["pM"] = ("SC", e1)
                    bank_free["pR"] = ("SC", e2)
                    ww = [("SC", e2), ("VE", s.val("VE"))]
                    for za, zb, off in ((Bz0, Bz1, 1), (Cz0, Cz1, 0)):
                        csl = slice(off + blk * 512, off + (blk + 1) * 512)
                        t1 = s.add("vector", lambda e, z=za, aa=a, c2=csl: e.tensor_mul(
                            rt[:, 0, :], z[:, aa, c2], cb[:, :]), ww, inc="VE")
                        t2 = s.add("vector", lambda e, z=zb, aa=a, c2=csl: e.tensor_mul(
                            rt[:, 1, :], z[:, aa, c2], sbb[:, :]), [("VE", t1)], inc="VE")
                        t3 = s.add("vector", lambda e, z=za, aa=a, c2=csl: e.tensor_mul(
                            rt[:, 2, :], z[:, aa, c2], sbb[:, :]), [("VE", t2)], inc="VE")
                        t4 = s.add("vector", lambda e, z=zb, aa=a, c2=csl: e.tensor_mul(
                            rt[:, 3, :], z[:, aa, c2], cb[:, :]), [("VE", t3)], inc="VE")
                        t5 = s.add("vector", lambda e, z=za, aa=a, c2=csl: e.tensor_sub(
                            z[:, aa, c2], rt[:, 0, :], rt[:, 1, :]), [("VE", t4)], inc="VE")
                        rope_done = s.add("vector", lambda e, z=zb, aa=a, c2=csl: e.tensor_add(
                            z[:, aa, c2], rt[:, 2, :], rt[:, 3, :]), [("VE", t5)], inc="VE")
                        ww = [("VE", rope_done)]

            if half == 0 and True:
                pass
            if half == 0:
                ve_h00 = s.add("vector", lambda e: e.memset(hstate[:, :, :], 0.0),
                               [("TE", s.val("TE")), hq["h_free"]], inc="VE")
                ve_h0 = s.add("vector", lambda e: e.memset(hstateb[:, :, :], 0.0),
                              [("VE", ve_h00)], inc="VE")
                hq["h_free"] = ("VE", ve_h0)
            for h in range(MH):
                a = h // 2
                hp = (h % 2) * 64
                ld_l = s.dma("sync", lambda e, hh=h: e.dma_start(
                    lpfh[0:1, 0:HT + 1],
                    lscr[hh, hc0:hc0 + HT + 1][None, :]),
                    [st_l, hq["lpfh_free"]])
                for cl in range(8):
                    w0 = 1 + cl * 128
                    ws = cl * 128
                    g_l = s.add("tensor", lambda e, o=w0: e.matmul(
                        bkA[:, 256:384], onesf[0:1, :], lpfh[0:1, o:o + 128],
                        start=True, stop=True),
                        [ld_l, bank_free["bkA"]], inc="TE")
                    v_dt = s.add("vector", lambda e, cc=cl, hh=h: e.tensor_scalar(
                        DTm[:, :], bkA[:, 256:384], colsb[:, 0, cc, hh:hh + 1], 0.0,
                        ALU.add, ALU.min),
                        [("TE", g_l), ("VE", s.val("VE")), ("SC", s.val("SC"))], inc="VE")
                    e_dt = s.add("scalar", lambda e: e.activation(
                        DTm[:, :], DTm[:, :], AF.Exp),
                        [("VE", v_dt)], inc="SC")
                    e_db = s.add("scalar", lambda e, cc=cl, hh=h: e.activation(
                        decb[:, :], bkA[:, 256:384], AF.Exp,
                        bias=colsb[:, 6, cc, hh:hh + 1]),
                        [("SC", e_dt)], inc="SC")
                    bank_free["bkA"] = ("SC", e_db)
                    m_t = s.add("vector", lambda e: e.tensor_mul(DTt[:, :], DTm[:, :],
                                                                 tri[:, :]),
                                [("SC", e_dt), ("VE", s.val("VE"))], inc="VE")
                    m_2 = s.add("vector", lambda e, cc=cl, hh=h: e.tensor_scalar_mul(
                        m2m[:, :], DTt[:, :], colsb[:, 1, cc, hh:hh + 1]),
                        [("VE", m_t)], inc="VE")
                    m_1 = s.add("vector", lambda e, cc=cl, hh=h: e.tensor_scalar_mul(
                        m1m[:, :], DTt[:, :], colsb[:, 2, cc, hh:hh + 1]),
                        [("VE", m_2)], inc="VE")
                    g_s2 = s.add("tensor", lambda e, aa=a, q=hp, w=w0: [
                        e.matmul(bkA[:, 0:128], Bz0[q:q + 64, aa, w:w + 128],
                                 Cz0[q:q + 64, aa, w - 1:w + 127], start=True, stop=False),
                        e.matmul(bkA[:, 0:128], Bz1[q:q + 64, aa, w:w + 128],
                                 Cz1[q:q + 64, aa, w - 1:w + 127], start=False, stop=True)][-1],
                        [("VE", rope_done), ("SC", e_db)], inc="TE")
                    g_s1 = s.add("tensor", lambda e, aa=a, q=hp, w=w0, w2=ws: [
                        e.matmul(bkA[:, 128:256], Bz0[q:q + 64, aa, w2:w2 + 128],
                                 Cz0[q:q + 64, aa, w - 1:w + 127], start=True, stop=False),
                        e.matmul(bkA[:, 128:256], Bz1[q:q + 64, aa, w2:w2 + 128],
                                 Cz1[q:q + 64, aa, w - 1:w + 127], start=False, stop=True)][-1],
                        [], inc="TE")
                    mm2 = s.add("vector", lambda e: e.tensor_mul(M2[:, :], bkA[:, 0:128],
                                                                 m2m[:, :]),
                                [("TE", g_s1), ("VE", m_1)], inc="VE")
                    mm1 = s.add("vector", lambda e: e.tensor_mul(M1[:, :], bkA[:, 128:256],
                                                                 m1m[:, :]),
                                [("VE", mm2)], inc="VE")
                    bank_free["bkA"] = ("VE", mm1)
                    cd0 = s.add("vector", lambda e, aa=a, q=hp, w=w0: e.tensor_mul(
                        Cdec[0:64, :], Cz0[q:q + 64, aa, w - 1:w + 127], decb[q:q + 64, :]),
                        [("SC", e_db), ("VE", mm1)], inc="VE")
                    cd1 = s.add("vector", lambda e, aa=a, q=hp, w=w0: e.tensor_mul(
                        Cdec[64:128, :], Cz1[q:q + 64, aa, w - 1:w + 127], decb[q:q + 64, :]),
                        [("VE", cd0)], inc="VE")
                    g_u = s.add("tensor", lambda e, aa=a, q=hp, w=w0, w2=ws: [
                        e.transpose(bkC[:, 0:64], uT[q:q + 64, aa, w:w + 128],
                                    idb[q:q + 64, q:q + 64]),
                        e.transpose(bkC[:, 64:128], uT[q:q + 64, aa, w2:w2 + 128],
                                    idb[q:q + 64, q:q + 64]),
                    ][-1], [bank_free["bkC"], ("VE", s.val("VE")), ("SC", s.val("SC"))],
                        inc="TE")
                    e_u = s.add("vector", lambda e: e.tensor_copy(Unat[:, :], bkC[:, 0:128]),
                                [("TE", g_u)], inc="VE")
                    g_y = s.add("tensor", lambda e, hh=h: [
                        e.matmul(bkB[:, 0:64], M2[:, :], Unat[:, 0:64], start=True,
                                 stop=False),
                        e.matmul(bkB[:, 0:64], M1[:, :], Unat[:, 64:128], start=False,
                                 stop=False),
                        e.matmul(bkB[:, 0:64], Cdec[:, :], hstateb[:, hh, :],
                                 start=False, stop=True)][-1],
                        [("VE", e_u), ("VE", mm1), ("VE", cd1), bank_free["bkB"],
                         hq["h_free"]], inc="TE")
                    yc0 = cl * 2048 + half * 512
                    y_done = s.add("vector", lambda e, c0=yc0, hh=h: e.tensor_copy(
                        xny[:, c0 + hh * 64:c0 + (hh + 1) * 64], bkB[:, 0:64]),
                        [("TE", g_y)], inc="VE")
                    g_b = s.add("tensor", lambda e, aa=a, q=hp, w=w0, w2=ws: [
                        e.transpose(bkC[:, 128:192], Bz0[q:q + 64, aa, w:w + 128],
                                    idb[q:q + 64, q:q + 64]),
                        e.transpose(bkC[:, 192:256], Bz1[q:q + 64, aa, w:w + 128],
                                    idb[q:q + 64, q:q + 64]),
                        e.transpose(bkC[:, 256:320], Bz0[q:q + 64, aa, w2:w2 + 128],
                                    idb[q:q + 64, q:q + 64]),
                        e.transpose(bkC[:, 320:384], Bz1[q:q + 64, aa, w2:w2 + 128],
                                    idb[q:q + 64, q:q + 64]),
                    ][-1], [("VE", y_done)], inc="TE")
                    e_bg = s.add("scalar", lambda e, cc=cl, hh=h: e.activation(
                        Bgn[:, :], bkC[:, 128:256], AF.Identity,
                        scale=colsb[:, 3, cc, hh:hh + 1]), [("TE", g_b)], inc="SC")
                    e_bb = s.add("scalar", lambda e, cc=cl, hh=h: e.activation(
                        Bbn[:, :], bkC[:, 256:384], AF.Identity,
                        scale=colsb[:, 4, cc, hh:hh + 1]), [("SC", e_bg)], inc="SC")
                    bank_free["bkC"] = ("SC", e_bb)
                    g_h = s.add("tensor", lambda e: [
                        e.matmul(bkB[0:64, 128:256], Unat[:, 0:64], Bgn[:, :],
                                 start=True, stop=False),
                        e.matmul(bkB[0:64, 128:256], Unat[:, 64:128], Bbn[:, :],
                                 start=False, stop=True)][-1],
                        [("SC", e_bb)], inc="TE")
                    e_h = s.add("vector", lambda e: e.tensor_copy(hacc[:, :],
                                                                  bkB[0:64, 128:256]),
                                [("TE", g_h)], inc="VE")
                    g_ht = s.add("tensor", lambda e: e.transpose(
                        bkB[:, 64:128], hacc[:, :], idf[0:64, 0:64]),
                        [("VE", e_h)], inc="TE")
                    h_new = s.add("vector", lambda e, cc=cl, hh=h: e.scalar_tensor_tensor(
                        hstate[:, hh, :], hstate[:, hh, :], colsb[:, 5, cc, hh:hh + 1],
                        bkB[:, 64:128], ALU.mult, ALU.add),
                        [("TE", g_ht)], inc="VE")
                    h_mir = s.add("vector", lambda e, hh=h: e.tensor_copy(
                        hstateb[:, hh, :], hstate[:, hh, :]), [("VE", h_new)], inc="VE")
                    hq["h_free"] = ("VE", h_mir)
                    bank_free["bkB"] = ("VE", h_new)
                hq["lpfh_free"] = ("TE", s.val("TE"))
            hq["lpad_free"] = ("ST", st_l[1])
            if b == 0 and half == 0 and "mb_dbg_rows" in hbm:
                dw = [("VE", s.val("VE")), ("SC", s.val("SC")), ("TE", s.val("TE"))]
                for key, buf in (("mb_dbg_M1", M1[:, :]), ("mb_dbg_M2", M2[:, :]),
                                 ("mb_dbg_DTm", DTm[:, :]), ("mb_dbg_DTt", DTt[:, :]),
                                 ("mb_dbg_decb", decb[:, :]), ("mb_dbg_Cdec", Cdec[:, :]),
                                 ("mb_dbg_Unat", Unat[:, :]), ("mb_dbg_Bgn", Bgn[:, :]),
                                 ("mb_dbg_Bbn", Bbn[:, :]), ("mb_dbg_hacc", hacc[:, :]),
                                 ("mb_dbg_winp", winp[:, :, :]),
                                 ("mb_dbg_sels", selS[:, :, :]),
                                 ("mb_dbg_rows", rb_[0:8, :, :]),
                                 ("mb_dbg_rf", rf_[0:8, :, :]),
                                 ("mb_dbg_lpad", lpad[0:8, :]),
                                 ("mb_dbg_cols", colsb[:, :, :, :]),
                                 ("mb_dbg_bz0", Bz0[:, :, :]),
                                 ("mb_dbg_bz1", Bz1[:, :, :]),
                                 ("mb_dbg_cz0", Cz0[:, :, :]),
                                 ("mb_dbg_cz1", Cz1[:, :, :]),
                                 ("mb_dbg_ut", uT[:, :, :]),
                                 ("mb_dbg_y", xny[:, :]),
                                 ("mb_dbg_h", hstate[:, :, :])):
                    s.dma("sync", lambda e, k=key, bb=buf: e.dma_start(hbm[k], bb),
                          dw, inc="ST")
                st2 = s.val("ST")
                s.add("vector", lambda e: e.nop(), [("ST", st2)])
                s.add("tensor", lambda e: e.nop(), [("ST", st2)])
                s.add("scalar", lambda e: e.nop(), [("ST", st2)])
            hq["y_done"] = y_done

        for half in range(2):
            _half_body(half)
            emit_cache(12)

        for c in range(NCK):
            yv_c = xny[:, (c % 8) * 2048 + (c // 8) * 512:
                       (c % 8) * 2048 + (c // 8) * 512 + 512]
            rsrc = res_in[:, tb0 + c * 128:tb0 + (c + 1) * 128].rearrange(
                "(a p) t -> p a t", p=128)
            ld_r = s.dma("sync", lambda e, sc2=rsrc: e.dma_start(xres[:, :, :], sc2),
                         [("VE", s.val("VE"))])
            sq = s.add("scalar", lambda e, yy=yv_c: e.activation(y2[:, :], yy, AF.Square),
                       [("VE", hq["y_done"]), ("SC", s.val("SC"))], inc="SC")
            rd = s.add("vector", lambda e: e.reduce_sum(
                ms[:, 0:1], y2[:, :].rearrange("p (a b) -> p a b", a=1), axis=AX.X),
                [("SC", sq)], inc="VE")
            l1 = s.add("scalar", lambda e: e.activation(
                ms[:, 1:2], ms[:, 0:1], AF.Ln, scale=1.0 / 512.0,
                bias=cvals[:, 1:2]), [("VE", rd)], inc="SC")
            l2 = s.add("scalar", lambda e: e.activation(
                ms[:, 1:2], ms[:, 1:2], AF.Exp, scale=-0.5), [("SC", l1)], inc="SC")
            ya = s.add("vector", lambda e, yy=yv_c: e.tensor_scalar_mul(
                yrms[:, :], yy, ms[:, 1:2]), [("SC", l2)], inc="VE")
            yb = s.add("vector", lambda e: e.tensor_mul(yrms[:, :], yrms[:, :], rmsb[:, :]),
                       [("VE", ya)], inc="VE")
            tds = None
            for q in range(4):
                tp = s.add("tensor", lambda e, qq=q: e.transpose(
                    bkC[:, 0:128], yrms[:, qq * 128:(qq + 1) * 128], idb[:, :]),
                    [("VE", yb), bank_free["bkC"]], inc="TE")
                tds = s.add("vector", lambda e, qq=q: e.tensor_copy(
                    yT[:, qq, :], bkC[:, 0:128]), [("TE", tp)], inc="VE")
                bank_free["bkC"] = ("VE", tds)
            for fo in range(8):
                pn = "pM" if fo % 2 == 0 else "pR"
                pp = pM if pn == "pM" else pR
                g = s.add("tensor", lambda e, f=fo, p=pp: [
                    e.matmul(p[:, 0:128], outp[:, fi, f * 128:(f + 1) * 128],
                             yT[:, fi, :], start=(fi == 0), stop=(fi == 3))
                    for fi in range(4)][-1],
                    [ld_outp, ("VE", tds), bank_free[pn], ld_r], inc="TE")
                r = fo % 2
                ve_o = s.add("vector", lambda e, f=fo, rr=r, p=pp: e.tensor_add(
                    ot[:, rr, :], p[:, 0:128], xres[:, f, :]),
                    [("TE", g), ("ST", ot_store[r])], inc="VE")
                bank_free[pn] = ("VE", ve_o)
                dst = res_out[fo * 128:(fo + 1) * 128, tb0 + c * 128:tb0 + (c + 1) * 128]
                st_total = s.dma("sync", lambda e, d=dst, rr=r: e.dma_start(
                    d, ot[:, rr, :]), [("VE", ve_o)], inc="ST")[1]
                ot_store[r] = st_total

    emit_cache(1 << 30)
    s.add("sync", lambda e: e.nop(), [("ST", st_total)])
    return {"at": cache_last["at"][1] if cache_last["at"] else 0,
            "f0": cache_last["f0"][1] if cache_last["f0"] else 0}


def _mamba_weight_dram(nc, hbm):
    hbm["mb_inp"] = nc.dram_tensor("inp", [128, 8, 512], F32, kind="ExternalInput").ap()
    hbm["mb_selT"] = nc.dram_tensor("selT", [16, 1024, 128], F32, kind="ExternalInput").ap()
    hbm["mb_selS"] = nc.dram_tensor("selS", [8, 128, 24], F32, kind="ExternalInput").ap()
    hbm["mb_selTb"] = nc.dram_tensor("selTb", [16, 1024, 128], BF16, kind="Internal").ap()
    hbm["mb_outp"] = nc.dram_tensor("outp", [128, 4, 1024], F32, kind="ExternalInput").ap()
    hbm["mb_tri"] = nc.dram_tensor("tri", [128, 128], F32, kind="ExternalInput").ap()
    hbm["mb_selm"] = nc.dram_tensor("selmh", [8, 512], BF16, kind="ExternalInput").ap()
    hbm["mb_avec"] = nc.dram_tensor("avec", [8, 4], F32, kind="ExternalInput").ap()
    hbm["mb_rmsw"] = nc.dram_tensor("rmsw", [1, 512], F32, kind="ExternalInput").ap()
    return hbm


def _const_dram(nc, hbm):
    hbm["c_onesf"] = nc.dram_tensor("onesf", [128, 128], F32, kind="ExternalInput").ap()
    hbm["c_onesb"] = nc.dram_tensor("onesb", [128, 128], BF16, kind="ExternalInput").ap()
    hbm["c_idf"] = nc.dram_tensor("idf", [128, 128], F32, kind="ExternalInput").ap()
    hbm["c_idb"] = nc.dram_tensor("idb", [128, 128], BF16, kind="ExternalInput").ap()
    hbm["c_cvals"] = nc.dram_tensor("cvals", [128, 8], F32, kind="ExternalInput").ap()
    hbm["c_onesrow"] = nc.dram_tensor("onesrow", [1, 512], BF16, kind="ExternalInput").ap()
    return hbm


def _mk_mamba_dram(nc):
    hbm = {}
    hbm["mb_res_in"] = nc.dram_tensor("res_in", [D, TQ], F32, kind="ExternalInput").ap()
    hbm["mb_res_out"] = nc.dram_tensor("res_out", [D, TQ], F32, kind="ExternalOutput").ap()
    _mamba_weight_dram(nc, hbm)
    _const_dram(nc, hbm)
    return hbm


def build_mamba_nc(dbg=False):
    nc = bass.Bass("TRN2", target_bir_lowering=False, debug=False)
    hbm = _mk_mamba_dram(nc)
    if dbg:
        for key, shape, dt in (("mb_dbg_M1", [128, 128], BF16),
                               ("mb_dbg_M2", [128, 128], BF16),
                               ("mb_dbg_DTm", [128, 128], F32),
                               ("mb_dbg_DTt", [128, 128], F32),
                               ("mb_dbg_decb", [128, 128], F32),
                               ("mb_dbg_Cdec", [128, 128], BF16),
                               ("mb_dbg_Unat", [128, 128], BF16),
                               ("mb_dbg_Bgn", [128, 128], BF16),
                               ("mb_dbg_Bbn", [128, 128], BF16),
                               ("mb_dbg_hacc", [64, 128], F32),
                               ("mb_dbg_winp", [128, 8, 512], BF16),
                               ("mb_dbg_sels", [128, 8, 24], BF16),
                               ("mb_dbg_rows", [8, 10, 2048], BF16),
                               ("mb_dbg_rf", [8, 2, 2048], F32),
                               ("mb_dbg_lpad", [8, 2049], F32),
                               ("mb_dbg_cols", [128, 7, 8, 8], F32),
                               ("mb_dbg_bz0", [128, 4, 1025], BF16),
                               ("mb_dbg_bz1", [128, 4, 1025], BF16),
                               ("mb_dbg_cz0", [128, 4, 1024], BF16),
                               ("mb_dbg_cz1", [128, 4, 1024], BF16),
                               ("mb_dbg_ut", [128, 4, 1025], BF16),
                               ("mb_dbg_y", [128, 16384], BF16),
                               ("mb_dbg_h", [128, 8, 64], F32)):
            hbm[key] = nc.dram_tensor(key, shape, dt, kind="ExternalOutput").ap()
    s = Sched()
    with ExitStack() as es:
        sems = {n: es.enter_context(nc.semaphore(n)) for n in SEM_NAMES}
        emit_mamba(s, nc, es, hbm)
        s.emit(nc, sems)
    return nc


def _lhsT_pack(w):
    fo, fi = w.shape
    wt = np.asarray(w, np.float32).T.reshape(fi // 128, 128, fo)
    return np.ascontiguousarray(wt.transpose(1, 0, 2))


def mamba_host_inputs(inputs):
    selp = np.asarray(inputs["L0_attn_sel_proj_w"], np.float32)
    perm = np.arange(2072)
    for blk in range(2):
        for h in range(8):
            for n in range(128):
                orig = blk * 1024 + h * 128 + n
                perm[blk * 1024 + (n % 2) * 512 + h * 64 + n // 2] = orig
    news = selp[perm]
    selT = np.ascontiguousarray(news[0:2048].T.reshape(1024, 16, 128)
                                .transpose(1, 0, 2))
    selS = np.ascontiguousarray(news[2048:2072].T.reshape(8, 128, 24))
    selm = np.zeros((8, 512), np.float32)
    for a in range(4):
        selm[2 * a, a * 128:a * 128 + 64] = 1.0
        selm[2 * a + 1, a * 128 + 64:(a + 1) * 128] = 1.0
    avec = np.zeros((8, 4), np.float32)
    avec[:, 0] = np.asarray(inputs["L0_attn_A_log"], np.float32)
    avec[:, 1] = np.asarray(inputs["L0_attn_dt_bias"], np.float32)
    tri = np.triu(np.ones((128, 128), np.float32))
    return {
        "inp": _lhsT_pack(inputs["L0_attn_in_proj_w"]),
        "selT": selT,
        "selS": selS,
        "outp": _lhsT_pack(inputs["L0_attn_out_proj_w"]),
        "tri": tri,
        "selmh": selm.astype(bf16),
        "avec": avec,
        "rmsw": np.asarray(inputs["L0_attn_rms_weight"], np.float32).reshape(1, 512),
        "onesf": np.ones((128, 128), np.float32),
        "onesb": np.ones((128, 128), bf16),
        "idf": np.eye(128, dtype=np.float32),
        "idb": np.eye(128, dtype=np.float32).astype(bf16),
        "cvals": _cvals(),
    }


def run_mamba(res_T_shards, common, core_ids=None, dbg=False):
    if core_ids is None:
        core_ids = list(range(len(res_T_shards)))
    nc = build_mamba_nc(dbg)
    in_maps = [dict(common, res_in=r) for r in res_T_shards]
    res = run_bass_kernel_spmd(nc, in_maps, core_ids=core_ids)
    if dbg:
        return res.results
    return [r["res_out"] for r in res.results]


def build_full_nc():
    nc = bass.Bass("TRN2", target_bir_lowering=False, debug=False)
    hbm = {}
    _const_dram(nc, hbm)
    _mamba_weight_dram(nc, hbm)
    _ffn0_weight_dram(nc, hbm)
    _attn_weight_dram(nc, hbm)
    _f1_weight_dram(nc, hbm)
    x_in = nc.dram_tensor("res_in", [D, TQ], F32, kind="ExternalInput").ap()
    r1 = nc.dram_tensor("fz_r1", [D, TQ], F32, kind="Internal").ap()
    r2 = nc.dram_tensor("fz_r2", [D, TQ], F32, kind="Internal").ap()
    r3 = nc.dram_tensor("fz_r3", [D, TQ], F32, kind="Internal").ap()
    out = nc.dram_tensor("res_out", [D, TQ], F32, kind="ExternalOutput").ap()

    cache_jobs = []
    for w in ("qw", "kw", "vw", "ow"):
        for i in range(2):
            cache_jobs.append(("at", hbm["at_" + w + "b"][4 * i:4 * i + 4],
                               hbm["at_" + w][4 * i:4 * i + 4]))
    for nm in ("w1", "w2"):
        for i in range(8):
            cache_jobs.append(("f0", hbm["f0_" + nm + "b"][4 * i:4 * i + 4],
                               hbm["f0_" + nm + "t"][4 * i:4 * i + 4]))
    for nm in ("h1", "h2"):
        for i in range(32):
            cache_jobs.append(("f0", hbm["f0_" + nm + "b"][i], hbm["f0_" + nm + "t"][i]))
    for i in range(8):
        cache_jobs.append(("f0", hbm["f0_w3b"][i], hbm["f0_w3t"][i]))

    s = Sched()
    with ExitStack() as eso:
        sems = {n: eso.enter_context(nc.semaphore(n)) for n in SEM_NAMES}
        with ExitStack() as e1:
            cache_ready = emit_mamba(s, nc, e1,
                                     dict(hbm, mb_res_in=x_in, mb_res_out=r1),
                                     cache_jobs=cache_jobs)
            s.barrier()
        with ExitStack() as e2:
            emit_ffn0(s, nc, e2, dict(hbm, f0_res_in=r1, f0_res_out=r2),
                      cache_ready=cache_ready)
            s.barrier()
        with ExitStack() as e3:
            emit_attn(s, nc, e3, dict(hbm, at_res_in=r2, at_res_out=r3),
                      cache_ready=cache_ready)
            s.barrier()
        with ExitStack() as e4:
            emit_f1(s, nc, e4, dict(hbm, f1_res_in=r3, f1_res_out=out))
        s.emit(nc, sems)
    return nc


def kernel(**inputs):
    x = np.asarray(inputs["L0_attn_x"], np.float32)
    shards = [np.ascontiguousarray(x[i * BPC:(i + 1) * BPC]) for i in range(NCORES)]
    resT = [_to_T(sh) for sh in shards]

    common = {}
    common.update(mamba_host_inputs(inputs))
    common.update(ffn0_host_inputs(inputs))
    common.update(attn_host_inputs(inputs))
    common.update(f1_host_inputs(inputs))

    nc = build_full_nc()
    in_maps = [dict(common, res_in=r) for r in resT]
    res = run_bass_kernel_spmd(nc, in_maps, core_ids=list(range(NCORES)))

    out = np.empty((32, T, D), np.float32)
    for i in range(NCORES):
        out[i * BPC:(i + 1) * BPC] = _from_T(np.asarray(res.results[i]["res_out"]),
                                             BPC, T, D)
    return out


# version 06 (final)
# speedup vs baseline: 1.0346x; 1.0346x over previous
import math
from contextlib import ExitStack

import numpy as np
import ml_dtypes

import concourse.bass as bass
import concourse.mybir as mybir
from concourse.bass_utils import run_bass_kernel_spmd

F32 = mybir.dt.float32
BF16 = mybir.dt.bfloat16
AF = mybir.ActivationFunctionType
ALU = mybir.AluOpType
AX = mybir.AxisListType
bf16 = ml_dtypes.bfloat16

NCORES = 8
BPC = 4
T = 2048
TQ = BPC * T
D = 1024


class Sched:

    ENGINES = ("sync", "tensor", "scalar", "vector", "gpsimd")

    def __init__(self):
        self.ops = {e: [] for e in self.ENGINES}
        self.counts = {}

    def val(self, sem):
        return self.counts.get(sem, 0)

    def add(self, eng, fn, waits=(), inc=None, inc_by=1):
        new_val = None
        if inc is not None:
            self.counts[inc] = self.counts.get(inc, 0) + inc_by
            new_val = self.counts[inc]
        self.ops[eng].append((fn, tuple(waits), inc, inc_by))
        return new_val

    def barrier(self):
        waits = tuple((k, self.val(k)) for k in ("TE", "SC", "VE", "GP", "LD",
                                                 "ST", "WD") if self.val(k) > 0)
        for eng in self.ENGINES:
            self.add(eng, lambda e: e.nop(), waits)

    def dma(self, queue, fn, waits=(), inc=None):
        if inc is None:
            inc = "WD" if queue == "gpsimd" else "LD"
        assert (queue == "gpsimd") == (inc == "WD"), (queue, inc)
        v = self.add(queue, fn, waits, inc, 16)
        return (inc, v)

    def emit(self, nc, sems):
        outer = self

        with nc.Block() as block:
            def make_body(eng_name):
                def body(eng):
                    last_wait = {}
                    for fn, waits, inc, inc_by in outer.ops[eng_name]:
                        for w in waits:
                            if w is None:
                                continue
                            sname, v = w
                            if v is None or v <= 0:
                                continue
                            if last_wait.get(sname, -1) >= v:
                                continue
                            eng.wait_ge(sems[sname], v)
                            last_wait[sname] = v
                        instr = fn(eng)
                        if inc is not None:
                            assert instr is not None, f"op on {eng_name} returned None"
                            instr.then_inc(sems[inc], inc_by)
                return body

            for name in self.ENGINES:
                if self.ops[name]:
                    getattr(block, name)(make_body(name))


SEM_NAMES = ("LD", "ST", "WD", "TE", "SC", "VE", "GP")


def _pretile_T(w, tile=128):
    fo, fi = w.shape
    wt = np.ascontiguousarray(np.asarray(w, np.float32).T)
    return np.ascontiguousarray(
        wt.reshape(fi, fo // tile, tile).transpose(1, 0, 2))


def _pack_scale(s, tile=128):
    fo = s.shape[0]
    return np.ascontiguousarray(np.asarray(s, np.float32).reshape(fo // tile, tile).T)


def _to_T(x):
    b, t, d = x.shape
    return np.ascontiguousarray(np.asarray(x).transpose(2, 0, 1).reshape(d, b * t))


def _from_T(xT, b, t, d):
    return np.ascontiguousarray(np.asarray(xT).reshape(d, b, t).transpose(1, 2, 0))


def emit_ln(s, bufs, res_src, t0, TH, xn_out, xn_col0, state):
    xb, x2b, srow = bufs["xb"], bufs["x2b"], bufs["srow"]
    pS1, pS2, pM, pR = bufs["pS1"], bufs["pS2"], bufs["pM"], bufs["pR"]
    ones_col_b = bufs["onesb"][:, 0:1]
    ones_row_f = bufs["onesf"][0:1, :]

    ld_x = state.pop("preload_ld", None)
    if ld_x is None:
        src = res_src[:, t0:t0 + TH].rearrange("(a p) t -> p a t", p=128)
        ld_x = s.dma("gpsimd", lambda e, sc=src: e.dma_start(xb[:, :, :], sc),
                     [state["xb_free"]])
    te_s1 = s.add("tensor", lambda e: [
        e.matmul(pS1[0:1, :], ones_col_b, xb[:, fi, :],
                 start=(fi == 0), stop=(fi == 7)) for fi in range(8)][-1],
        [ld_x, state["pS_free"]], inc="TE")
    te_s2 = None
    for rnd in range(4):
        sc_sq = s.add("scalar", lambda e, r=rnd: e.activation(
            x2b[:, :, :], xb[:, 2 * r:2 * (r + 1), :], AF.Square),
            [ld_x, ("TE", te_s1 if rnd == 0 else te_s2)], inc="SC")
        te_s2 = s.add("tensor", lambda e, r=rnd: [
            e.matmul(pS2[0:1, :], ones_col_b, x2b[:, i, :],
                     start=(r == 0 and i == 0), stop=(r == 3 and i == 1))
            for i in range(2)][-1],
            [("SC", sc_sq)], inc="TE")
    sc_rows = s.add("scalar", lambda e: [
        e.activation(srow[0:1, 0, :], pS1[0:1, :], AF.Identity, scale=1.0 / 1024.0),
        e.activation(srow[0:1, 1, :], pS2[0:1, :], AF.Identity, scale=1.0 / 1024.0),
    ][-1], [("TE", te_s2), state["xb_free"]], inc="SC")
    ve_v1 = s.add("vector", lambda e: e.tensor_mul(
        srow[0:1, 2, :], srow[0:1, 0, :], srow[0:1, 0, :]),
        [("SC", sc_rows)], inc="VE")
    ve_var = s.add("vector", lambda e: e.tensor_sub(
        srow[0:1, 2, :], srow[0:1, 1, :], srow[0:1, 2, :]),
        [("VE", ve_v1)], inc="VE")
    sc_ln = s.add("scalar", lambda e: e.activation(
        srow[0:1, 2, :], srow[0:1, 2, :], AF.Ln, bias=bufs["cvals"][0:1, 0:1]),
        [("VE", ve_var)], inc="SC")
    sc_rstd = s.add("scalar", lambda e: e.activation(
        srow[0:1, 2, :], srow[0:1, 2, :], AF.Exp, scale=-0.5),
        [("SC", sc_ln)], inc="SC")
    te_bc = s.add("tensor", lambda e: [
        e.matmul(pM[:, :], ones_row_f, srow[0:1, 0, :], start=True, stop=True),
        e.matmul(pR[:, :], ones_row_f, srow[0:1, 2, :], start=True, stop=True),
    ][-1], [("SC", sc_rstd), state["pMR_free"]], inc="TE")
    ve_apply = None
    xn_free = state.get("xn_free")
    for fi in range(8):
        ve_d = s.add("vector", lambda e, f=fi: e.tensor_sub(
            xn_out[:, f, xn_col0:xn_col0 + TH], xb[:, f, :], pM[:, :]),
            [("TE", te_bc), xn_free] if xn_free else [("TE", te_bc)], inc="VE")
        ve_apply = s.add("vector", lambda e, f=fi: e.tensor_mul(
            xn_out[:, f, xn_col0:xn_col0 + TH],
            xn_out[:, f, xn_col0:xn_col0 + TH], pR[:, :]),
            [("VE", ve_d)], inc="VE")
    state["xb_free"] = ("VE", ve_apply)
    state["pMR_free"] = ("VE", ve_apply)
    state["pS_free"] = ("SC", sc_rows)
    return ve_apply


def emit_ffn0(s, nc, es, hbm, pfx="f0_", cache_ready=None):
    sb = lambda name, shape, dt: es.enter_context(nc.sbuf_tensor(pfx + name, shape, dt))
    ps = lambda name, shape: es.enter_context(nc.psum_tensor(pfx + name, shape, F32))

    res_in = hbm["f0_res_in"]
    res_out = hbm["f0_res_out"]
    wts = {"w1": hbm["f0_w1b"], "w2": hbm["f0_w2b"], "h1": hbm["f0_h1b"],
           "h2": hbm["f0_h2b"], "w3": hbm["f0_w3b"]}

    TB = 1024
    NTB = TQ // TB
    TH = 512

    onesf = sb("onesf", [128, 128], F32)
    onesb = sb("onesb", [128, 128], BF16)
    scl = sb("scl", [128, 136], F32)
    xb = sb("xb", [128, 8, TH], BF16)
    x2b = sb("x2b", [128, 2, TH], BF16)
    xn = sb("xn", [128, 8, TB], BF16)
    h0 = sb("h0", [128, 32, TB], BF16)
    h1 = sb("h1", [128, 32, TB], BF16)
    wsl = sb("wsl", [128, 8192], BF16)
    xres = sb("xres", [128, 2 * TH], F32)
    tmp = sb("tmp", [128, 2, TH], F32)
    ot = sb("ot", [128, 2, TH], F32)
    srow = sb("srow", [1, 3, TH], F32)

    pA = ps("pA", [128, TH])
    pB = ps("pB", [128, TH])
    pS1 = ps("pS1", [1, TH])
    pS2 = ps("pS2", [1, TH])
    pM = ps("pM", [128, TH])
    pR = ps("pR", [128, TH])

    def wsl_q(q):
        return wsl[:, q * 2048:(q + 1) * 2048].rearrange("p (r c) -> p r c", c=128)

    def wsl_h(h):
        return wsl[:, h * 4096:(h + 1) * 4096].rearrange("p (r c) -> p r c", c=128)

    s.dma("sync", lambda e: e.dma_start(onesf[:, :], hbm["c_onesf"][:, :]))
    s.dma("sync", lambda e: e.dma_start(onesb[:, :], hbm["c_onesb"][:, :]))
    s.dma("sync", lambda e: e.dma_start(scl[:, :], hbm["f0_scl"][:, :]))
    ld_const = s.val("LD")

    cvals = sb("cvals", [128, 8], F32)
    s.dma("sync", lambda e: e.dma_start(cvals[:, :], hbm["c_cvals"][:, :]))
    ln_bufs = {"xb": xb, "x2b": x2b, "srow": srow, "pS1": pS1, "pS2": pS2,
               "pM": pM, "pR": pR, "onesb": onesb, "onesf": onesf, "cvals": cvals}
    ln_state = {"xb_free": ("LD", ld_const), "pMR_free": None, "pS_free": None,
                "xn_free": None}

    cache_wait = ("WD", cache_ready["f0"]) if cache_ready else None

    qtr_te = [None, None, None, None]

    def dma_slab(src, dst, quarters):
        waits = [cache_wait] if cache_wait else []
        for q in quarters:
            if qtr_te[q] is not None:
                waits.append(("TE", qtr_te[q]))
        return s.dma("gpsimd", lambda e, d=dst, sc=src: e.dma_start(d, sc), waits)

    bank_free = {"pA": None, "pB": None}
    ot_store = [0, 0]
    st_total = 0

    ve_ln_cur = None
    for th in range(2):
        ve_ln_cur = emit_ln(s, ln_bufs, res_in, th * TH, TH, xn, th * TH, ln_state)

    for tb in range(NTB):
        t0 = tb * TB

        pair_wd = [None] * 32
        n_pair = [0]

        def load_pair():
            k = n_pair[0]
            if k >= 32:
                return
            p_i = k // 2
            wname = "w1" if k % 2 == 0 else "w2"
            src = wts[wname][2 * p_i:2 * p_i + 2].rearrange(
                "s (a p) f -> p (s a) f", p=128)
            pair_wd[k] = dma_slab(src, wsl_q(k % 4), (k % 4,))
            n_pair[0] += 1

        for _ in range(4):
            load_pair()

        ve_ln = ve_ln_cur
        for fo in range(32):
            off = (fo % 2) * 8
            k1, k2 = 2 * (fo // 2), 2 * (fo // 2) + 1
            q1, q2 = k1 % 4, k2 % 4
            for th in range(2):
                g1 = s.add("tensor", lambda e, q=q1, o=off, t=th: [
                    e.matmul(pA[:, :], wsl_q(q)[:, o + fi, :],
                             xn[:, fi, t * TH:(t + 1) * TH],
                             start=(fi == 0), stop=(fi == 7)) for fi in range(8)][-1],
                    [pair_wd[k1], ("VE", ve_ln), bank_free["pA"]], inc="TE")
                g2 = s.add("tensor", lambda e, q=q2, o=off, t=th: [
                    e.matmul(pB[:, :], wsl_q(q)[:, o + fi, :],
                             xn[:, fi, t * TH:(t + 1) * TH],
                             start=(fi == 0), stop=(fi == 7)) for fi in range(8)][-1],
                    [pair_wd[k2], bank_free["pB"]], inc="TE")
                sc_e = s.add("scalar", lambda e, f=fo, t=th: e.activation(
                    tmp[:, t, :], pA[:, :], AF.Silu, scale=scl[:, f:f + 1]),
                    [("TE", g1), ("VE", s.val("VE"))], inc="SC")
                bank_free["pA"] = ("SC", sc_e)
                ve_h0 = s.add("vector", lambda e, f=fo, t=th: e.scalar_tensor_tensor(
                    h0[:, f, t * TH:(t + 1) * TH], pB[:, :], scl[:, 32 + f:33 + f],
                    tmp[:, t, :], ALU.mult, ALU.mult),
                    [("TE", g2), ("SC", sc_e)], inc="VE")
                bank_free["pB"] = ("VE", ve_h0)
            if fo % 2 == 1:
                qtr_te[q1] = s.val("TE")
                qtr_te[q2] = s.val("TE")
                load_pair()
                load_pair()
        te_12 = s.val("TE")

        slab_wd = {}

        def load_big(which, fo2, seq):
            src = wts[which][fo2].rearrange("(a p) f -> p a f", p=128)
            h = seq % 2
            slab_wd[(which, fo2)] = dma_slab(src, wsl_h(h), (2 * h, 2 * h + 1))

        load_big("h1", 0, 0)
        load_big("h1", 1, 1)
        big_seq = [2]

        def load_next_big():
            sq = big_seq[0]
            if sq < 32:
                load_big("h1", sq, sq)
            elif sq < 64:
                load_big("h2", sq - 32, sq)
            elif sq < 72:
                load_big("w3", sq - 64, sq)
            else:
                return
            big_seq[0] += 1

        ln_next = [None]

        def preload_ln(th2):
            if tb + 1 >= NTB:
                return
            src = res_in[:, (tb + 1) * TB + th2 * TH:
                         (tb + 1) * TB + (th2 + 1) * TH].rearrange(
                "(a p) t -> p a t", p=128)
            ln_state["preload_ld"] = s.dma(
                "gpsimd", lambda e, sc=src: e.dma_start(xb[:, :, :], sc),
                [ln_state["xb_free"]])

        def emit_ln_next(th2):
            if tb + 1 >= NTB:
                return
            ln_state["xn_free"] = ("TE", te_12)
            ln_next[0] = emit_ln(s, ln_bufs, res_in, (tb + 1) * TB + th2 * TH,
                                 TH, xn, th2 * TH, ln_state)
            ln_state["xn_free"] = None

        rhs_ready = ("VE", s.val("VE"))
        for stage, (rhs_buf, out_buf, scol, which) in enumerate((
                (h0, h1, 64, "h1"), (h1, h0, 96, "h2"))):
            for fo in range(32):
                h = (stage * 32 + fo) % 2
                for th in range(2):
                    pname = "pA" if (fo * 2 + th) % 2 == 0 else "pB"
                    pp = pA if pname == "pA" else pB
                    g = s.add("tensor", lambda e, hh=h, t=th, rb=rhs_buf, p=pp: [
                        e.matmul(p[:, :], wsl_h(hh)[:, fi, :],
                                 rb[:, fi, t * TH:(t + 1) * TH],
                                 start=(fi == 0), stop=(fi == 31)) for fi in range(32)][-1],
                        [slab_wd[(which, fo)], bank_free[pname], rhs_ready], inc="TE")
                    sc_e = s.add("scalar", lambda e, f=fo, t=th, ob=out_buf, sc0=scol, p=pp:
                                 e.activation(ob[:, f, t * TH:(t + 1) * TH], p[:, :],
                                              AF.Gelu, scale=scl[:, sc0 + f:sc0 + f + 1]),
                                 [("TE", g)], inc="SC")
                    bank_free[pname] = ("SC", sc_e)
                qtr_te[2 * h] = s.val("TE")
                qtr_te[2 * h + 1] = s.val("TE")
                load_next_big()
                if fo == 2:
                    preload_ln(stage)
                if fo == 10:
                    emit_ln_next(stage)
            rhs_ready = ("SC", s.val("SC"))

        for fo in range(8):
            h = fo % 2
            src = res_in[fo * 128:(fo + 1) * 128, t0:t0 + TB]
            ld_r = s.dma("sync", lambda e, sc=src: e.dma_start(xres[:, :], sc),
                         [("VE", s.val("VE"))])
            for th in range(2):
                pname = "pA" if (fo * 2 + th) % 2 == 0 else "pB"
                pp = pA if pname == "pA" else pB
                g = s.add("tensor", lambda e, hh=h, t=th, p=pp: [
                    e.matmul(p[:, :], wsl_h(hh)[:, fi, :],
                             h0[:, fi, t * TH:(t + 1) * TH],
                             start=(fi == 0), stop=(fi == 31)) for fi in range(32)][-1],
                    [slab_wd[("w3", fo)], bank_free[pname], rhs_ready], inc="TE")
                r = (fo * 2 + th) % 2
                ve_o = s.add("vector", lambda e, f=fo, t=th, rr=r, p=pp:
                             e.scalar_tensor_tensor(
                                 ot[:, rr, :], p[:, :], scl[:, 128 + f:129 + f],
                                 xres[:, t * TH:(t + 1) * TH], ALU.mult, ALU.add),
                             [("TE", g), ld_r, ("ST", ot_store[r])], inc="VE")
                bank_free[pname] = ("VE", ve_o)
                dst = res_out[fo * 128:(fo + 1) * 128, t0 + th * TH:t0 + (th + 1) * TH]
                st_total = s.dma("sync", lambda e, d=dst, rr=r: e.dma_start(d, ot[:, rr, :]),
                                 [("VE", ve_o)], inc="ST")[1]
                ot_store[r] = st_total
            qtr_te[2 * h] = s.val("TE")
            qtr_te[2 * h + 1] = s.val("TE")
            load_next_big()
        ve_ln_cur = ln_next[0]

    s.add("sync", lambda e: e.nop(), [("ST", st_total)])
    return st_total


def _ffn0_weight_dram(nc, hbm):
    hbm["f0_w1t"] = nc.dram_tensor("w1t", [32, 1024, 128], F32, kind="ExternalInput").ap()
    hbm["f0_w2t"] = nc.dram_tensor("w2t", [32, 1024, 128], F32, kind="ExternalInput").ap()
    hbm["f0_h1t"] = nc.dram_tensor("h1t", [32, 4096, 128], F32, kind="ExternalInput").ap()
    hbm["f0_h2t"] = nc.dram_tensor("h2t", [32, 4096, 128], F32, kind="ExternalInput").ap()
    hbm["f0_w3t"] = nc.dram_tensor("w3t", [8, 4096, 128], F32, kind="ExternalInput").ap()
    hbm["f0_scl"] = nc.dram_tensor("scl", [128, 136], F32, kind="ExternalInput").ap()
    hbm["f0_w1b"] = nc.dram_tensor("w1b", [32, 1024, 128], BF16, kind="Internal").ap()
    hbm["f0_w2b"] = nc.dram_tensor("w2b", [32, 1024, 128], BF16, kind="Internal").ap()
    hbm["f0_h1b"] = nc.dram_tensor("h1b", [32, 4096, 128], BF16, kind="Internal").ap()
    hbm["f0_h2b"] = nc.dram_tensor("h2b", [32, 4096, 128], BF16, kind="Internal").ap()
    hbm["f0_w3b"] = nc.dram_tensor("w3b", [8, 4096, 128], BF16, kind="Internal").ap()
    return hbm


def _mk_ffn0_dram(nc):
    hbm = {}
    hbm["f0_res_in"] = nc.dram_tensor("res_in", [D, TQ], F32, kind="ExternalInput").ap()
    hbm["f0_res_out"] = nc.dram_tensor("res_out", [D, TQ], F32, kind="ExternalOutput").ap()
    _ffn0_weight_dram(nc, hbm)
    _const_dram(nc, hbm)
    return hbm


def build_ffn0_nc():
    nc = bass.Bass("TRN2", target_bir_lowering=False, debug=False)
    hbm = _mk_ffn0_dram(nc)
    s = Sched()
    with ExitStack() as es:
        sems = {n: es.enter_context(nc.semaphore(n)) for n in SEM_NAMES}
        emit_ffn0(s, nc, es, hbm)
        s.emit(nc, sems)
    return nc


def _cvals():
    c = np.zeros((128, 8), np.float32)
    c[:, 0] = 1e-5
    c[:, 1] = 1e-6
    c[:, 2] = 1.0
    return c


def ffn0_host_inputs(inputs):
    scl = np.zeros((128, 136), np.float32)
    scl[:, 0:32] = _pack_scale(inputs["L0_ffn_w1_s"])
    scl[:, 32:64] = _pack_scale(inputs["L0_ffn_w2_s"])
    scl[:, 64:96] = _pack_scale(inputs["L0_ffn_h1_s"])
    scl[:, 96:128] = _pack_scale(inputs["L0_ffn_h2_s"])
    scl[:, 128:136] = _pack_scale(inputs["L0_ffn_w3_s"])
    return {
        "w1t": _pretile_T(inputs["L0_ffn_w1_w"]),
        "w2t": _pretile_T(inputs["L0_ffn_w2_w"]),
        "h1t": _pretile_T(inputs["L0_ffn_h1_w"]),
        "h2t": _pretile_T(inputs["L0_ffn_h2_w"]),
        "w3t": _pretile_T(inputs["L0_ffn_w3_w"]),
        "scl": scl,
        "onesf": np.ones((128, 128), np.float32),
        "onesb": np.ones((128, 128), bf16),
        "cvals": _cvals(),
    }


def run_ffn0(res_T_shards, common, core_ids=None):
    if core_ids is None:
        core_ids = list(range(len(res_T_shards)))
    nc = build_ffn0_nc()
    in_maps = [dict(common, res_in=r) for r in res_T_shards]
    res = run_bass_kernel_spmd(nc, in_maps, core_ids=core_ids)
    return [r["res_out"] for r in res.results]



NH = 16
HD = 64
TK = 1024


def _rhs_tile(w):
    fo, fi = w.shape
    return np.ascontiguousarray(
        np.asarray(w, np.float32).T.reshape(fi // 128, 128, fo))


def emit_attn(s, nc, es, hbm, pfx="at_", cache_ready=None):
    sb = lambda name, shape, dt: es.enter_context(nc.sbuf_tensor(pfx + name, shape, dt))
    ps = lambda name, shape: es.enter_context(nc.psum_tensor(pfx + name, shape, F32))

    res_in = hbm["at_res_in"]
    res_out = hbm["at_res_out"]

    onesf = sb("onesf", [128, 128], F32)
    onesb = sb("onesb", [128, 128], BF16)
    onesrow = sb("onesrow", [1, 512], BF16)
    cvals = sb("cvals", [128, 8], F32)
    cpk = sb("cpk", [128, 48], F32)
    vob = sb("vob", [1, 2048], BF16)
    xb = sb("xb", [128, 8, 512], BF16)
    x2b = sb("x2b", [128, 2, 512], BF16)
    srow = sb("srow", [1, 3, 512], F32)
    xn = sb("xn", [128, 8, T], BF16)
    xd = sb("xd", [128, 8, TK], BF16)
    QT = sb("QT", [128, 8, T], BF16)
    KT = sb("KT", [128, 8, TK], BF16)
    V65 = sb("V65", [128, 8, NH * 65], BF16)
    expS = sb("expS", [128, 8, T], BF16)
    wq = sb("wq", [128, 8, 1024], BF16)
    orw = sb("orw", [128, 2, 512], F32)
    rcp = sb("rcp", [1, 2, 512], F32)
    xres = sb("xres", [128, T], F32)
    ot = sb("ot", [128, 2, 512], F32)

    pA = ps("pA", [128, 512])
    pB = ps("pB", [128, 512])
    pS1 = ps("pS1", [1, 512])
    pS2 = ps("pS2", [1, 512])
    pM = ps("pM", [128, 512])
    pR = ps("pR", [128, 512])
    pO = ps("pO", [65, 512])
    pD = ps("pD", [64, 512])

    for dst, key in ((onesf, "c_onesf"), (onesb, "c_onesb"), (onesrow, "c_onesrow"),
                     (cvals, "c_cvals"), (cpk, "at_cpk"), (vob, "at_vob")):
        s.dma("sync", lambda e, d=dst, k=key: e.dma_start(d[:, :], hbm[k][:, :]))
    ld_const = s.val("LD")

    ln_bufs = {"xb": xb, "x2b": x2b, "srow": srow, "pS1": pS1, "pS2": pS2,
               "pM": pM, "pR": pR, "onesb": onesb, "onesf": onesf, "cvals": cvals}
    ln_state = {"xb_free": ("LD", ld_const), "pMR_free": None, "pS_free": None}

    bank_free = {"pA": None, "pB": None, "pO": None, "pD": None}
    wq_free = None
    ot_store = [0, 0]
    st_total = 0

    def proj(n_blk, rhs_buf, out_buf, bias_col, rhs_ready, wd_val, n_fi=8):
        nonlocal st_total
        for fo in range(8):
            for blk in range(n_blk):
                pname = "pA" if (fo * n_blk + blk) % 2 == 0 else "pB"
                pp = pA if pname == "pA" else pB
                g = s.add("tensor", lambda e, f=fo, bl=blk, p=pp, rb=rhs_buf: [
                    e.matmul(p[:, :], wq[:, fi, f * 128:(f + 1) * 128],
                             rb[:, fi, bl * 512:(bl + 1) * 512],
                             start=(fi == 0), stop=(fi == n_fi - 1))
                    for fi in range(n_fi)][-1],
                    [wd_val, bank_free[pname], rhs_ready], inc="TE")
                sc_e = s.add("scalar", lambda e, f=fo, bl=blk, p=pp, ob=out_buf, bc=bias_col:
                             e.activation(ob[:, f, bl * 512:(bl + 1) * 512], p[:, :],
                                          AF.Identity, bias=cpk[:, bc + f:bc + f + 1]),
                             [("TE", g)], inc="SC")
                bank_free[pname] = ("SC", sc_e)
        return s.val("TE"), ("SC", s.val("SC"))

    cache_at = ("WD", cache_ready["at"]) if cache_ready else None

    for b in range(BPC):
        tb0 = b * T
        ve_ln = None
        for j in range(4):
            ve_ln = emit_ln(s, ln_bufs, res_in, tb0 + j * 512, 512, xn, j * 512, ln_state)
        bank_free["pM"] = ln_state["pMR_free"]
        bank_free["pR"] = ln_state["pMR_free"]

        ld_q = s.dma("gpsimd", lambda e: e.dma_start(
            wq[:, :, :], hbm["at_qwb"].rearrange("a p f -> p a f")),
            [wq_free, cache_at])
        te_q, rdy_q = proj(4, xn, QT, 0, ("VE", ve_ln), ld_q)

        conv_done = None
        for fi in range(8):
            c1 = s.add("vector", lambda e, f=fi: e.tensor_scalar(
                xd[:, f, :], xn[:, f, 0:T:2], cpk[:, 24 + f:25 + f],
                cpk[:, 40 + f:41 + f], ALU.mult, ALU.add),
                [("VE", ve_ln)], inc="VE")
            c2 = s.add("vector", lambda e, f=fi: e.scalar_tensor_tensor(
                xd[:, f, :], xn[:, f, 1:T:2], cpk[:, 32 + f:33 + f], xd[:, f, :],
                ALU.mult, ALU.add),
                [("VE", c1)], inc="VE")
            conv_done = s.add("vector", lambda e, f=fi: e.scalar_tensor_tensor(
                xd[:, f, 1:TK], xn[:, f, 1:T - 2:2], cpk[:, 16 + f:17 + f],
                xd[:, f, 1:TK], ALU.mult, ALU.add),
                [("VE", c2)], inc="VE")

        ld_k = s.dma("gpsimd", lambda e: e.dma_start(
            wq[:, :, :], hbm["at_kwb"].rearrange("a p f -> p a f")),
            [("TE", te_q)])
        te_k, rdy_k = proj(2, xd, KT, 8, ("VE", conv_done), ld_k)

        ld_v = s.dma("gpsimd", lambda e: e.dma_start(
            wq[:, :, :], hbm["at_vwb"].rearrange("a p f -> p a f")),
            [("TE", te_k)])
        ve_ms = s.add("vector", lambda e: e.memset(V65[:, :, :], 1.0),
                      [("TE", s.val("TE"))], inc="VE")
        v_evac = None
        for tkt in range(8):
            for fb in range(2):
                pname = "pA" if (tkt * 2 + fb) % 2 == 0 else "pB"
                pp = pA if pname == "pA" else pB
                g = s.add("tensor", lambda e, tk=tkt, f=fb, p=pp: ([
                    e.matmul(p[:, :], xd[:, fi, tk * 128:(tk + 1) * 128],
                             wq[:, fi, f * 512:(f + 1) * 512],
                             start=(fi == 0), stop=False) for fi in range(8)] + [
                    e.matmul(p[:, :], onesb[0:1, :],
                             vob[0:1, f * 512:(f + 1) * 512], start=False, stop=True)])[-1],
                    [ld_v, bank_free[pname], ("VE", conv_done)], inc="TE")
                v_evac = s.add("vector", lambda e, tk=tkt, f=fb, p=pp: e.tensor_copy(
                    V65[:, tk, :].rearrange("p (h c) -> p h c", c=65)[:, f * 8:(f + 1) * 8, 0:64],
                    p[:, :].rearrange("p (h c) -> p h c", c=64)),
                    [("TE", g), ("VE", ve_ms)], inc="VE")
                bank_free[pname] = ("VE", v_evac)

        vem_v = [None, None]
        pend_r = [None, None]
        for h in range(NH):
            hp = (h % 2) * 64
            ha = h // 2
            exp_blk = [None] * 4
            for tkt in range(8):
                for blk in range(4):
                    pname = "pA" if (tkt * 4 + blk) % 2 == 0 else "pB"
                    pp = pA if pname == "pA" else pB
                    g = s.add("tensor", lambda e, tk=tkt, bl=blk, p=pp, q0=hp, a0=ha:
                              e.matmul(p[:, :], KT[q0:q0 + 64, a0, tk * 128:(tk + 1) * 128],
                                       QT[q0:q0 + 64, a0, bl * 512:(bl + 1) * 512],
                                       start=True, stop=True),
                              [bank_free[pname], rdy_q, rdy_k, ("VE", v_evac)], inc="TE")
                    sc_exp = s.add("scalar", lambda e, tk=tkt, bl=blk, p=pp: e.activation(
                        expS[:, tk, bl * 512:(bl + 1) * 512], p[:, :], AF.Exp, scale=0.125),
                        [("TE", g)], inc="SC")
                    bank_free[pname] = ("SC", sc_exp)
                    exp_blk[blk] = sc_exp
            ver_v = [None] * 4

            def emit_td(bl):
                r2 = bl % 2
                key_d = "pD" if r2 == 0 else "pR"
                ppd = pD if r2 == 0 else pR
                td = s.add("tensor", lambda e, rr=r2, p=ppd: e.matmul(
                    p[0:64, :], onesf[0:1, 0:64], rcp[0:1, rr, :],
                    start=True, stop=True),
                    [("VE", ver_v[bl]), bank_free[key_d]], inc="TE")
                vm = s.add("vector", lambda e, bl2=bl, rr=r2, q0=hp, a0=ha, p=ppd:
                           e.tensor_mul(
                               xn[q0:q0 + 64, a0, bl2 * 512:(bl2 + 1) * 512],
                               orw[0:64, rr, :], p[0:64, :]),
                           [("TE", td)], inc="VE")
                bank_free[key_d] = ("VE", vm)
                vem_v[r2] = vm
                pend_r[r2] = td

            for blk in range(4):
                r = blk % 2
                key_o = "pO" if r == 0 else "pM"
                ppo = pO if r == 0 else pM
                g = s.add("tensor", lambda e, bl=blk, hh=h, p=ppo: [
                    e.matmul(p[0:65, :], V65[:, tk, hh * 65:(hh + 1) * 65],
                             expS[:, tk, bl * 512:(bl + 1) * 512],
                             start=(tk == 0), stop=(tk == 7)) for tk in range(8)][-1],
                    [bank_free[key_o], ("SC", exp_blk[blk])], inc="TE")
                sc_o = s.add("scalar", lambda e, rr=r, p=ppo: e.activation(
                    orw[0:65, rr, :], p[0:65, :], AF.Identity),
                    [("TE", g)] + ([("VE", vem_v[r])] if vem_v[r] else []), inc="SC")
                bank_free[key_o] = ("SC", sc_o)
                ver_v[blk] = s.add("vector", lambda e, rr=r: e.reciprocal(
                    rcp[0:1, rr, :], orw[64:65, rr, :]),
                    [("SC", sc_o)] + ([("TE", pend_r[r])] if pend_r[r] else []),
                    inc="VE")
                if blk >= 1:
                    emit_td(blk - 1)
            emit_td(3)
        ln_state["pMR_free"] = ("VE", s.val("VE"))

        ld_o = s.dma("gpsimd", lambda e: e.dma_start(
            wq[:, :, :], hbm["at_owb"].rearrange("a p f -> p a f")),
            [("TE", s.val("TE"))])
        ve_last = s.val("VE")
        for fo in range(8):
            src = res_in[fo * 128:(fo + 1) * 128, tb0:tb0 + T]
            ld_r = s.dma("sync", lambda e, sc=src: e.dma_start(xres[:, :], sc),
                         [("VE", s.val("VE"))])
            for blk in range(4):
                pname = "pA" if (fo * 4 + blk) % 2 == 0 else "pB"
                pp = pA if pname == "pA" else pB
                g = s.add("tensor", lambda e, f=fo, bl=blk, p=pp: ([
                    e.matmul(p[:, :], wq[:, fi, f * 128:(f + 1) * 128],
                             xn[:, fi, bl * 512:(bl + 1) * 512],
                             start=(fi == 0), stop=False) for fi in range(8)] + [
                    e.matmul(p[:, :], vob[0:1, 1024 + f * 128:1024 + (f + 1) * 128],
                             onesrow[0:1, :], start=False, stop=True)])[-1],
                    [ld_o, bank_free[pname], ("VE", ve_last), ld_r],
                    inc="TE")
                r = (fo * 4 + blk) % 2
                ve_o = s.add("vector", lambda e, bl=blk, rr=r, p=pp: e.tensor_add(
                    ot[:, rr, :], p[:, :], xres[:, bl * 512:(bl + 1) * 512]),
                    [("TE", g), ("ST", ot_store[r])], inc="VE")
                bank_free[pname] = ("VE", ve_o)
                dst = res_out[fo * 128:(fo + 1) * 128, tb0 + blk * 512:tb0 + (blk + 1) * 512]
                st_total = s.dma("sync", lambda e, d=dst, rr=r: e.dma_start(d, ot[:, rr, :]),
                                 [("VE", ve_o)], inc="ST")[1]
                ot_store[r] = st_total
        wq_free = ("TE", s.val("TE"))

    s.add("sync", lambda e: e.nop(), [("ST", st_total)])
    return st_total


def _attn_weight_dram(nc, hbm):
    for w in ("qw", "kw", "vw", "ow"):
        hbm["at_" + w] = nc.dram_tensor(w, [8, 128, 1024], F32, kind="ExternalInput").ap()
        hbm["at_" + w + "b"] = nc.dram_tensor(w + "b", [8, 128, 1024], BF16,
                                              kind="Internal").ap()
    hbm["at_cpk"] = nc.dram_tensor("cpk", [128, 48], F32, kind="ExternalInput").ap()
    hbm["at_vob"] = nc.dram_tensor("vob", [1, 2048], BF16, kind="ExternalInput").ap()
    return hbm


def _mk_attn_dram(nc):
    hbm = {}
    hbm["at_res_in"] = nc.dram_tensor("res_in", [D, TQ], F32, kind="ExternalInput").ap()
    hbm["at_res_out"] = nc.dram_tensor("res_out", [D, TQ], F32, kind="ExternalOutput").ap()
    _attn_weight_dram(nc, hbm)
    _const_dram(nc, hbm)
    return hbm


def build_attn_nc():
    nc = bass.Bass("TRN2", target_bir_lowering=False, debug=False)
    hbm = _mk_attn_dram(nc)
    s = Sched()
    with ExitStack() as es:
        sems = {n: es.enter_context(nc.semaphore(n)) for n in SEM_NAMES}
        emit_attn(s, nc, es, hbm)
        s.emit(nc, sems)
    return nc


def attn_host_inputs(inputs):
    cpk = np.zeros((128, 48), np.float32)
    cpk[:, 0:8] = _pack_scale(inputs["L1_attn_q_b"])
    cpk[:, 8:16] = _pack_scale(inputs["L1_attn_k_b"])
    cw = np.asarray(inputs["L1_attn_conv_w"], np.float32)
    cpk[:, 16:24] = _pack_scale(cw[:, 0, 0])
    cpk[:, 24:32] = _pack_scale(cw[:, 0, 1])
    cpk[:, 32:40] = _pack_scale(cw[:, 0, 2])
    cpk[:, 40:48] = _pack_scale(inputs["L1_attn_conv_b"])
    vob = np.zeros((1, 2048), np.float32)
    vob[0, 0:1024] = inputs["L1_attn_v_b"]
    vob[0, 1024:2048] = inputs["L1_attn_out_b"]
    return {
        "qw": _rhs_tile(inputs["L1_attn_q_w"]),
        "kw": _rhs_tile(inputs["L1_attn_k_w"]),
        "vw": _rhs_tile(inputs["L1_attn_v_w"]),
        "ow": _rhs_tile(inputs["L1_attn_out_w"]),
        "cpk": cpk,
        "vob": vob.astype(bf16),
        "onesf": np.ones((128, 128), np.float32),
        "onesb": np.ones((128, 128), bf16),
        "onesrow": np.ones((1, 512), bf16),
        "cvals": _cvals(),
    }


def run_attn(res_T_shards, common, core_ids=None):
    if core_ids is None:
        core_ids = list(range(len(res_T_shards)))
    nc = build_attn_nc()
    in_maps = [dict(common, res_in=r) for r in res_T_shards]
    res = run_bass_kernel_spmd(nc, in_maps, core_ids=core_ids)
    return [r["res_out"] for r in res.results]



FI1 = 2816


def emit_quant(s, src, dst, W, tm, src_ready, sgn_src=None):
    nb = W // 16
    xa, v, a1, a2 = tm["xa"], tm["v"], tm["a1"], tm["a2"]
    sgn, scq, sc2, rcp = tm["sgn"], tm["scq"], tm["sc2"], tm["rcp"]
    pve, pgp, psc = tm.get("fVE", 0), tm.get("fGP", 0), tm.get("fSC", 0)

    w1 = list(src_ready) + [("VE", pve), ("GP", pgp), ("SC", psc)]
    r1 = s.add("vector", lambda e: e.reduce_max(
        scq[:, 0:nb], src.rearrange("p (a b) -> p a b", b=16),
        axis=AX.X, apply_absolute_value=True), w1, inc="VE")
    r2 = s.add("vector", lambda e: e.tensor_scalar(
        sc2[:, 0:nb], scq[:, 0:nb], 1.0 / 6.0, 1e-12, ALU.mult, ALU.max),
        [("VE", r1)], inc="VE")
    r3 = s.add("vector", lambda e: e.reciprocal(rcp[:, 0:nb], sc2[:, 0:nb]),
               [("VE", r2)], inc="VE")
    c4 = s.add("scalar", lambda e: e.activation(xa[:, 0:W], src, AF.Abs),
               w1, inc="SC")
    c5 = s.add("scalar", lambda e: e.activation(sgn[:, 0:W], src, AF.Sign),
               [("SC", c4)], inc="SC")
    rb = rcp[:, 0:nb].unsqueeze(2).broadcast_to((128, nb, 16))
    c6 = s.add("gpsimd", lambda e: e.tensor_tensor(
        v[:, 0:W].rearrange("p (a b) -> p a b", b=16),
        xa[:, 0:W].rearrange("p (a b) -> p a b", b=16), rb, ALU.mult),
        [("SC", c4), ("VE", r3), ("GP", pgp)], inc="GP")
    g = s.add("vector", lambda e: e.tensor_single_scalar(
        a2[:, 0:W], v[:, 0:W], 2.5, ALU.is_gt), [("GP", c6)], inc="VE")
    for thr in (3.5, 5.0, 5.0):
        g = s.add("vector", lambda e, t=thr: e.scalar_tensor_tensor(
            a2[:, 0:W], v[:, 0:W], t, a2[:, 0:W], ALU.is_gt, ALU.add),
            [("VE", g)], inc="VE")
    q = s.add("vector", lambda e: e.tensor_scalar(
        a1[:, 0:W], v[:, 0:W], 2.0, MAGIC, ALU.mult, ALU.add),
        [("GP", c6)], inc="VE")
    q = s.add("vector", lambda e: e.tensor_scalar(
        a1[:, 0:W], a1[:, 0:W], MAGIC, 4.0, ALU.subtract, ALU.min),
        [("VE", q)], inc="VE")
    q2 = s.add("vector", lambda e: e.scalar_tensor_tensor(
        xa[:, 0:W], a1[:, 0:W], 0.5, a2[:, 0:W], ALU.mult, ALU.add),
        [("VE", q), ("VE", g)], inc="VE")
    q3 = s.add("vector", lambda e: e.tensor_tensor(
        a1[:, 0:W], xa[:, 0:W], sgn[:, 0:W], ALU.mult),
        [("VE", q2), ("SC", c5)], inc="VE")
    sb2 = sc2[:, 0:nb].unsqueeze(2).broadcast_to((128, nb, 16))
    q4 = s.add("vector", lambda e: e.tensor_tensor(
        dst.rearrange("p (a b) -> p a b", b=16),
        a1[:, 0:W].rearrange("p (a b) -> p a b", b=16), sb2, ALU.mult),
        [("VE", q3)], inc="VE")
    tm["fVE"], tm["fGP"], tm["fSC"] = q4, c6, c5
    return q4


def emit_f1(s, nc, es, hbm, pfx="f1_"):
    sb = lambda name, shape, dt: es.enter_context(nc.sbuf_tensor(pfx + name, shape, dt))
    ps = lambda name, shape, dt=F32: es.enter_context(nc.psum_tensor(pfx + name, shape, dt))

    res_in = hbm["f1_res_in"]
    res_out = hbm["f1_res_out"]
    wgq = nc.dram_tensor("f1wgq", [8, 128, FI1], BF16, kind="Internal").ap()
    wuq = nc.dram_tensor("f1wuq", [8, 128, FI1], BF16, kind="Internal").ap()
    wdq = nc.dram_tensor("f1wdq", [8, FI1, 128], BF16, kind="Internal").ap()

    onesf = sb("onesf", [128, 128], F32)
    onesb = sb("onesb", [128, 128], BF16)
    idb = sb("idb", [128, 128], BF16)
    idf = sb("idf", [128, 128], F32)
    cvals = sb("cvals", [128, 8], F32)
    xb = sb("xb", [128, 8, 512], BF16)
    x2b = sb("x2b", [128, 2, 512], BF16)
    srow = sb("srow", [1, 3, 512], F32)
    xn = sb("xn", [128, 8, T], BF16)
    xnat = sb("xnat", [128, 8, 128], BF16)
    xqn = sb("xqn", [128, 8, 128], BF16)
    xqT = sb("xqT", [128, 8, 512], BF16)
    htile = sb("htile", [128, 2, 512], F32)
    stmp = sb("stmp", [128, 2, 512], F32)
    h_nat = sb("h_nat", [128, 4, FI1], BF16)
    hqn = sb("hqn", [128, 2, FI1], BF16)
    hqT = sb("hqT", [128, 22, 512], BF16)
    wgu = sb("wgu", [128, 2, 8, 128], BF16)
    wdsl = sb("wdsl", [128, 2, 22, 128], BF16)
    wload = sb("wload", [128, FI1], F32)
    wqnat = sb("wqnat", [128, FI1], BF16)
    wstage = sb("wstage", [128, 22, 128], BF16)
    xres = sb("xres", [128, 2, 512], F32)
    ot = sb("ot", [128, 2, 512], F32)
    q_xa = sb("q_xa", [128, 1408], F32)
    q_v = sb("q_v", [128, 1408], F32)
    q_a1 = sb("q_a1", [128, 1408], F32)
    q_a2 = sb("q_a2", [128, 1408], F32)
    q_sgn = sb("q_sgn", [128, 1408], BF16)
    q_scq = sb("q_scq", [128, 88], F32)
    q_sc2 = sb("q_sc2", [128, 88], F32)
    q_rcp = sb("q_rcp", [128, 88], F32)
    tm = {"xa": q_xa, "v": q_v, "a1": q_a1, "a2": q_a2, "sgn": q_sgn,
          "scq": q_scq, "sc2": q_sc2, "rcp": q_rcp}

    pA = ps("pA", [128, 512])
    pB = ps("pB", [128, 512])
    pS1 = ps("pS1", [1, 512])
    pS2 = ps("pS2", [1, 512])
    pM = ps("pM", [128, 512])
    pR = ps("pR", [128, 512])
    pTb = ps("pTb", [128, 128], BF16)
    pTf = ps("pTf", [128, 128], F32)

    for dst, key in ((onesf, "c_onesf"), (onesb, "c_onesb"), (idb, "c_idb"),
                     (idf, "c_idf"), (cvals, "c_cvals")):
        s.dma("sync", lambda e, d=dst, k=key: e.dma_start(d[:, :], hbm[k][:, :]))
    ld_const = s.val("LD")

    ln_bufs = {"xb": xb, "x2b": x2b, "srow": srow, "pS1": pS1, "pS2": pS2,
               "pM": pM, "pR": pR, "onesb": onesb, "onesf": onesf, "cvals": cvals}
    ln_state = {"xb_free": ("LD", ld_const), "pMR_free": None, "pS_free": None}

    bank_free = {"pA": None, "pB": None, "pTb": None, "pTf": None}
    st_scratch = 0

    wl_free = None
    for wsrc, wdst, ntile, Wq in ((hbm["f1_wg"], wgq, 22, 1024),
                                  (hbm["f1_wu"], wuq, 22, 1024),
                                  (hbm["f1_wd"], wdq, 8, FI1)):
        for a in range(ntile):
            src = wsrc[a * 128:(a + 1) * 128, :]
            ld = s.dma("sync", lambda e, sc=src, w=Wq: e.dma_start(wload[:, 0:w], sc),
                       [wl_free] if wl_free else [])
            for off in range(0, Wq, 1408):
                wqw = min(1408, Wq - off)
                qd = emit_quant(s, wload[:, off:off + wqw], wqnat[:, off:off + wqw],
                                wqw, tm, [ld])
            wl_free = ("VE", qd)
            nfb = Wq // 128
            ev = None
            for fb in range(nfb):
                tp = s.add("tensor", lambda e, f=fb: e.transpose(
                    pTb[:, :], wqnat[:, f * 128:(f + 1) * 128], idb[:, :]),
                    [("VE", qd), bank_free["pTb"]], inc="TE")
                ev = s.add("vector", lambda e, f=fb: e.tensor_copy(
                    wstage[:, f, :], pTb[:, :]),
                    [("TE", tp), ("ST", st_scratch)], inc="VE")
                bank_free["pTb"] = ("VE", ev)
            if Wq == 1024:
                dst = wdst[:, :, a * 128:(a + 1) * 128].rearrange("b p f -> p b f")
                st_scratch = s.dma("sync", lambda e, d=dst: e.dma_start(
                    d, wstage[:, 0:8, :]), [("VE", ev)], inc="ST")[1]
            else:
                dst = wdst[a].rearrange("(b p) f -> p b f", p=128)
                st_scratch = s.dma("sync", lambda e, d=dst: e.dma_start(
                    d, wstage[:, 0:22, :]), [("VE", ev)], inc="ST")[1]
    wq_ready = ("ST", st_scratch)

    slab_meta = []
    for i in range(16):
        for fo in range(22):
            slab_meta.append(("g", fo))
            slab_meta.append(("u", fo))
        for fo in range(8):
            slab_meta.append(("d", fo))
    slab_done = [None] * len(slab_meta)
    slab_wd = [None] * len(slab_meta)
    n_loaded = [0]

    def load_slab():
        k = n_loaded[0]
        if k >= len(slab_meta):
            return
        kind, fo = slab_meta[k]
        waits = [wq_ready]
        if k >= 2:
            assert slab_done[k - 2] is not None
            waits.append(("TE", slab_done[k - 2]))
        if kind == "d":
            buf = k % 2
            src = wdq[fo].rearrange("(b p) f -> p b f", p=128)
            dst = wdsl[:, buf, :, :]
        else:
            buf = k % 2
            hb = wgq if kind == "g" else wuq
            src = hb[:, :, fo * 128:(fo + 1) * 128].rearrange("b p f -> p b f")
            dst = wgu[:, buf, :, :]
        slab_wd[k] = s.dma("gpsimd", lambda e, d=dst, sc=src: e.dma_start(d, sc), waits)
        n_loaded[0] += 1

    load_slab()
    load_slab()

    ot_store = [0, 0]
    st_total = 0

    for b in range(BPC):
        tb0 = b * T
        ve_ln = None
        for j in range(4):
            ve_ln = emit_ln(s, ln_bufs, res_in, tb0 + j * 512, 512, xn, j * 512, ln_state)

        for tblk in range(4):
            tc0 = tblk * 512
            xqT_ready = None
            for tt in range(4):
                tp_done = None
                for ft in range(8):
                    tp = s.add("tensor", lambda e, f=ft, c=tc0 + tt * 128: e.transpose(
                        pTb[:, :], xn[:, f, c:c + 128], idb[:, :]),
                        [("VE", ve_ln), bank_free["pTb"]], inc="TE")
                    tp_done = s.add("vector", lambda e, f=ft: e.tensor_copy(
                        xnat[:, f, :], pTb[:, :]), [("TE", tp)], inc="VE")
                    bank_free["pTb"] = ("VE", tp_done)
                qd = emit_quant(s, xnat[:, :, :].rearrange("p a b -> p (a b)"),
                                xqn[:, :, :].rearrange("p a b -> p (a b)"),
                                1024, tm, [("VE", tp_done), ("TE", s.val("TE"))])
                for ft in range(8):
                    tp = s.add("tensor", lambda e, f=ft: e.transpose(
                        pTb[:, :], xqn[:, f, :], idb[:, :]),
                        [("VE", qd), bank_free["pTb"]], inc="TE")
                    xqT_ready = s.add("vector", lambda e, f=ft, c=tt * 128: e.tensor_copy(
                        xqT[:, f, c:c + 128], pTb[:, :]), [("TE", tp)], inc="VE")
                    bank_free["pTb"] = ("VE", xqT_ready)

            h_ready = None
            for fo in range(22):
                kg = slab_meta_base = (b * 4 + tblk) * 52 + 2 * fo
                ku = kg + 1
                bg, bu = kg % 2, ku % 2
                g1 = s.add("tensor", lambda e, bb=bg: [
                    e.matmul(pA[:, :], wgu[:, bb, fi, :], xqT[:, fi, :],
                             start=(fi == 0), stop=(fi == 7)) for fi in range(8)][-1],
                    [slab_wd[kg], ("VE", xqT_ready), bank_free["pA"]], inc="TE")
                g2 = s.add("tensor", lambda e, bb=bu: [
                    e.matmul(pB[:, :], wgu[:, bb, fi, :], xqT[:, fi, :],
                             start=(fi == 0), stop=(fi == 7)) for fi in range(8)][-1],
                    [slab_wd[ku], bank_free["pB"]], inc="TE")
                r = fo % 2
                sc_s = s.add("scalar", lambda e, rr=r: e.activation(
                    stmp[:, rr, :], pA[:, :], AF.Silu),
                    [("TE", g1), ("VE", s.val("VE"))], inc="SC")
                bank_free["pA"] = ("SC", sc_s)
                ve_h = s.add("vector", lambda e, rr=r: e.tensor_mul(
                    htile[:, rr, :], stmp[:, rr, :], pB[:, :]),
                    [("TE", g2), ("SC", sc_s)], inc="VE")
                bank_free["pB"] = ("VE", ve_h)
                for tt in range(4):
                    tp = s.add("tensor", lambda e, rr=r, c=tt * 128: e.transpose(
                        pTf[:, :], htile[:, rr, c:c + 128], idf[:, :]),
                        [("VE", ve_h), bank_free["pTf"]], inc="TE")
                    h_ready = s.add("vector", lambda e, f=fo, t=tt: e.tensor_copy(
                        h_nat[:, t, f * 128:(f + 1) * 128], pTf[:, :]),
                        [("TE", tp), ("SC", tm.get("fSC", 0))], inc="VE")
                    bank_free["pTf"] = ("VE", h_ready)
                slab_done[kg] = s.val("TE")
                slab_done[ku] = s.val("TE")
                load_slab()
                load_slab()

            hqT_ready = None
            for tt in range(4):
                rr = tt % 2
                qd_half = [None, None]
                for half in range(2):
                    qd_half[half] = emit_quant(
                        s, h_nat[:, tt, half * 1408:(half + 1) * 1408],
                        hqn[:, rr, half * 1408:(half + 1) * 1408],
                        1408, tm, [("VE", h_ready), ("TE", s.val("TE"))])
                    for ft in range(11 * half, 11 * (half + 1)):
                        tp = s.add("tensor", lambda e, f=ft, r2=rr: e.transpose(
                            pTb[:, :], hqn[:, r2, f * 128:(f + 1) * 128], idb[:, :]),
                            [("VE", qd_half[half]), bank_free["pTb"]], inc="TE")
                        hqT_ready = s.add("vector", lambda e, f=ft, c=tt * 128:
                                          e.tensor_copy(hqT[:, f, c:c + 128], pTb[:, :]),
                                          [("TE", tp)], inc="VE")
                        bank_free["pTb"] = ("VE", hqT_ready)

            for fo in range(8):
                kd = (b * 4 + tblk) * 52 + 44 + fo
                bd = kd % 2
                src = res_in[fo * 128:(fo + 1) * 128, tb0 + tc0:tb0 + tc0 + 512]
                r = fo % 2
                ld_r = s.dma("sync", lambda e, sc=src, rr=r: e.dma_start(
                    xres[:, rr, :], sc), [("VE", s.val("VE"))])
                pname = "pA" if fo % 2 == 0 else "pB"
                pp = pA if pname == "pA" else pB
                g = s.add("tensor", lambda e, bb=bd, p=pp: [
                    e.matmul(p[:, :], wdsl[:, bb, fi, :], hqT[:, fi, :],
                             start=(fi == 0), stop=(fi == 21)) for fi in range(22)][-1],
                    [slab_wd[kd], ("VE", hqT_ready), bank_free[pname]], inc="TE")
                ve_o = s.add("vector", lambda e, rr=r, p=pp: e.tensor_add(
                    ot[:, rr, :], p[:, :], xres[:, rr, :]),
                    [("TE", g), ld_r, ("ST", ot_store[r])], inc="VE")
                bank_free[pname] = ("VE", ve_o)
                dst = res_out[fo * 128:(fo + 1) * 128, tb0 + tc0:tb0 + tc0 + 512]
                st_total = s.dma("sync", lambda e, d=dst, rr=r: e.dma_start(
                    d, ot[:, rr, :]), [("VE", ve_o)], inc="ST")[1]
                ot_store[r] = st_total
                slab_done[kd] = s.val("TE")
                load_slab()

    s.add("sync", lambda e: e.nop(), [("ST", st_total)])
    return st_total


def _f1_weight_dram(nc, hbm):
    hbm["f1_wg"] = nc.dram_tensor("wg", [FI1, 1024], F32, kind="ExternalInput").ap()
    hbm["f1_wu"] = nc.dram_tensor("wu", [FI1, 1024], F32, kind="ExternalInput").ap()
    hbm["f1_wd"] = nc.dram_tensor("wd", [1024, FI1], F32, kind="ExternalInput").ap()
    return hbm


def _mk_f1_dram(nc):
    hbm = {}
    hbm["f1_res_in"] = nc.dram_tensor("res_in", [D, TQ], F32, kind="ExternalInput").ap()
    hbm["f1_res_out"] = nc.dram_tensor("res_out", [D, TQ], F32, kind="ExternalOutput").ap()
    _f1_weight_dram(nc, hbm)
    _const_dram(nc, hbm)
    return hbm


def build_f1_nc():
    nc = bass.Bass("TRN2", target_bir_lowering=False, debug=False)
    hbm = _mk_f1_dram(nc)
    s = Sched()
    with ExitStack() as es:
        sems = {n: es.enter_context(nc.semaphore(n)) for n in SEM_NAMES}
        emit_f1(s, nc, es, hbm)
        s.emit(nc, sems)
    return nc


def f1_host_inputs(inputs):
    return {
        "wg": np.ascontiguousarray(np.asarray(inputs["L1_ffn_w_gate"], np.float32)),
        "wu": np.ascontiguousarray(np.asarray(inputs["L1_ffn_w_up"], np.float32)),
        "wd": np.ascontiguousarray(np.asarray(inputs["L1_ffn_w_down"], np.float32)),
        "onesf": np.ones((128, 128), np.float32),
        "onesb": np.ones((128, 128), bf16),
        "idb": np.eye(128, dtype=np.float32).astype(bf16),
        "idf": np.eye(128, dtype=np.float32),
        "cvals": _cvals(),
    }


def run_f1(res_T_shards, common, core_ids=None):
    if core_ids is None:
        core_ids = list(range(len(res_T_shards)))
    nc = build_f1_nc()
    in_maps = [dict(common, res_in=r) for r in res_T_shards]
    res = run_bass_kernel_spmd(nc, in_maps, core_ids=core_ids)
    return [r["res_out"] for r in res.results]



MH = 8
MP = 64
MN = 128
NCK = 16
MAGIC = 12582912.0
TWO_PI = float(2.0 * math.pi)


def emit_mamba(s, nc, es, hbm, pfx="mb_", cache_jobs=()):
    sb = lambda name, shape, dt: es.enter_context(nc.sbuf_tensor(pfx + name, shape, dt))
    ps = lambda name, shape, dt=F32: es.enter_context(nc.psum_tensor(pfx + name, shape, dt))

    res_in = hbm["mb_res_in"]
    res_out = hbm["mb_res_out"]
    lscr = nc.dram_tensor("mb_lscr", [8, 2049], F32, kind="Internal").ap()

    onesf = sb("onesf", [128, 128], F32)
    onesb = sb("onesb", [128, 128], BF16)
    cvals = sb("cvals", [128, 8], F32)
    idf = sb("idf", [128, 128], F32)
    idb = sb("idb", [128, 128], BF16)
    tri = sb("tri", [128, 128], F32)
    selm = sb("selm", [8, 512], BF16)
    avec = sb("avec", [8, 4], F32)
    rmsw = sb("rmsw", [1, 512], F32)
    rmsb = sb("rmsb", [128, 512], BF16)
    xb = sb("xb", [128, 8, 512], BF16)
    x2b = sb("x2b", [128, 2, 512], BF16)
    srow = sb("srow", [1, 3, 512], F32)
    xny = sb("xny", [128, 16384], BF16)
    HT = 1024
    uT = sb("uT", [128, 4, HT + 1], BF16)
    Bz0 = sb("Bz0", [128, 4, HT + 1], BF16)
    Bz1 = sb("Bz1", [128, 4, HT + 1], BF16)
    Cz0 = sb("Cz0", [128, 4, HT], BF16)
    Cz1 = sb("Cz1", [128, 4, HT], BF16)
    winp = sb("winp", [128, 8, 512], BF16)
    selS = sb("selS", [128, 8, 24], BF16)
    outp = sb("outp", [128, 4, 1024], BF16)
    wsl = sb("wsl", [128, 2, 8, 128], BF16)
    rb_ = sb("rb", [8, 10, 2048], BF16)
    rf_ = sb("rf", [8, 2, 2048], F32)
    lpad = sb("lpad", [8, 2049], F32)
    lpfh = sb("lpfh", [1, 1032], F32)
    colsb = sb("colsb", [128, 7, 8, 8], F32)
    hstate = sb("hstate", [128, 8, 64], F32)
    hstateb = sb("hstateb", [128, 8, 64], BF16)
    DTm = sb("DTm", [128, 128], F32)
    DTt = sb("DTt", [128, 128], F32)
    m1m = sb("m1m", [128, 128], F32)
    m2m = sb("m2m", [128, 128], F32)
    M1 = sb("M1", [128, 128], BF16)
    M2 = sb("M2", [128, 128], BF16)
    decb = sb("decb", [128, 128], F32)
    Cdec = sb("Cdec", [128, 128], BF16)
    Unat = sb("Unat", [128, 128], BF16)
    Bgn = sb("Bgn", [128, 128], BF16)
    Bbn = sb("Bbn", [128, 128], BF16)
    hacc = sb("hacc", [64, 128], F32)
    cb = sb("cb", [128, 512], BF16)
    sbb = sb("sbb", [128, 512], BF16)
    rt = sb("rt", [128, 4, 512], BF16)
    y2 = sb("y2", [128, 512], BF16)
    ms = sb("ms", [128, 2], F32)
    yrms = sb("yrms", [128, 512], BF16)
    yT = sb("yT", [128, 4, 128], BF16)
    xres = sb("xres", [128, 8, 128], F32)
    ot = sb("ot", [128, 2, 128], F32)

    pS1 = ps("pS1", [1, 512])
    pS2 = ps("pS2", [1, 512])
    pM = ps("pM", [128, 512])
    pR = ps("pR", [128, 512])
    bkA = ps("bkA", [128, 512])
    bkB = ps("bkB", [128, 512])
    bkC = ps("bkC", [128, 512], BF16)

    for dst, key in ((onesf, "c_onesf"), (onesb, "c_onesb"), (cvals, "c_cvals"),
                     (idf, "c_idf"), (idb, "c_idb"), (tri, "mb_tri"),
                     (selm, "mb_selm"), (avec, "mb_avec"), (rmsw, "mb_rmsw")):
        s.dma("sync", lambda e, d=dst, k=key: e.dma_start(d[:, :], hbm[k][:, :]))
    ld_wres = s.dma("gpsimd", lambda e: e.dma_start(
        winp[:, :, :], hbm["mb_inp"][:, :, :]))
    ld_sels = s.dma("gpsimd", lambda e: e.dma_start(
        selS[:, :, :], hbm["mb_selS"].rearrange("a p f -> p a f")))
    ld_outp = s.dma("gpsimd", lambda e: e.dma_start(
        outp[:, :, :], hbm["mb_outp"][:, :, :]))
    ld_const = s.val("LD")

    cache_it = iter(cache_jobs)
    cache_last = {"at": None, "f0": None}

    def emit_cache(n):
        for _ in range(n):
            job = next(cache_it, None)
            if job is None:
                return
            tag, dst, src = job
            cache_last[tag] = s.dma("gpsimd",
                                    lambda e, d=dst, sc=src: e.dma_start(d, sc))

    ln_bufs = {"xb": xb, "x2b": x2b, "srow": srow, "pS1": pS1, "pS2": pS2,
               "pM": pM, "pR": pR, "onesb": onesb, "onesf": onesf, "cvals": cvals}
    ln_state = {"xb_free": ("LD", ld_const), "pMR_free": None, "pS_free": None}

    xnv = xny[:, :].rearrange("p (a t) -> p a t", t=T)

    sc_a = s.add("scalar", lambda e: e.activation(
        avec[0:8, 2:3], avec[0:8, 0:1], AF.Exp), [("LD", ld_const)], inc="SC")
    ve_a = s.add("vector", lambda e: e.tensor_scalar_mul(
        avec[0:8, 2:3], avec[0:8, 2:3], -1.0), [("SC", sc_a)], inc="VE")
    te_rw = s.add("tensor", lambda e: e.matmul(
        pM[:, :], onesf[0:1, :], rmsw[0:1, :], start=True, stop=True),
        [("LD", ld_const)], inc="TE")
    ve_rw = s.add("vector", lambda e: e.tensor_copy(rmsb[:, :], pM[:, :]),
                  [("TE", te_rw)], inc="VE")
    ln_state["pMR_free"] = ("VE", ve_rw)

    bank_free = {"pM": ("VE", ve_rw), "pR": None, "bkA": None, "bkB": None,
                 "bkC": None}
    n_slab = [0]
    slab_wd = [None] * (16 * 8)
    slab_done = [None] * (16 * 8)

    def load_slab():
        k = n_slab[0]
        if k >= 16 * 8:
            return
        waits = []
        if k >= 2:
            waits.append(("TE", slab_done[k - 2]))
        ft = k % 16
        src = hbm["mb_selT"][ft].rearrange("(a p) f -> p a f", p=128)
        dst = wsl[:, k % 2, :, :]
        slab_wd[k] = s.dma("gpsimd", lambda e, d=dst, sc=src: e.dma_start(d, sc), waits)
        n_slab[0] += 1

    load_slab()
    load_slab()

    ot_store = [0, 0]
    st_total = 0
    hq = {"lpad_free": None, "lpfh_free": None, "h_free": None}
    y_done = None
    RB = lambda i: rb_[0:8, i, :]
    RF = lambda i: rf_[0:8, i, :]

    for b in range(BPC):
        tb0 = b * T
        ve_ln = None
        for j in range(4):
            ve_ln = emit_ln(s, ln_bufs, res_in, tb0 + j * 512, 512, xnv, j * 512, ln_state)
        bank_free["pM"] = ln_state["pMR_free"]
        bank_free["pR"] = ln_state["pMR_free"]

        def _half_body(half):
            hc0 = half * HT
            nsl = b * 32 + half * 16

            if half == 0:
                ve_z = s.add("vector", lambda e: [
                    e.memset(uT[:, :, 0:1], 0.0),
                    e.memset(Bz0[:, :, 0:1], 0.0),
                    e.memset(Bz1[:, :, 0:1], 0.0),
                ][-1], [("TE", s.val("TE"))], inc="VE")
            else:
                ve_z = s.add("vector", lambda e: [
                    e.tensor_copy(uT[:, :, 0:1], uT[:, :, HT:HT + 1]),
                    e.tensor_copy(Bz0[:, :, 0:1], Bz0[:, :, HT:HT + 1]),
                    e.tensor_copy(Bz1[:, :, 0:1], Bz1[:, :, HT:HT + 1]),
                ][-1], [("TE", s.val("TE")), ("VE", s.val("VE"))], inc="VE")

            for a in range(4):
                for blk in range(2):
                    pn = "pM" if (a * 2 + blk) % 2 == 0 else "pR"
                    pp = pM if pn == "pM" else pR
                    g = s.add("tensor", lambda e, aa=a, bl=blk, p=pp: [
                        e.matmul(p[:, :], winp[:, fi, aa * 128:(aa + 1) * 128],
                                 xnv[:, fi, hc0 + bl * 512:hc0 + (bl + 1) * 512],
                                 start=(fi == 0), stop=(fi == 7)) for fi in range(8)][-1],
                        [ld_wres, ("VE", ve_ln), bank_free[pn]], inc="TE")
                    sc_e = s.add("scalar", lambda e, aa=a, bl=blk, p=pp: e.activation(
                        uT[:, aa, 1 + bl * 512:1 + (bl + 1) * 512], p[:, :], AF.Identity),
                        [("TE", g), ("VE", ve_z)], inc="SC")
                    bank_free[pn] = ("SC", sc_e)

            for ft in range(16):
                k = nsl + ft
                if ft < 4:
                    dstb, da, c0 = Bz0, ft, 1
                elif ft < 8:
                    dstb, da, c0 = Bz1, ft - 4, 1
                elif ft < 12:
                    dstb, da, c0 = Cz0, ft - 8, 0
                else:
                    dstb, da, c0 = Cz1, ft - 12, 0
                for blk in range(2):
                    pn = "pM" if blk % 2 == 0 else "pR"
                    pp = pM if pn == "pM" else pR
                    g = s.add("tensor", lambda e, kk=k, bl=blk, p=pp: [
                        e.matmul(p[:, :], wsl[:, kk % 2, fi, :],
                                 xnv[:, fi, hc0 + bl * 512:hc0 + (bl + 1) * 512],
                                 start=(fi == 0), stop=(fi == 7)) for fi in range(8)][-1],
                        [slab_wd[k], bank_free[pn]], inc="TE")
                    sc_e = s.add("scalar", lambda e, d=dstb, aa=da, bl=blk, cc=c0, p=pp:
                                 e.activation(d[:, aa, cc + bl * 512:cc + (bl + 1) * 512],
                                              p[:, :], AF.Identity),
                                 [("TE", g), ("VE", ve_z)], inc="SC")
                    bank_free[pn] = ("SC", sc_e)
                slab_done[k] = s.val("TE")
                load_slab()

            for si in range(3):
                for blk in range(2):
                    g = s.add("tensor", lambda e, ss=si, bl=blk: [
                        e.matmul(pM[0:8, :], selS[:, fi, ss * 8:(ss + 1) * 8],
                                 xnv[:, fi, hc0 + bl * 512:hc0 + (bl + 1) * 512],
                                 start=(fi == 0), stop=(fi == 7)) for fi in range(8)][-1],
                        [ld_sels, bank_free["pM"]], inc="TE")
                    sc_e = s.add("scalar", lambda e, ss=si, bl=blk: e.activation(
                        rb_[0:8, ss, hc0 + bl * 512:hc0 + (bl + 1) * 512], pM[0:8, :],
                        AF.Identity), [("TE", g)], inc="SC")
                    bank_free["pM"] = ("SC", sc_e)

            H = lambda i: rb_[0:8, i, hc0:hc0 + HT]
            HF = lambda i: rf_[0:8, i, hc0:hc0 + HT]
            sc_last = s.val("SC")
            v1 = s.add("vector", lambda e: e.tensor_scalar_add(H(0), H(0), avec[0:8, 1:2]),
                       [("SC", sc_last), ("VE", ve_a)], inc="VE")
            v2 = s.add("scalar", lambda e: e.activation(H(4), H(0), AF.Abs),
                       [("VE", v1)], inc="SC")
            s1 = s.add("scalar", lambda e: e.activation(H(4), H(4), AF.Exp, scale=-1.0),
                       [("SC", v2)], inc="SC")
            s2 = s.add("scalar", lambda e: e.activation(
                H(4), H(4), AF.Ln, bias=cvals[0:8, 2:3]), [("SC", s1)], inc="SC")
            v3 = s.add("vector", lambda e: e.tensor_scalar_max(H(3), H(0), 0.0),
                       [("SC", s2)], inc="VE")
            v4 = s.add("vector", lambda e: e.tensor_add(H(3), H(3), H(4)),
                       [("VE", v3), ("SC", s2)], inc="VE")
            s3 = s.add("scalar", lambda e: e.activation(H(1), H(1), AF.Sigmoid),
                       [("SC", sc_last)], inc="SC")
            v5 = s.add("vector", lambda e: e.tensor_mul(H(5), H(1), H(3)),
                       [("SC", s3), ("VE", v4)], inc="VE")
            v6 = s.add("vector", lambda e: e.tensor_scalar_mul(HF(1), H(3), avec[0:8, 2:3]),
                       [("VE", v5)], inc="VE")
            if half == 0:
                zp = s.add("vector", lambda e: e.memset(lpad[0:8, 0:1], 0.0),
                           [hq["lpad_free"]], inc="VE")
                zbc = cvals[0:8, 3:4].broadcast_to((8, HT))
                v7 = s.add("vector", lambda e, zb=zbc: e.tensor_tensor_scan(
                    lpad[0:8, 1:1 + HT], zb, HF(1), 0.0, ALU.add, ALU.add),
                    [("VE", v6), ("VE", zp), hq["lpad_free"]], inc="VE")
            else:
                zbc = cvals[0:8, 3:4].broadcast_to((8, HT))
                v7 = s.add("vector", lambda e, zb=zbc: e.tensor_tensor_scan(
                    lpad[0:8, 1 + HT:2049], zb, HF(1), lpad[0:8, HT:HT + 1],
                    ALU.add, ALU.add),
                    [("VE", v6)], inc="VE")
            s4 = s.add("scalar", lambda e: e.activation(H(4), HF(1), AF.Exp),
                       [("VE", v7)], inc="SC")
            v8 = s.add("vector", lambda e: e.tensor_scalar(
                H(6), H(1), -1.0, 1.0, ALU.mult, ALU.add), [("SC", s4)], inc="VE")
            v9 = s.add("vector", lambda e: e.tensor_mul(H(6), H(6), H(3)),
                       [("VE", v8)], inc="VE")
            v10 = s.add("vector", lambda e: e.tensor_mul(H(6), H(6), H(4)),
                        [("VE", v9), ("SC", s4)], inc="VE")
            v11 = s.add("vector", lambda e: e.tensor_mul(H(4), H(3), H(2)),
                        [("VE", v10)], inc="VE")
            zbc2 = cvals[0:8, 3:4].broadcast_to((8, HT))
            if half == 0:
                v12 = s.add("vector", lambda e, zb=zbc2: e.tensor_tensor_scan(
                    HF(0), zb, H(4), 0.0, ALU.add, ALU.add), [("VE", v11)], inc="VE")
            else:
                v12 = s.add("vector", lambda e, zb=zbc2: e.tensor_tensor_scan(
                    HF(0), zb, H(4), rf_[0:8, 0, HT - 1:HT], ALU.add, ALU.add),
                    [("VE", v11)], inc="VE")

            def rr_sin(dst_slot, shift, wprev):
                w1a = s.add("vector", lambda e, sh=shift: e.tensor_scalar(
                    HF(1), HF(0), 1.0 / TWO_PI, sh / TWO_PI, ALU.mult, ALU.add),
                    [wprev], inc="VE")
                w1 = s.add("vector", lambda e: e.tensor_scalar_add(HF(1), HF(1), MAGIC),
                           [("VE", w1a)], inc="VE")
                w2 = s.add("vector", lambda e: e.tensor_scalar_sub(HF(1), HF(1), MAGIC),
                           [("VE", w1)], inc="VE")
                w3 = s.add("vector", lambda e, sh=shift: e.tensor_scalar(
                    HF(1), HF(1), -TWO_PI, sh, ALU.mult, ALU.add), [("VE", w2)], inc="VE")
                w4 = s.add("vector", lambda e: e.tensor_add(HF(1), HF(1), HF(0)),
                           [("VE", w3)], inc="VE")
                w5 = s.add("scalar", lambda e, d=dst_slot: e.activation(H(d), HF(1), AF.Sin),
                           [("VE", w4)], inc="SC")
                return w5
            s5 = rr_sin(2, 0.0, ("VE", v12))
            s6 = rr_sin(9, math.pi / 2.0, ("SC", s5))

            v13 = s.add("vector", lambda e: e.tensor_copy(
                HF(1).rearrange("p (c q) -> p c q", q=128),
                lpad[0:8, 128 + hc0:1025 + hc0:128].unsqueeze(2)
                .broadcast_to((8, 8, 128))), [("SC", s6)], inc="VE")
            v14 = s.add("vector", lambda e: e.tensor_sub(
                HF(1).rearrange("p (c q) -> p c q", q=128),
                HF(1).rearrange("p (c q) -> p c q", q=128),
                lpad[0:8, hc0:hc0 + HT:128].unsqueeze(2).broadcast_to((8, 8, 128))),
                [("VE", v13)], inc="VE")
            s7 = s.add("scalar", lambda e: e.activation(H(4), HF(1), AF.Exp),
                       [("VE", v14)], inc="SC")
            v15 = s.add("vector", lambda e: e.tensor_copy(
                HF(1).rearrange("p (c q) -> p c q", q=128),
                lpad[0:8, 128 + hc0:1025 + hc0:128].unsqueeze(2)
                .broadcast_to((8, 8, 128))), [("SC", s7)], inc="VE")
            v16 = s.add("vector", lambda e: e.tensor_sub(
                HF(1), HF(1), lpad[0:8, 1 + hc0:1 + hc0 + HT]), [("VE", v15)], inc="VE")
            s8 = s.add("scalar", lambda e: e.activation(H(7), HF(1), AF.Exp),
                       [("VE", v16)], inc="SC")
            v17 = s.add("vector", lambda e: e.tensor_mul(H(8), H(7), H(6)),
                        [("SC", s8)], inc="VE")
            v18 = s.add("vector", lambda e: e.tensor_mul(H(7), H(7), H(5)),
                        [("VE", v17)], inc="VE")
            v19 = s.add("vector", lambda e: e.tensor_scalar_mul(
                HF(1).rearrange("p (c q) -> p c q", q=128),
                lpad[0:8, hc0:hc0 + HT:128].unsqueeze(2).broadcast_to((8, 8, 128)),
                -1.0), [("VE", v18), ("SC", s8)], inc="VE")

            st_l = s.dma("sync", lambda e: e.dma_start(
                lscr[:, hc0:hc0 + HT + 1], lpad[0:8, hc0:hc0 + HT + 1]),
                [("VE", v7), hq["lpfh_free"]], inc="ST")

            col_done = None
            for cl in range(8):
                ca = half * 8 + cl
                tp = s.add("tensor", lambda e, o=1 + ca * 128: e.transpose(
                    bkA[:, 384:392], lpad[0:8, o:o + 128], idf[0:8, 0:8]),
                    [("VE", v19), bank_free["bkA"]], inc="TE")
                col_done = s.add("vector", lambda e, cc=cl: e.tensor_scalar_mul(
                    colsb[:, 0, cc, :], bkA[:, 384:392], -1.0),
                    [("TE", tp)], inc="VE")
                bank_free["bkA"] = ("VE", col_done)
                tp = s.add("tensor", lambda e, o=hc0 + cl * 128: e.transpose(
                    bkA[:, 384:392], rf_[0:8, 1, o:o + 128], idf[0:8, 0:8]),
                    [("VE", v19), bank_free["bkA"]], inc="TE")
                col_done = s.add("vector", lambda e, cc=cl: e.tensor_copy(
                    colsb[:, 6, cc, :], bkA[:, 384:392]),
                    [("TE", tp)], inc="VE")
                bank_free["bkA"] = ("VE", col_done)
                for fam, slot in ((1, 5), (2, 6), (3, 7), (4, 8), (5, 4)):
                    tp = s.add("tensor", lambda e, sl=slot, o=hc0 + cl * 128: e.transpose(
                        bkC[:, 384:392], rb_[0:8, sl, o:o + 128], idb[0:8, 0:8]),
                        [("VE", v18), ("SC", s7), bank_free["bkC"]], inc="TE")
                    col_done = s.add("vector", lambda e, ff=fam, cc=cl: e.tensor_copy(
                        colsb[:, ff, cc, :], bkC[:, 384:392]),
                        [("TE", tp)], inc="VE")
                    bank_free["bkC"] = ("VE", col_done)

            rope_done = None
            for a in range(4):
                for blk in range(2):
                    cof = hc0 + blk * 512
                    g1 = s.add("tensor", lambda e, aa=a, c2=cof: e.matmul(
                        pM[:, :], selm[0:8, aa * 128:(aa + 1) * 128],
                        rb_[0:8, 9, c2:c2 + 512], start=True, stop=True),
                        [("SC", s6), bank_free["pM"]], inc="TE")
                    g2 = s.add("tensor", lambda e, aa=a, c2=cof: e.matmul(
                        pR[:, :], selm[0:8, aa * 128:(aa + 1) * 128],
                        rb_[0:8, 2, c2:c2 + 512], start=True, stop=True),
                        [bank_free["pR"]], inc="TE")
                    e1 = s.add("scalar", lambda e: e.activation(cb[:, :], pM[:, :],
                                                                AF.Identity),
                               [("TE", g1), ("VE", s.val("VE"))], inc="SC")
                    e2 = s.add("scalar", lambda e: e.activation(sbb[:, :], pR[:, :],
                                                                AF.Identity),
                               [("TE", g2), ("SC", e1)], inc="SC")
                    bank_free["pM"] = ("SC", e1)
                    bank_free["pR"] = ("SC", e2)
                    ww = [("SC", e2), ("VE", s.val("VE"))]
                    for za, zb, off in ((Bz0, Bz1, 1), (Cz0, Cz1, 0)):
                        csl = slice(off + blk * 512, off + (blk + 1) * 512)
                        t1 = s.add("vector", lambda e, z=za, aa=a, c2=csl: e.tensor_mul(
                            rt[:, 0, :], z[:, aa, c2], cb[:, :]), ww, inc="VE")
                        t2 = s.add("vector", lambda e, z=zb, aa=a, c2=csl: e.tensor_mul(
                            rt[:, 1, :], z[:, aa, c2], sbb[:, :]), [("VE", t1)], inc="VE")
                        t3 = s.add("vector", lambda e, z=za, aa=a, c2=csl: e.tensor_mul(
                            rt[:, 2, :], z[:, aa, c2], sbb[:, :]), [("VE", t2)], inc="VE")
                        t4 = s.add("vector", lambda e, z=zb, aa=a, c2=csl: e.tensor_mul(
                            rt[:, 3, :], z[:, aa, c2], cb[:, :]), [("VE", t3)], inc="VE")
                        t5 = s.add("vector", lambda e, z=za, aa=a, c2=csl: e.tensor_sub(
                            z[:, aa, c2], rt[:, 0, :], rt[:, 1, :]), [("VE", t4)], inc="VE")
                        rope_done = s.add("vector", lambda e, z=zb, aa=a, c2=csl: e.tensor_add(
                            z[:, aa, c2], rt[:, 2, :], rt[:, 3, :]), [("VE", t5)], inc="VE")
                        ww = [("VE", rope_done)]

            if half == 0 and True:
                pass
            if half == 0:
                ve_h00 = s.add("vector", lambda e: e.memset(hstate[:, :, :], 0.0),
                               [("TE", s.val("TE")), hq["h_free"]], inc="VE")
                ve_h0 = s.add("vector", lambda e: e.memset(hstateb[:, :, :], 0.0),
                              [("VE", ve_h00)], inc="VE")
                hq["h_free"] = ("VE", ve_h0)
            for h in range(MH):
                a = h // 2
                hp = (h % 2) * 64
                ld_l = s.dma("sync", lambda e, hh=h: e.dma_start(
                    lpfh[0:1, 0:HT + 1],
                    lscr[hh, hc0:hc0 + HT + 1][None, :]),
                    [st_l, hq["lpfh_free"]])
                for cl in range(8):
                    w0 = 1 + cl * 128
                    ws = cl * 128
                    g_l = s.add("tensor", lambda e, o=w0: e.matmul(
                        bkA[:, 256:384], onesf[0:1, :], lpfh[0:1, o:o + 128],
                        start=True, stop=True),
                        [ld_l, bank_free["bkA"]], inc="TE")
                    v_dt = s.add("vector", lambda e, cc=cl, hh=h: e.tensor_scalar(
                        DTm[:, :], bkA[:, 256:384], colsb[:, 0, cc, hh:hh + 1], 0.0,
                        ALU.add, ALU.min),
                        [("TE", g_l), ("VE", s.val("VE")), ("SC", s.val("SC"))], inc="VE")
                    e_dt = s.add("scalar", lambda e: e.activation(
                        DTm[:, :], DTm[:, :], AF.Exp),
                        [("VE", v_dt)], inc="SC")
                    e_db = s.add("scalar", lambda e, cc=cl, hh=h: e.activation(
                        decb[:, :], bkA[:, 256:384], AF.Exp,
                        bias=colsb[:, 6, cc, hh:hh + 1]),
                        [("SC", e_dt)], inc="SC")
                    bank_free["bkA"] = ("SC", e_db)
                    m_t = s.add("vector", lambda e: e.tensor_mul(DTt[:, :], DTm[:, :],
                                                                 tri[:, :]),
                                [("SC", e_dt), ("VE", s.val("VE"))], inc="VE")
                    m_2 = s.add("vector", lambda e, cc=cl, hh=h: e.tensor_scalar_mul(
                        m2m[:, :], DTt[:, :], colsb[:, 1, cc, hh:hh + 1]),
                        [("VE", m_t)], inc="VE")
                    m_1 = s.add("vector", lambda e, cc=cl, hh=h: e.tensor_scalar_mul(
                        m1m[:, :], DTt[:, :], colsb[:, 2, cc, hh:hh + 1]),
                        [("VE", m_2)], inc="VE")
                    g_s2 = s.add("tensor", lambda e, aa=a, q=hp, w=w0: [
                        e.matmul(bkA[:, 0:128], Bz0[q:q + 64, aa, w:w + 128],
                                 Cz0[q:q + 64, aa, w - 1:w + 127], start=True, stop=False),
                        e.matmul(bkA[:, 0:128], Bz1[q:q + 64, aa, w:w + 128],
                                 Cz1[q:q + 64, aa, w - 1:w + 127], start=False, stop=True)][-1],
                        [("VE", rope_done), ("SC", e_db)], inc="TE")
                    g_s1 = s.add("tensor", lambda e, aa=a, q=hp, w=w0, w2=ws: [
                        e.matmul(bkA[:, 128:256], Bz0[q:q + 64, aa, w2:w2 + 128],
                                 Cz0[q:q + 64, aa, w - 1:w + 127], start=True, stop=False),
                        e.matmul(bkA[:, 128:256], Bz1[q:q + 64, aa, w2:w2 + 128],
                                 Cz1[q:q + 64, aa, w - 1:w + 127], start=False, stop=True)][-1],
                        [], inc="TE")
                    mm2 = s.add("vector", lambda e: e.tensor_mul(M2[:, :], bkA[:, 0:128],
                                                                 m2m[:, :]),
                                [("TE", g_s1), ("VE", m_1)], inc="VE")
                    mm1 = s.add("vector", lambda e: e.tensor_mul(M1[:, :], bkA[:, 128:256],
                                                                 m1m[:, :]),
                                [("VE", mm2)], inc="VE")
                    bank_free["bkA"] = ("VE", mm1)
                    cd0 = s.add("vector", lambda e, aa=a, q=hp, w=w0: e.tensor_mul(
                        Cdec[0:64, :], Cz0[q:q + 64, aa, w - 1:w + 127], decb[q:q + 64, :]),
                        [("SC", e_db), ("VE", mm1)], inc="VE")
                    cd1 = s.add("vector", lambda e, aa=a, q=hp, w=w0: e.tensor_mul(
                        Cdec[64:128, :], Cz1[q:q + 64, aa, w - 1:w + 127], decb[q:q + 64, :]),
                        [("VE", cd0)], inc="VE")
                    g_u = s.add("tensor", lambda e, aa=a, q=hp, w=w0, w2=ws: [
                        e.transpose(bkC[:, 0:64], uT[q:q + 64, aa, w:w + 128],
                                    idb[q:q + 64, q:q + 64]),
                        e.transpose(bkC[:, 64:128], uT[q:q + 64, aa, w2:w2 + 128],
                                    idb[q:q + 64, q:q + 64]),
                    ][-1], [bank_free["bkC"], ("VE", s.val("VE")), ("SC", s.val("SC"))],
                        inc="TE")
                    e_u = s.add("vector", lambda e: e.tensor_copy(Unat[:, :], bkC[:, 0:128]),
                                [("TE", g_u)], inc="VE")
                    g_y = s.add("tensor", lambda e, hh=h: [
                        e.matmul(bkB[:, 0:64], M2[:, :], Unat[:, 0:64], start=True,
                                 stop=False),
                        e.matmul(bkB[:, 0:64], M1[:, :], Unat[:, 64:128], start=False,
                                 stop=False),
                        e.matmul(bkB[:, 0:64], Cdec[:, :], hstateb[:, hh, :],
                                 start=False, stop=True)][-1],
                        [("VE", e_u), ("VE", mm1), ("VE", cd1), bank_free["bkB"],
                         hq["h_free"]], inc="TE")
                    yc0 = cl * 2048 + half * 512
                    y_done = s.add("vector", lambda e, c0=yc0, hh=h: e.tensor_copy(
                        xny[:, c0 + hh * 64:c0 + (hh + 1) * 64], bkB[:, 0:64]),
                        [("TE", g_y)], inc="VE")
                    g_b = s.add("tensor", lambda e, aa=a, q=hp, w=w0, w2=ws: [
                        e.transpose(bkC[:, 128:192], Bz0[q:q + 64, aa, w:w + 128],
                                    idb[q:q + 64, q:q + 64]),
                        e.transpose(bkC[:, 192:256], Bz1[q:q + 64, aa, w:w + 128],
                                    idb[q:q + 64, q:q + 64]),
                        e.transpose(bkC[:, 256:320], Bz0[q:q + 64, aa, w2:w2 + 128],
                                    idb[q:q + 64, q:q + 64]),
                        e.transpose(bkC[:, 320:384], Bz1[q:q + 64, aa, w2:w2 + 128],
                                    idb[q:q + 64, q:q + 64]),
                    ][-1], [("VE", y_done)], inc="TE")
                    e_bg = s.add("scalar", lambda e, cc=cl, hh=h: e.activation(
                        Bgn[:, :], bkC[:, 128:256], AF.Identity,
                        scale=colsb[:, 3, cc, hh:hh + 1]), [("TE", g_b)], inc="SC")
                    e_bb = s.add("scalar", lambda e, cc=cl, hh=h: e.activation(
                        Bbn[:, :], bkC[:, 256:384], AF.Identity,
                        scale=colsb[:, 4, cc, hh:hh + 1]), [("SC", e_bg)], inc="SC")
                    bank_free["bkC"] = ("SC", e_bb)
                    g_h = s.add("tensor", lambda e: [
                        e.matmul(bkB[0:64, 128:256], Unat[:, 0:64], Bgn[:, :],
                                 start=True, stop=False),
                        e.matmul(bkB[0:64, 128:256], Unat[:, 64:128], Bbn[:, :],
                                 start=False, stop=True)][-1],
                        [("SC", e_bb)], inc="TE")
                    e_h = s.add("vector", lambda e: e.tensor_copy(hacc[:, :],
                                                                  bkB[0:64, 128:256]),
                                [("TE", g_h)], inc="VE")
                    g_ht = s.add("tensor", lambda e: e.transpose(
                        bkB[:, 64:128], hacc[:, :], idf[0:64, 0:64]),
                        [("VE", e_h)], inc="TE")
                    h_new = s.add("vector", lambda e, cc=cl, hh=h: e.scalar_tensor_tensor(
                        hstate[:, hh, :], hstate[:, hh, :], colsb[:, 5, cc, hh:hh + 1],
                        bkB[:, 64:128], ALU.mult, ALU.add),
                        [("TE", g_ht)], inc="VE")
                    h_mir = s.add("vector", lambda e, hh=h: e.tensor_copy(
                        hstateb[:, hh, :], hstate[:, hh, :]), [("VE", h_new)], inc="VE")
                    hq["h_free"] = ("VE", h_mir)
                    bank_free["bkB"] = ("VE", h_new)
                hq["lpfh_free"] = ("TE", s.val("TE"))
            hq["lpad_free"] = ("ST", st_l[1])
            if b == 0 and half == 0 and "mb_dbg_rows" in hbm:
                dw = [("VE", s.val("VE")), ("SC", s.val("SC")), ("TE", s.val("TE"))]
                for key, buf in (("mb_dbg_M1", M1[:, :]), ("mb_dbg_M2", M2[:, :]),
                                 ("mb_dbg_DTm", DTm[:, :]), ("mb_dbg_DTt", DTt[:, :]),
                                 ("mb_dbg_decb", decb[:, :]), ("mb_dbg_Cdec", Cdec[:, :]),
                                 ("mb_dbg_Unat", Unat[:, :]), ("mb_dbg_Bgn", Bgn[:, :]),
                                 ("mb_dbg_Bbn", Bbn[:, :]), ("mb_dbg_hacc", hacc[:, :]),
                                 ("mb_dbg_winp", winp[:, :, :]),
                                 ("mb_dbg_sels", selS[:, :, :]),
                                 ("mb_dbg_rows", rb_[0:8, :, :]),
                                 ("mb_dbg_rf", rf_[0:8, :, :]),
                                 ("mb_dbg_lpad", lpad[0:8, :]),
                                 ("mb_dbg_cols", colsb[:, :, :, :]),
                                 ("mb_dbg_bz0", Bz0[:, :, :]),
                                 ("mb_dbg_bz1", Bz1[:, :, :]),
                                 ("mb_dbg_cz0", Cz0[:, :, :]),
                                 ("mb_dbg_cz1", Cz1[:, :, :]),
                                 ("mb_dbg_ut", uT[:, :, :]),
                                 ("mb_dbg_y", xny[:, :]),
                                 ("mb_dbg_h", hstate[:, :, :])):
                    s.dma("sync", lambda e, k=key, bb=buf: e.dma_start(hbm[k], bb),
                          dw, inc="ST")
                st2 = s.val("ST")
                s.add("vector", lambda e: e.nop(), [("ST", st2)])
                s.add("tensor", lambda e: e.nop(), [("ST", st2)])
                s.add("scalar", lambda e: e.nop(), [("ST", st2)])
            hq["y_done"] = y_done

        for half in range(2):
            _half_body(half)
            emit_cache(12)

        for c in range(NCK):
            yv_c = xny[:, (c % 8) * 2048 + (c // 8) * 512:
                       (c % 8) * 2048 + (c // 8) * 512 + 512]
            rsrc = res_in[:, tb0 + c * 128:tb0 + (c + 1) * 128].rearrange(
                "(a p) t -> p a t", p=128)
            ld_r = s.dma("sync", lambda e, sc2=rsrc: e.dma_start(xres[:, :, :], sc2),
                         [("VE", s.val("VE"))])
            sq = s.add("scalar", lambda e, yy=yv_c: e.activation(y2[:, :], yy, AF.Square),
                       [("VE", hq["y_done"]), ("SC", s.val("SC"))], inc="SC")
            rd = s.add("vector", lambda e: e.reduce_sum(
                ms[:, 0:1], y2[:, :].rearrange("p (a b) -> p a b", a=1), axis=AX.X),
                [("SC", sq)], inc="VE")
            l1 = s.add("scalar", lambda e: e.activation(
                ms[:, 1:2], ms[:, 0:1], AF.Ln, scale=1.0 / 512.0,
                bias=cvals[:, 1:2]), [("VE", rd)], inc="SC")
            l2 = s.add("scalar", lambda e: e.activation(
                ms[:, 1:2], ms[:, 1:2], AF.Exp, scale=-0.5), [("SC", l1)], inc="SC")
            ya = s.add("vector", lambda e, yy=yv_c: e.tensor_scalar_mul(
                yrms[:, :], yy, ms[:, 1:2]), [("SC", l2)], inc="VE")
            yb = s.add("vector", lambda e: e.tensor_mul(yrms[:, :], yrms[:, :], rmsb[:, :]),
                       [("VE", ya)], inc="VE")
            tds = None
            for q in range(4):
                tp = s.add("tensor", lambda e, qq=q: e.transpose(
                    bkC[:, 0:128], yrms[:, qq * 128:(qq + 1) * 128], idb[:, :]),
                    [("VE", yb), bank_free["bkC"]], inc="TE")
                tds = s.add("vector", lambda e, qq=q: e.tensor_copy(
                    yT[:, qq, :], bkC[:, 0:128]), [("TE", tp)], inc="VE")
                bank_free["bkC"] = ("VE", tds)
            for fo in range(8):
                pn = "pM" if fo % 2 == 0 else "pR"
                pp = pM if pn == "pM" else pR
                g = s.add("tensor", lambda e, f=fo, p=pp: [
                    e.matmul(p[:, 0:128], outp[:, fi, f * 128:(f + 1) * 128],
                             yT[:, fi, :], start=(fi == 0), stop=(fi == 3))
                    for fi in range(4)][-1],
                    [ld_outp, ("VE", tds), bank_free[pn], ld_r], inc="TE")
                r = fo % 2
                ve_o = s.add("vector", lambda e, f=fo, rr=r, p=pp: e.tensor_add(
                    ot[:, rr, :], p[:, 0:128], xres[:, f, :]),
                    [("TE", g), ("ST", ot_store[r])], inc="VE")
                bank_free[pn] = ("VE", ve_o)
                dst = res_out[fo * 128:(fo + 1) * 128, tb0 + c * 128:tb0 + (c + 1) * 128]
                st_total = s.dma("sync", lambda e, d=dst, rr=r: e.dma_start(
                    d, ot[:, rr, :]), [("VE", ve_o)], inc="ST")[1]
                ot_store[r] = st_total

    emit_cache(1 << 30)
    s.add("sync", lambda e: e.nop(), [("ST", st_total)])
    return {"at": cache_last["at"][1] if cache_last["at"] else 0,
            "f0": cache_last["f0"][1] if cache_last["f0"] else 0}


def _mamba_weight_dram(nc, hbm):
    hbm["mb_inp"] = nc.dram_tensor("inp", [128, 8, 512], F32, kind="ExternalInput").ap()
    hbm["mb_selT"] = nc.dram_tensor("selT", [16, 1024, 128], F32, kind="ExternalInput").ap()
    hbm["mb_selS"] = nc.dram_tensor("selS", [8, 128, 24], F32, kind="ExternalInput").ap()
    hbm["mb_selTb"] = nc.dram_tensor("selTb", [16, 1024, 128], BF16, kind="Internal").ap()
    hbm["mb_outp"] = nc.dram_tensor("outp", [128, 4, 1024], F32, kind="ExternalInput").ap()
    hbm["mb_tri"] = nc.dram_tensor("tri", [128, 128], F32, kind="ExternalInput").ap()
    hbm["mb_selm"] = nc.dram_tensor("selmh", [8, 512], BF16, kind="ExternalInput").ap()
    hbm["mb_avec"] = nc.dram_tensor("avec", [8, 4], F32, kind="ExternalInput").ap()
    hbm["mb_rmsw"] = nc.dram_tensor("rmsw", [1, 512], F32, kind="ExternalInput").ap()
    return hbm


def _const_dram(nc, hbm):
    hbm["c_onesf"] = nc.dram_tensor("onesf", [128, 128], F32, kind="ExternalInput").ap()
    hbm["c_onesb"] = nc.dram_tensor("onesb", [128, 128], BF16, kind="ExternalInput").ap()
    hbm["c_idf"] = nc.dram_tensor("idf", [128, 128], F32, kind="ExternalInput").ap()
    hbm["c_idb"] = nc.dram_tensor("idb", [128, 128], BF16, kind="ExternalInput").ap()
    hbm["c_cvals"] = nc.dram_tensor("cvals", [128, 8], F32, kind="ExternalInput").ap()
    hbm["c_onesrow"] = nc.dram_tensor("onesrow", [1, 512], BF16, kind="ExternalInput").ap()
    return hbm


def _mk_mamba_dram(nc):
    hbm = {}
    hbm["mb_res_in"] = nc.dram_tensor("res_in", [D, TQ], F32, kind="ExternalInput").ap()
    hbm["mb_res_out"] = nc.dram_tensor("res_out", [D, TQ], F32, kind="ExternalOutput").ap()
    _mamba_weight_dram(nc, hbm)
    _const_dram(nc, hbm)
    return hbm


def build_mamba_nc(dbg=False):
    nc = bass.Bass("TRN2", target_bir_lowering=False, debug=False)
    hbm = _mk_mamba_dram(nc)
    if dbg:
        for key, shape, dt in (("mb_dbg_M1", [128, 128], BF16),
                               ("mb_dbg_M2", [128, 128], BF16),
                               ("mb_dbg_DTm", [128, 128], F32),
                               ("mb_dbg_DTt", [128, 128], F32),
                               ("mb_dbg_decb", [128, 128], F32),
                               ("mb_dbg_Cdec", [128, 128], BF16),
                               ("mb_dbg_Unat", [128, 128], BF16),
                               ("mb_dbg_Bgn", [128, 128], BF16),
                               ("mb_dbg_Bbn", [128, 128], BF16),
                               ("mb_dbg_hacc", [64, 128], F32),
                               ("mb_dbg_winp", [128, 8, 512], BF16),
                               ("mb_dbg_sels", [128, 8, 24], BF16),
                               ("mb_dbg_rows", [8, 10, 2048], BF16),
                               ("mb_dbg_rf", [8, 2, 2048], F32),
                               ("mb_dbg_lpad", [8, 2049], F32),
                               ("mb_dbg_cols", [128, 7, 8, 8], F32),
                               ("mb_dbg_bz0", [128, 4, 1025], BF16),
                               ("mb_dbg_bz1", [128, 4, 1025], BF16),
                               ("mb_dbg_cz0", [128, 4, 1024], BF16),
                               ("mb_dbg_cz1", [128, 4, 1024], BF16),
                               ("mb_dbg_ut", [128, 4, 1025], BF16),
                               ("mb_dbg_y", [128, 16384], BF16),
                               ("mb_dbg_h", [128, 8, 64], F32)):
            hbm[key] = nc.dram_tensor(key, shape, dt, kind="ExternalOutput").ap()
    s = Sched()
    with ExitStack() as es:
        sems = {n: es.enter_context(nc.semaphore(n)) for n in SEM_NAMES}
        emit_mamba(s, nc, es, hbm)
        s.emit(nc, sems)
    return nc


def _lhsT_pack(w):
    fo, fi = w.shape
    wt = np.asarray(w, np.float32).T.reshape(fi // 128, 128, fo)
    return np.ascontiguousarray(wt.transpose(1, 0, 2))


def mamba_host_inputs(inputs):
    selp = np.asarray(inputs["L0_attn_sel_proj_w"], np.float32)
    perm = np.arange(2072)
    for blk in range(2):
        for h in range(8):
            for n in range(128):
                orig = blk * 1024 + h * 128 + n
                perm[blk * 1024 + (n % 2) * 512 + h * 64 + n // 2] = orig
    news = selp[perm]
    selT = np.ascontiguousarray(news[0:2048].T.reshape(1024, 16, 128)
                                .transpose(1, 0, 2))
    selS = np.ascontiguousarray(news[2048:2072].T.reshape(8, 128, 24))
    selm = np.zeros((8, 512), np.float32)
    for a in range(4):
        selm[2 * a, a * 128:a * 128 + 64] = 1.0
        selm[2 * a + 1, a * 128 + 64:(a + 1) * 128] = 1.0
    avec = np.zeros((8, 4), np.float32)
    avec[:, 0] = np.asarray(inputs["L0_attn_A_log"], np.float32)
    avec[:, 1] = np.asarray(inputs["L0_attn_dt_bias"], np.float32)
    tri = np.triu(np.ones((128, 128), np.float32))
    return {
        "inp": _lhsT_pack(inputs["L0_attn_in_proj_w"]),
        "selT": selT,
        "selS": selS,
        "outp": _lhsT_pack(inputs["L0_attn_out_proj_w"]),
        "tri": tri,
        "selmh": selm.astype(bf16),
        "avec": avec,
        "rmsw": np.asarray(inputs["L0_attn_rms_weight"], np.float32).reshape(1, 512),
        "onesf": np.ones((128, 128), np.float32),
        "onesb": np.ones((128, 128), bf16),
        "idf": np.eye(128, dtype=np.float32),
        "idb": np.eye(128, dtype=np.float32).astype(bf16),
        "cvals": _cvals(),
    }


def run_mamba(res_T_shards, common, core_ids=None, dbg=False):
    if core_ids is None:
        core_ids = list(range(len(res_T_shards)))
    nc = build_mamba_nc(dbg)
    in_maps = [dict(common, res_in=r) for r in res_T_shards]
    res = run_bass_kernel_spmd(nc, in_maps, core_ids=core_ids)
    if dbg:
        return res.results
    return [r["res_out"] for r in res.results]


def build_full_nc():
    nc = bass.Bass("TRN2", target_bir_lowering=False, debug=False)
    hbm = {}
    _const_dram(nc, hbm)
    _mamba_weight_dram(nc, hbm)
    _ffn0_weight_dram(nc, hbm)
    _attn_weight_dram(nc, hbm)
    _f1_weight_dram(nc, hbm)
    x_in = nc.dram_tensor("res_in", [D, TQ], F32, kind="ExternalInput").ap()
    r1 = nc.dram_tensor("fz_r1", [D, TQ], F32, kind="Internal").ap()
    r2 = nc.dram_tensor("fz_r2", [D, TQ], F32, kind="Internal").ap()
    r3 = nc.dram_tensor("fz_r3", [D, TQ], F32, kind="Internal").ap()
    out = nc.dram_tensor("res_out", [D, TQ], F32, kind="ExternalOutput").ap()

    cache_jobs = []
    for w in ("qw", "kw", "vw", "ow"):
        for i in range(2):
            cache_jobs.append(("at", hbm["at_" + w + "b"][4 * i:4 * i + 4],
                               hbm["at_" + w][4 * i:4 * i + 4]))
    for nm in ("w1", "w2"):
        for i in range(8):
            cache_jobs.append(("f0", hbm["f0_" + nm + "b"][4 * i:4 * i + 4],
                               hbm["f0_" + nm + "t"][4 * i:4 * i + 4]))
    for nm in ("h1", "h2"):
        for i in range(32):
            cache_jobs.append(("f0", hbm["f0_" + nm + "b"][i], hbm["f0_" + nm + "t"][i]))
    for i in range(8):
        cache_jobs.append(("f0", hbm["f0_w3b"][i], hbm["f0_w3t"][i]))

    s = Sched()
    with ExitStack() as eso:
        sems = {n: eso.enter_context(nc.semaphore(n)) for n in SEM_NAMES}
        with ExitStack() as e1:
            cache_ready = emit_mamba(s, nc, e1,
                                     dict(hbm, mb_res_in=x_in, mb_res_out=r1),
                                     cache_jobs=cache_jobs)
            s.barrier()
        with ExitStack() as e2:
            emit_ffn0(s, nc, e2, dict(hbm, f0_res_in=r1, f0_res_out=r2),
                      cache_ready=cache_ready)
            s.barrier()
        with ExitStack() as e3:
            emit_attn(s, nc, e3, dict(hbm, at_res_in=r2, at_res_out=r3),
                      cache_ready=cache_ready)
            s.barrier()
        with ExitStack() as e4:
            emit_f1(s, nc, e4, dict(hbm, f1_res_in=r3, f1_res_out=out))
        s.emit(nc, sems)
    return nc


def kernel(**inputs):
    x = np.asarray(inputs["L0_attn_x"], np.float32)
    shards = [np.ascontiguousarray(x[i * BPC:(i + 1) * BPC]) for i in range(NCORES)]
    resT = [_to_T(sh) for sh in shards]

    common = {}
    common.update(mamba_host_inputs(inputs))
    common.update(ffn0_host_inputs(inputs))
    common.update(attn_host_inputs(inputs))
    common.update(f1_host_inputs(inputs))

    nc = build_full_nc()
    in_maps = [dict(common, res_in=r) for r in resT]
    res = run_bass_kernel_spmd(nc, in_maps, core_ids=list(range(NCORES)))

    out = np.empty((32, T, D), np.float32)
    for i in range(NCORES):
        out[i * BPC:(i + 1) * BPC] = _from_T(np.asarray(res.results[i]["res_out"]),
                                             BPC, T, D)
    return out
